# version 03
# baseline (speedup 1.0000x reference)
import numpy as np
import concourse.bass as bass
import concourse.mybir as mybir
from concourse.bass_utils import run_bass_kernel_spmd
from contextlib import ExitStack

F32 = mybir.dt.float32
BF16 = mybir.dt.bfloat16
AF = mybir.ActivationFunctionType
ALU = mybir.AluOpType
AX = mybir.AxisListType

T = 4096
C = 1024
NCH = C // 128
NT = T // 512
MAGIC = float(1.5 * 2 ** 23)
TPI_HI = 12868.0 / 2048.0
TPI_LO = float(2.0 * np.pi - 12868.0 / 2048.0)
LNEPS = 1e-5

ENGINES = ("tensor", "vector", "scalar", "gpsimd", "sync")


class DInstr:
    __slots__ = ("calls",)

    def __init__(self):
        self.calls = []

    def then_inc(self, sem, n):
        self.calls.append(("then_inc", (sem, n), {}))
        return self


class EngRec:

    def __init__(self, ctx, name):
        object.__setattr__(self, "_ctx", ctx)
        object.__setattr__(self, "_name", name)

    def __getattr__(self, attr):
        ctx, name = self._ctx, self._name
        def call(*a, **k):
            d = DInstr()
            ctx.streams[name].append((attr, a, k, d))
            if attr != "wait_ge":
                ctx.last_real[name] = d
            return d
        return call

    def wait(self, ctr, val=None):
        self.wait_ge(ctr.sem, ctr.n if val is None else val)


class Ctr:

    def __init__(self, sem, step=1):
        self.sem = sem
        self.step = step
        self.n = 0

    def inc(self, dinstr, k=1):
        amt = self.step * k
        self.n += amt
        dinstr.then_inc(self.sem, amt)
        return self.n


class Ctx:
    def __init__(self, nc, es):
        self.nc = nc
        self.es = es
        self.streams = {e: [] for e in ENGINES}
        self.last_real = {e: None for e in ENGINES}
        self._bar_mark = {e: 0 for e in ENGINES}
        self.t = EngRec(self, "tensor")
        self.v = EngRec(self, "vector")
        self.s = EngRec(self, "scalar")
        self.g = EngRec(self, "gpsimd")
        self.q = EngRec(self, "sync")
        self.ld = Ctr(es.enter_context(nc.semaphore("c_ld")), 16)
        self.lg = Ctr(es.enter_context(nc.semaphore("c_lg")), 16)
        self.sp = Ctr(es.enter_context(nc.semaphore("c_sp")), 16)
        self.bar = Ctr(es.enter_context(nc.semaphore("c_bar")), 1)
        self.fine = [Ctr(es.enter_context(nc.semaphore(f"c_f{i}")), 1)
                     for i in range(26)]
        self.dmac = [Ctr(es.enter_context(nc.semaphore(f"c_d{i}")), 16)
                     for i in range(20)]
        self._bar_dmam = {}
        self.sb = {}
        self._names = {}
        self.dram = {}
        self._scopes = []

    def _uniq(self, name):
        n = self._names.get(name, 0)
        self._names[name] = n + 1
        return name if n == 0 else f"{name}__{n}"

    def sbuf(self, name, shape, dtype):
        st = self._scopes[-1] if self._scopes else self.es
        return st.enter_context(self.nc.sbuf_tensor(self._uniq(name), shape, dtype))

    def psum(self, name, shape, dtype=F32):
        st = self._scopes[-1] if self._scopes else self.es
        return st.enter_context(self.nc.psum_tensor(self._uniq(name), shape, dtype))

    def scope(self):
        from contextlib import contextmanager

        @contextmanager
        def _s():
            st = ExitStack()
            self._scopes.append(st)
            try:
                yield
            finally:
                self._scopes.pop()
                st.close()
        return _s()

    def all_dma_ctrs(self):
        return [self.ld, self.lg, self.sp] + self.dmac

    def barrier(self):
        used = []
        if len(self.streams["sync"]) > self._bar_mark["sync"]:
            for c in self.all_dma_ctrs():
                if c.n > self._bar_dmam.get(id(c), 0):
                    self.q.wait(c)
                    self._bar_dmam[id(c)] = c.n
            self.q.nop()
        if len(self.streams["gpsimd"]) > self._bar_mark["gpsimd"]:
            self.g.wait(self.lg)
            self.g.nop()
        for e in ENGINES:
            if len(self.streams[e]) > self._bar_mark[e] and self.last_real[e] is not None:
                used.append(e)
        proxies = {"tensor": self.t, "vector": self.v, "scalar": self.s,
                   "gpsimd": self.g, "sync": self.q}
        for e in used:
            d = self.last_real[e]
            if any(c[0] == "then_inc" for c in d.calls):
                alt = None
                for (attr, a, k, dd) in reversed(self.streams[e][-8:]):
                    if dd is not None and attr != "wait_ge" and not dd.calls:
                        alt = dd
                        break
                d = alt if alt is not None else proxies[e].nop()
            self.bar.inc(d)
        tot = self.bar.n
        for e in ENGINES:
            self.streams[e].append(("wait_ge", (self.bar.sem, tot), {}, None))
            self._bar_mark[e] = len(self.streams[e])
        for e in ENGINES:
            self.last_real[e] = None


def _replay(stream):
    def run(eng):
        for (attr, a, k, d) in stream:
            ins = getattr(eng, attr)(*a, **k)
            if d is not None:
                for (m, ma, mk) in d.calls:
                    ins = getattr(ins, m)(*ma, **mk)
    return run


def phase_consts(cx: Ctx):
    cx.sb["ones_col"] = cx.sbuf("ones_col", [128, 1], F32)
    cx.sb["ones_row"] = cx.sbuf("ones_row", [1, 128], F32)
    cx.sb["ones_row_bf"] = cx.sbuf("ones_row_bf", [1, 128], BF16)
    cx.sb["ones_col_bf"] = cx.sbuf("ones_col_bf", [128, 1], BF16)
    cx.sb["eps11"] = cx.sbuf("eps11", [1, 1], F32)
    v = cx.v
    v.memset(cx.sb["ones_col"][:], 1.0)
    v.memset(cx.sb["ones_row"][:], 1.0)
    v.memset(cx.sb["ones_row_bf"][:], 1.0)
    v.memset(cx.sb["ones_col_bf"][:], 1.0)
    d = v.memset(cx.sb["eps11"][:], float(LNEPS))
    c0 = cx.fine[0]
    c0.inc(d)
    for e in ("tensor", "scalar", "gpsimd", "sync"):
        cx.streams[e].append(("wait_ge", (c0.sem, c0.n), {}, None))
    cx._bar_mark = {e: len(cx.streams[e]) for e in ENGINES}


def emit_ln_cm(cx: Ctx, src_name, dst_name, sc):
    t_, v, s, q = cx.t, cx.v, cx.s, cx.q
    src, dst = cx.dram[src_name], cx.dram[dst_name]
    xin, xsq = sc["ln_xin"], sc["ln_xsq"]
    stat_m, stat_v = sc["ln_stat_m"], sc["ln_stat_v"]
    bca, bcb = sc["ln_bca"], sc["ln_bcb"]
    sm, xout = sc["ln_small"], sc["ln_xout"]
    ones_col, ones_row = cx.sb["ones_col"], cx.sb["ones_row"]
    eps11 = cx.sb["eps11"]
    lda = [cx.dmac[0], cx.dmac[1]]
    spa = [cx.dmac[2], cx.dmac[3]]
    c_sq, c_st, c_var, c_rstd, c_mr, c_bc, c_ap = (cx.fine[i] for i in range(1, 8))

    for it in range(NT):
        buf = it % 2
        t0 = 512 * it
        if it >= 2:
            q.wait(c_ap, c_ap.n - 1)
        for ci in range(NCH):
            d = q.dma_start(out=xin[:, buf, ci, :],
                            in_=src[128 * ci:128 * (ci + 1), t0:t0 + 512])
            lda[buf].inc(d)
        s.wait(lda[buf])
        if it >= 1:
            s.wait(c_st)
        d = s.activation(xsq[:], xin.ap()[:, buf, :, :].rearrange("p a b -> p (a b)"),
                         AF.Square)
        c_sq.inc(d)
        t_.wait(lda[buf])
        if it >= 1:
            t_.wait(c_var)
        for ci in range(NCH):
            t_.matmul(stat_m[0:1, :], ones_col[:, 0:1], xin[:, buf, ci, :],
                      start=(ci == 0), stop=(ci == NCH - 1))
        t_.wait(c_sq)
        for ci in range(NCH):
            d = t_.matmul(stat_v[0:1, :], ones_col[:, 0:1],
                          xsq[:, 512 * ci:512 * (ci + 1)],
                          start=(ci == 0), stop=(ci == NCH - 1))
        c_st.inc(d)
        v.wait(c_st)
        cva = cx.fine[8]
        d = v.tensor_scalar(sm[:, 0:512], stat_m[0:1, :], 1.0 / C, None,
                            op0=ALU.mult)
        cva.inc(d)
        d = v.tensor_scalar(sm[:, 512:1024], stat_v[0:1, :], 1.0 / C, None,
                            op0=ALU.mult)
        cva.inc(d)
        v.wait(cva)
        d = v.tensor_mul(sm[:, 1024:1536], sm[:, 0:512], sm[:, 0:512])
        cva.inc(d)
        v.wait(cva)
        d = v.tensor_sub(sm[:, 512:1024], sm[:, 512:1024], sm[:, 1024:1536])
        c_var.inc(d)
        s.wait(c_var)
        d = s.activation(sm[:, 1024:1536], sm[:, 1024:1536] if False else sm[:, 512:1024],
                         AF.Ln, bias=eps11[0:1, 0:1])
        cx.fine[9].inc(d)
        s.wait(cx.fine[9])
        d = s.activation(sm[:, 1024:1536], sm[:, 1024:1536], AF.Exp, scale=-0.5)
        c_rstd.inc(d)
        v.wait(c_rstd)
        d = v.scalar_tensor_tensor(sm[:, 1536:2048], sm[:, 0:512], -1.0,
                                   sm[:, 1024:1536], op0=ALU.mult, op1=ALU.mult)
        c_mr.inc(d)
        t_.wait(c_mr)
        if it >= 1:
            t_.wait(c_ap)
        t_.matmul(bca[:], ones_row[0:1, :], sm[:, 1024:1536], start=True, stop=True)
        d = t_.matmul(bcb[:], ones_row[0:1, :], sm[:, 1536:2048], start=True, stop=True)
        c_bc.inc(d)
        v.wait(c_bc)
        if it >= 2:
            v.wait(spa[buf])
        for ci in range(NCH):
            d = v.tensor_mul(xout[:, buf, ci, :], xin[:, buf, ci, :], bca[:])
            cva.inc(d)
            v.wait(cva)
            d = v.tensor_add(xout[:, buf, ci, :], xout[:, buf, ci, :], bcb[:])
            if ci < NCH - 1:
                cva.inc(d)
                v.wait(cva)
        c_ap.inc(d)
        q.wait(c_ap)
        for ci in range(NCH):
            d = q.dma_start(out=dst[128 * ci:128 * (ci + 1), t0:t0 + 512],
                            in_=xout[:, buf, ci, :])
            spa[buf].inc(d)


def alloc_ln_scratch(cx: Ctx):
    return {
        "ln_xin": cx.sbuf("ln_xin", [128, 2, NCH, 512], F32),
        "ln_xsq": cx.sbuf("ln_xsq", [128, NCH * 512], F32),
        "ln_stat_m": cx.psum("ln_stat_m", [1, 512]),
        "ln_stat_v": cx.psum("ln_stat_v", [1, 512]),
        "ln_bca": cx.psum("ln_bca", [128, 512]),
        "ln_bcb": cx.psum("ln_bcb", [128, 512]),
        "ln_small": cx.sbuf("ln_small", [1, 2048], F32),
        "ln_xout": cx.sbuf("ln_xout", [128, 2, NCH, 512], F32),
    }


def emit_row_load(cx: Ctx, dram_ap, row_sb, ctr=None):
    d = cx.q.dma_start(out=row_sb[:], in_=dram_ap)
    (ctr or cx.ld).inc(d)
    return d


def emit_pp_from_row(cx: Ctx, row_sb_ap, L, out_sb, pcol, c_done):
    ncol = (L + 127) // 128
    cx.t.wait(c_done)
    for j in range(ncol):
        w = min(128, L - 128 * j)
        d = cx.t.matmul(pcol[0:w, j:j + 1], row_sb_ap[0:1, 128 * j:128 * j + w],
                        cx.sb["ones_row"][0:1, 0:1], start=True, stop=True)
    c_done.inc(d)
    cx.v.wait(c_done)
    wlast = L - 128 * (ncol - 1)
    if wlast == 128:
        d = cx.v.tensor_copy(out_sb[0:128, 0:ncol], pcol[0:128, 0:ncol])
    else:
        if ncol > 1:
            cx.v.tensor_copy(out_sb[0:128, 0:ncol-1], pcol[0:128, 0:ncol-1])
        d = cx.v.tensor_copy(out_sb[0:wlast, ncol-1:ncol],
                             pcol[0:wlast, ncol-1:ncol])
    c_done.inc(d)


def emit_add3(cx: Ctx, a_name, b_name, c_name, dst_name):
    v, g, q = cx.v, cx.g, cx.q
    A, B, Cc, D = (cx.dram[n] for n in (a_name, b_name, c_name, dst_name))
    with cx.scope():
        xa = cx.sbuf("d3_a", [128, 2, NCH, 512], F32)
        xb = cx.sbuf("d3_b", [128, 2, NCH, 512], F32)
        xc = cx.sbuf("d3_c", [128, 2, NCH, 512], F32)
        xo = cx.sbuf("d3_o", [128, 2, NCH, 512], F32)
        lda = [cx.dmac[0], cx.dmac[1]]
        spa = [cx.dmac[2], cx.dmac[3]]
        c_ap = cx.fine[8]
        ap0 = c_ap.n
        for it in range(NT):
            buf = it % 2
            t0 = 512 * it
            if it >= 2:
                q.wait(c_ap, ap0 + it - 1)
            for ci in range(NCH):
                sl = (slice(128 * ci, 128 * (ci + 1)), slice(t0, t0 + 512))
                lda[buf].inc(q.dma_start(out=xa[:, buf, ci, :], in_=A[sl]))
                lda[buf].inc(q.dma_start(out=xb[:, buf, ci, :], in_=B[sl]))
                lda[buf].inc(q.dma_start(out=xc[:, buf, ci, :], in_=Cc[sl]))
            v.wait(lda[buf])
            if it >= 2:
                v.wait(spa[buf])
            cvd = cx.fine[9]
            for ci in range(NCH):
                d = v.tensor_add(xo[:, buf, ci, :], xa[:, buf, ci, :],
                                 xb[:, buf, ci, :])
                cvd.inc(d)
                v.wait(cvd)
                d = v.tensor_add(xo[:, buf, ci, :], xo[:, buf, ci, :],
                                 xc[:, buf, ci, :])
                if ci < NCH - 1:
                    cvd.inc(d)
                    v.wait(cvd)
            c_ap.inc(d)
            q.wait(c_ap)
            for ci in range(NCH):
                sl = (slice(128 * ci, 128 * (ci + 1)), slice(t0, t0 + 512))
                spa[buf].inc(q.dma_start(out=D[sl], in_=xo[:, buf, ci, :]))
    cx.barrier()


def emit_l1ffn_combine(cx: Ctx):
    t_, v, s, g, q = cx.t, cx.v, cx.s, cx.g, cx.q
    xn1, x1, P, out = (cx.dram[n] for n in ("xn1_cm", "x1_cm", "P_cm", "out_cm"))
    TCH = 256
    NIT = T // TCH
    with cx.scope():
        wr = cx.sbuf("g_wr", [128, NCH, C], BF16)
        wk = cx.sbuf("g_wk", [128, NCH, C], BF16)
        wv = cx.sbuf("g_wv", [128, NCH, C], BF16)
        mu_row = cx.sbuf("g_murow", [1, 2 * C], F32)
        mu_pp = cx.sbuf("g_mupp", [128, 2 * NCH], F32)
        xin = cx.sbuf("g_xin", [128, 2, NCH, TCH + 1], F32)
        dtmp = cx.sbuf("g_dtmp", [128, NCH, TCH], F32)
        xr = cx.sbuf("g_xr", [128, 2, NCH, TCH], BF16)
        xk = cx.sbuf("g_xk", [128, 2, NCH, TCH], BF16)
        r_sb = cx.sbuf("g_r", [128, 2, NCH, TCH], F32)
        krelu = cx.sbuf("g_krelu", [128, 2, TCH], F32)
        kact = cx.sbuf("g_kact", [128, 2, NCH, TCH], BF16)
        x1in = cx.sbuf("g_x1", [128, 2, NCH, TCH], F32)
        pin = cx.sbuf("g_p", [128, 2, NCH, TCH], F32)
        osb = cx.sbuf("g_o", [128, 2, NCH, TCH], F32)
        pcol = cx.psum("g_pcol", [128, 16])
        pr = [cx.psum(f"g_pr{i}", [128, 512]) for i in range(2)]
        pk = [cx.psum(f"g_pk{i}", [128, 512]) for i in range(2)]
        pvp = [cx.psum(f"g_pv{i}", [128, 512]) for i in range(2)]
        ld, lg, sp = cx.ld, cx.lg, cx.sp
        lda = [cx.dmac[4], cx.dmac[5]]
        spg = [cx.dmac[6], cx.dmac[7]]
        c_pp = cx.fine[8]
        c_xr = cx.fine[9]
        c_pr = cx.fine[10]
        c_pk = cx.fine[11]
        c_sig = cx.fine[12]
        c_kact = cx.fine[13]
        c_pv = cx.fine[14]
        c_comb = cx.fine[15]
        c_ini = cx.fine[7]

        lgw = cx.dmac[14]
        for ci in range(NCH):
            lgw.inc(g.dma_start(out=wr[:, ci, :], in_=cx.dram["L1_ffn_W_rT"][128*ci:128*(ci+1), :]))
            lgw.inc(g.dma_start(out=wk[:, ci, :], in_=cx.dram["L1_ffn_W_kT"][128*ci:128*(ci+1), :]))
            lgw.inc(g.dma_start(out=wv[:, ci, :], in_=cx.dram["L1_ffn_W_vT"][128*ci:128*(ci+1), :]))
        ldr = cx.dmac[9]
        emit_row_load(cx, cx.dram["L1_ffn_mu_r"][0:1, :], mu_row[:, 0:C], ldr)
        emit_row_load(cx, cx.dram["L1_ffn_mu_k"][0:1, :], mu_row[:, C:2*C], ldr)
        cx.t.wait(ldr)
        emit_pp_from_row(cx, mu_row[:, 0:2*C], 2 * C, mu_pp, pcol, c_pp)
        d = v.memset(xin[:, 0, :, 0:1], 0.0)
        c_ini.inc(d)
        q.wait(c_ini)
        t_.wait(lgw)

        xr0, pr0, pk0, sig0, kact0, pv0, comb0 = (c.n for c in
            (c_xr, c_pr, c_pk, c_sig, c_kact, c_pv, c_comb))
        for it in range(NIT + 1):
            buf = it % 2
            t0 = TCH * it
            if it < NIT:
                if it >= 2:
                    q.wait(c_xr, xr0 + it - 1)
                    q.wait(c_comb, comb0 + NCH * (it - 1))
                for ci in range(NCH):
                    lo = max(t0 - 1, 0)
                    w = t0 + TCH - lo
                    lda[buf].inc(q.dma_start(out=xin[:, buf, ci, (TCH + 1 - w):],
                                             in_=xn1[128*ci:128*(ci+1), lo:t0 + TCH]))
                for ci in range(NCH):
                    lda[buf].inc(q.dma_start(out=x1in[:, buf, ci, :],
                                             in_=x1[128*ci:128*(ci+1), t0:t0+TCH]))
                    lda[buf].inc(q.dma_start(out=pin[:, buf, ci, :],
                                             in_=P[128*ci:128*(ci+1), t0:t0+TCH]))
                v.wait(lda[buf])
                v.wait(c_pp)
                if it >= 2:
                    v.wait(c_pk, pk0 + NCH * (it - 1))
                cvg2 = cx.fine[6]
                for ci in range(NCH):
                    xv = xin[:, buf, ci, 1:TCH+1]
                    xp = xin[:, buf, ci, 0:TCH]
                    d = v.tensor_sub(dtmp[:, ci, :], xv, xp)
                    cvg2.inc(d)
                    v.wait(cvg2)
                    v.scalar_tensor_tensor(xr[:, buf, ci, :], dtmp[:, ci, :],
                                           mu_pp[:, ci:ci+1], xp,
                                           op0=ALU.mult, op1=ALU.add)
                    d = v.scalar_tensor_tensor(xk[:, buf, ci, :], dtmp[:, ci, :],
                                               mu_pp[:, NCH+ci:NCH+ci+1], xp,
                                               op0=ALU.mult, op1=ALU.add)
                c_xr.inc(d)
                for dc in range(NCH):
                    gidx = it * NCH + dc
                    slot = gidx % 2
                    t_.wait(c_xr)
                    if gidx >= 2:
                        t_.wait(c_sig, sig0 + gidx - 1)
                    for ci in range(NCH):
                        d = t_.matmul(pr[slot][:, 0:TCH], wr[:, ci, 128*dc:128*(dc+1)],
                                      xr[:, buf, ci, :], start=(ci == 0), stop=(ci == NCH-1))
                    c_pr.inc(d)
                    if gidx >= 2:
                        t_.wait(c_kact, kact0 + gidx - 1)
                    for ci in range(NCH):
                        d = t_.matmul(pk[slot][:, 0:TCH], wk[:, ci, 128*dc:128*(dc+1)],
                                      xk[:, buf, ci, :], start=(ci == 0), stop=(ci == NCH-1))
                    c_pk.inc(d)
                    s.wait(c_pr, pr0 + gidx + 1)
                    if it >= 2:
                        s.wait(c_comb, comb0 + NCH * (it - 1))
                    d = s.activation(r_sb[:, buf, dc, :], pr[slot][:, 0:TCH], AF.Sigmoid)
                    c_sig.inc(d)
                    s.wait(c_pk, pk0 + gidx + 1)
                    crl = cx.fine[16]
                    d = s.activation(krelu[:, slot, :], pk[slot][:, 0:TCH], AF.Relu)
                    crl.inc(d)
                    s.wait(crl)
                    d = s.activation(kact[:, buf, dc, :], krelu[:, slot, :], AF.Square)
                    c_kact.inc(d)
            if it >= 1:
                jt = it - 1
                jbuf = jt % 2
                for dc in range(NCH):
                    gidx = jt * NCH + dc
                    slot = gidx % 2
                    t_.wait(c_kact, kact0 + NCH * it)
                    if gidx >= 2:
                        t_.wait(c_comb, comb0 + gidx - 1)
                    for ci in range(NCH):
                        d = t_.matmul(pvp[slot][:, 0:TCH], wv[:, ci, 128*dc:128*(dc+1)],
                                      kact[:, jbuf, ci, :], start=(ci == 0), stop=(ci == NCH-1))
                    c_pv.inc(d)
                    v.wait(c_pv, pv0 + gidx + 1)
                    if jt >= 2 and dc == 0:
                        v.wait(spg[jbuf])
                    cvg2 = cx.fine[6]
                    d = v.scalar_tensor_tensor(osb[:, jbuf, dc, :],
                                               pvp[slot][:, 0:TCH], 1.0,
                                               r_sb[:, jbuf, dc, :],
                                               op0=ALU.mult, op1=ALU.mult)
                    cvg2.inc(d)
                    v.wait(cvg2)
                    d = v.tensor_add(osb[:, jbuf, dc, :], osb[:, jbuf, dc, :],
                                     x1in[:, jbuf, dc, :])
                    cvg2.inc(d)
                    v.wait(cvg2)
                    d = v.tensor_add(osb[:, jbuf, dc, :], osb[:, jbuf, dc, :],
                                     pin[:, jbuf, dc, :])
                    c_comb.inc(d)
                q.wait(c_comb)
                for ci in range(NCH):
                    spg[jbuf].inc(q.dma_start(
                        out=out[128*ci:128*(ci+1), TCH*jt:TCH*(jt+1)],
                        in_=osb[:, jbuf, ci, :]))
    cx.barrier()


def emit_l0_attn(cx: Ctx):
    t_, v, s, g, q = cx.t, cx.v, cx.s, cx.g, cx.q
    H, D = 16, 64
    ld, lg, sp = cx.ld, cx.lg, cx.sp
    with cx.scope():
        ycm = cx.sbuf("b_ycm", [128, NCH, T], BF16)
        idb = cx.sbuf("b_idb", [128, 128], BF16)
        madd = cx.sbuf("b_madd", [128, 128], F32)
        iot = cx.sbuf("b_iot", [128, 128], F32)
        vcol = cx.sbuf("b_vcol", [128, 128], F32)
        c_ini = cx.fine[7]
        g.iota(iot[:], pattern=[[1, 128]], base=0, channel_multiplier=0,
               allow_small_or_imprecise_dtypes=True)
        d = g.iota(vcol[:], pattern=[[0, 128]], base=0, channel_multiplier=1,
                   allow_small_or_imprecise_dtypes=True)
        c_ini.inc(d)
        v.wait(c_ini)
        v.tensor_tensor(idb[:], iot[:], vcol[:], op=ALU.is_equal)
        d = v.tensor_tensor(madd[:], iot[:], vcol[:], op=ALU.is_gt)
        c_ini.inc(d)
        v.wait(c_ini)
        d = v.tensor_scalar(madd[:], madd[:], -1e30, None, op0=ALU.mult)
        c_ini.inc(d)
        t_.wait(c_ini)

        with cx.scope():
            wqs = cx.sbuf("b_wq", [128, NCH, C], BF16)
            wks = cx.sbuf("b_wk", [128, NCH, C], BF16)
            wvs = cx.sbuf("b_wv", [128, NCH, C], BF16)
            brow = cx.sbuf("b_brow", [1, 2 * C], F32)
            bvrow = cx.sbuf("b_bvrow", [1, C], BF16)
            bpp = cx.sbuf("b_bpp", [128, 2 * NCH], F32)
            xbf = cx.sbuf("b_xbf", [128, 2, NCH, 512], BF16)
            qko = cx.sbuf("b_qko", [128, 2, 512], BF16)
            vo = cx.sbuf("b_vo", [128, 2, 512], BF16)
            pcol = cx.psum("b_pcol", [128, 16])
            pqk = [cx.psum(f"b_pqk{i}", [128, 512]) for i in range(2)]
            pv = [cx.psum(f"b_pv{i}", [128, 512]) for i in range(2)]
            c_pp = cx.fine[8]
            c_mm = cx.fine[9]
            c_ev = cx.fine[10]
            lgw = cx.dmac[14]
            ldr = cx.dmac[9]
            lgx = [cx.dmac[17], cx.dmac[18]]
            spb = [cx.dmac[13], cx.dmac[8]]
            for ci in range(NCH):
                r = slice(128 * ci, 128 * (ci + 1))
                lgw.inc(g.dma_start(out=wqs[:, ci, :], in_=cx.dram["L0_attn_Wq"][r, :]))
                lgw.inc(g.dma_start(out=wks[:, ci, :], in_=cx.dram["L0_attn_Wk"][r, :]))
                lgw.inc(g.dma_start(out=wvs[:, ci, :], in_=cx.dram["L0_attn_Wv"][r, :]))
            emit_row_load(cx, cx.dram["L0_attn_bq"][0:1, :], brow[:, 0:C], ldr)
            emit_row_load(cx, cx.dram["L0_attn_bk"][0:1, :], brow[:, C:2*C], ldr)
            lgw.inc(g.dma_start(out=bvrow[:], in_=cx.dram["L0_attn_bv"][0:1, :]))
            t_.wait(ldr)
            t_.wait(lgw)
            emit_pp_from_row(cx, brow[:, 0:2*C], 2 * C, bpp, pcol, c_pp)
            v.wait(c_pp)
            d = v.tensor_scalar(bpp[:, 0:NCH], bpp[:, 0:NCH], 0.125, None,
                                op0=ALU.mult)
            c_pp.inc(d)
            s.wait(c_pp)
            mm0, ev0 = c_mm.n, c_ev.n
            gi = 0
            for it in range(NT):
                buf = it % 2
                t0 = 512 * it
                if it >= 2:
                    g.wait(c_mm, mm0 + 24 * (it - 1))
                for ci in range(NCH):
                    lgx[buf].inc(g.dma_start(out=xbf[:, buf, ci, :],
                                             in_=cx.dram["xn_cm"][128*ci:128*(ci+1), t0:t0+512]))
                t_.wait(lgx[buf])
                for wsb, boff, dst in ((wqs, 0, "q0_cm"), (wks, NCH, "k0_cm")):
                    for nb in range(NCH):
                        slot = gi % 2
                        if gi >= 2:
                            t_.wait(c_ev, ev0 + gi - 1)
                        for ci in range(NCH):
                            d = t_.matmul(pqk[slot][:], wsb[:, ci, 128*nb:128*(nb+1)],
                                          xbf[:, buf, ci, :], start=(ci == 0),
                                          stop=(ci == NCH - 1))
                        c_mm.inc(d)
                        s.wait(c_mm)
                        if gi >= 2:
                            s.wait(spb[slot])
                        sc = 0.125 if boff == 0 else 1.0
                        d = s.activation(qko[:, slot, :], pqk[slot][:], AF.Identity,
                                         scale=sc, bias=bpp[:, boff+nb:boff+nb+1])
                        c_ev.inc(d)
                        q.wait(c_ev, ev0 + gi + 1)
                        spb[slot].inc(q.dma_start(
                            out=cx.dram[dst][128*nb:128*(nb+1), t0:t0+512],
                            in_=qko[:, slot, :]))
                        gi += 1
                for tt in range(4):
                    for nb2 in range(2):
                        slot = gi % 2
                        if gi >= 2:
                            t_.wait(c_ev, ev0 + gi - 1)
                        nsl = slice(512 * nb2, 512 * (nb2 + 1))
                        t_.matmul(pv[slot][:], cx.sb["ones_row_bf"][0:1, :],
                                  bvrow[0:1, nsl], start=True, stop=False)
                        for ci in range(NCH):
                            d = t_.matmul(pv[slot][:], xbf[:, buf, ci, 128*tt:128*(tt+1)],
                                          wvs[:, ci, nsl], start=False,
                                          stop=(ci == NCH - 1))
                        c_mm.inc(d)
                        s.wait(c_mm)
                        if gi >= 2:
                            s.wait(spb[slot])
                        d = s.activation(vo[:, slot, :], pv[slot][:], AF.Copy)
                        c_ev.inc(d)
                        q.wait(c_ev, ev0 + gi + 1)
                        tr = slice(t0 + 128 * tt, t0 + 128 * (tt + 1))
                        for hh in range(8):
                            h = nb2 * 8 + hh
                            spb[slot].inc(q.dma_start(out=cx.dram["v0_tm"][h, tr, :],
                                                      in_=vo[:, slot, 64*hh:64*(hh+1)]))
                        gi += 1
        cx.barrier()
        with cx.scope():
            q_sb = cx.sbuf("b_qsb", [64, 2, T], BF16)
            k_sb = cx.sbuf("b_ksb", [64, 2, T], BF16)
            v_sb = cx.sbuf("b_vsb", [128, 2, 32, D], BF16)
            P_sb = cx.sbuf("b_psb", [128, 2, 1024], BF16)
            PT_sb = cx.sbuf("b_ptsb", [128, 2, 512], BF16)
            ytmp = cx.sbuf("b_ytmp", [128, 2, D], BF16)
            mrow = cx.sbuf("b_mrow", [128, 10], F32)
            psB = [cx.psum(f"b_psB{i}", [128, 1024]) for i in range(2)]
            ptp = [cx.psum(f"b_ptp{i}", [128, 1024], BF16) for i in range(2)]
            pav = [cx.psum(f"b_pav{i}", [128, 512]) for i in range(2)]
            yacc = cx.sbuf("b_yacc", [128, 2, 64], F32)
            c_fold = cx.fine[19]
            fold0 = c_fold.n
            c_hd = cx.fine[6]
            cx.v.memset(ptp[0].ap().bitcast(F32), 0.0)
            d = cx.v.memset(ptp[1].ap().bitcast(F32), 0.0)
            cx.fine[17].inc(d)
            t_.wait(cx.fine[17])
            c_sim = cx.fine[8]
            c_red = cx.fine[9]
            c_exp = cx.fine[10]
            c_lup = cx.fine[11]
            c_tp = cx.fine[12]
            c_ptev = [cx.fine[13], cx.fine[16]]
            c_av = cx.fine[14]
            c_y = cx.fine[15]
            sim0, red0, exp0, lup0, tp0, av0, y0, hd0 = (c.n for c in
                (c_sim, c_red, c_exp, c_lup, c_tp, c_av, c_y, c_hd))
            ptev0 = [c_ptev[0].n, c_ptev[1].n]
            cnum = 0
            gnum = 0
            ynum = 0
            tpmark = []
            avmark = []
            ldmark = {}

            ldh = [cx.dmac[11], cx.dmac[12]]

            def load_head(h):
                hb = h % 2
                if h >= 2:
                    q.wait(c_hd, hd0 + h - 1)
                r = slice(64 * h, 64 * (h + 1))
                ldh[hb].inc(q.dma_start(out=q_sb[:, hb, :], in_=cx.dram["q0_cm"][r, :]))
                ldh[hb].inc(q.dma_start(out=k_sb[:, hb, :], in_=cx.dram["k0_cm"][r, :]))
                ldh[hb].inc(q.dma_start(
                    out=v_sb[:, hb, :, :],
                    in_=cx.dram["v0_tm"][h, :, :].rearrange("(a p) d -> p a d", p=128)))
                ldmark[h] = ldh[hb].n
            load_head(0)
            load_head(1)
            for h in range(H):
                hb = h % 2
                if h + 2 < H:
                    load_head(h + 2)
                t_.wait_ge(ldh[h % 2].sem, ldmark[h])
                for i in range(32):
                    rows = (i + 1) * 128
                    nch = (rows + 1023) // 1024
                    ypar = ynum % 2
                    for c in range(nch):
                        jlo = 1024 * c
                        jw = min(1024, rows - jlo)
                        par = cnum % 2
                        if cnum >= 2:
                            t_.wait(c_exp, exp0 + cnum - 1)
                        for sl in range(0, jw, 512):
                            sw = min(512, jw - sl)
                            d = t_.matmul(psB[par][:, sl:sl+sw],
                                          q_sb[:, hb, 128*i:128*(i+1)],
                                          k_sb[:, hb, jlo+sl:jlo+sl+sw],
                                          start=True, stop=True)
                        c_sim.inc(d)
                        cv = cx.fine[5]
                        v.wait(c_sim, sim0 + cnum + 1)
                        if c == nch - 1:
                            d = v.tensor_add(psB[par][:, jw-128:jw],
                                             psB[par][:, jw-128:jw], madd[:])
                            cv.inc(d)
                            v.wait(cv)
                        if c == 0:
                            d = v.reduce_max(mrow[:, 3:4], psB[par][:, 0:jw],
                                             axis=AX.X, negate=True)
                            c_red.inc(d)
                        else:
                            d = v.reduce_max(mrow[:, 1:2], psB[par][:, 0:jw],
                                             axis=AX.X, negate=True)
                            cv.inc(d)
                            v.wait(cv)
                            d = v.tensor_tensor(mrow[:, 1:2], mrow[:, 3:4],
                                                mrow[:, 1:2], op=ALU.min)
                            cv.inc(d)
                            v.wait(cv)
                            d = v.tensor_sub(mrow[:, 2:3], mrow[:, 1:2], mrow[:, 3:4])
                            cv.inc(d)
                            v.wait(cv)
                            d = v.tensor_copy(mrow[:, 3:4], mrow[:, 1:2])
                            c_red.inc(d)
                        s.wait(c_red, red0 + cnum + 1)
                        if cnum >= 1:
                            s.wait(c_lup, lup0 + cnum)
                            s.wait(c_fold, fold0 + cnum)
                        if cnum >= 2:
                            s.wait(c_tp, tpmark[cnum - 2])
                        if c > 0:
                            s.activation(mrow[:, 6:7], mrow[:, 2:3], AF.Exp)
                        lcc = 4 if cnum % 2 == 0 else 8
                        d = s.activation(P_sb[:, par, 0:jw], psB[par][:, 0:jw], AF.Exp,
                                         bias=mrow[:, 3:4],
                                         accum_out=mrow[:, lcc:lcc+1])
                        c_exp.inc(d)
                        v.wait(c_exp, exp0 + cnum + 1)
                        if c == 0:
                            d = v.tensor_copy(mrow[:, 5:6], mrow[:, lcc:lcc+1])
                            c_lup.inc(d)
                        else:
                            d = v.scalar_tensor_tensor(mrow[:, 5:6], mrow[:, 5:6],
                                                       mrow[:, 6:7], mrow[:, lcc:lcc+1],
                                                       op0=ALU.mult, op1=ALU.add)
                            c_lup.inc(d)
                        for sl in range(0, jw, 512):
                            sw = min(512, jw - sl)
                            gpar = gnum % 2
                            ntp = (sw + 127) // 128
                            if gnum >= 2:
                                t_.wait(c_ptev[gpar], ptev0[gpar] + gnum // 2)
                            t_.wait(c_exp, exp0 + cnum + 1)
                            for jt in range(ntp):
                                w2 = min(128, sw - 128 * jt)
                                d = t_.transpose(ptp[gpar][0:w2, 128*jt:128*jt+128],
                                                 P_sb[:, par, sl+128*jt:sl+128*jt+w2],
                                                 idb[:])
                            c_tp.inc(d)
                            eng = v if (gnum % 2 == 0) else s
                            eng.wait(c_tp)
                            if eng is s:
                                d = s.activation(PT_sb[:, gpar, :], ptp[gpar][:, 0:512],
                                                 AF.Copy)
                            else:
                                d = v.tensor_copy(PT_sb[:, gpar, :], ptp[gpar][:, 0:512])
                            c_ptev[gpar].inc(d)
                            t_.wait(c_ptev[gpar], ptev0[gpar] + gnum // 2 + 1)
                            if sl == 0 and cnum >= 2:
                                t_.wait(c_fold, fold0 + cnum - 1)
                            cpar = cnum % 2
                            for jt in range(ntp):
                                w2 = min(128, sw - 128 * jt)
                                kt = (jlo + sl + 128 * jt) // 128
                                d = t_.matmul(pav[cpar][:, 0:D],
                                              PT_sb[0:w2, gpar, 128*jt:128*jt+128],
                                              v_sb[0:w2, hb, kt, :],
                                              start=(sl == 0 and jt == 0),
                                              stop=(sl + 512 >= jw
                                                    and jt == ntp - 1))
                            c_av.inc(d)
                            gnum += 1
                        v.wait(c_av)
                        if c == 0:
                            d = v.tensor_copy(yacc[:, ypar, :], pav[cnum % 2][:, 0:D])
                        else:
                            v.wait(c_fold)
                            d = v.scalar_tensor_tensor(yacc[:, ypar, :],
                                                       yacc[:, ypar, :], mrow[:, 6:7],
                                                       pav[cnum % 2][:, 0:D],
                                                       op0=ALU.mult, op1=ALU.add)
                        c_fold.inc(d)
                        tpmark.append(c_tp.n)
                        avmark.append(c_av.n)
                        cnum += 1
                    v.wait(c_fold)
                    cv = cx.fine[5]
                    d = v.reciprocal(mrow[:, 7:8], mrow[:, 5:6])
                    cv.inc(d)
                    v.wait(cv)
                    d = v.tensor_scalar(ytmp[:, ypar, :], yacc[:, ypar, :],
                                        mrow[:, 7:8], None, op0=ALU.mult)
                    c_y.inc(d)
                    gpar = gnum % 2
                    if gnum >= 2:
                        t_.wait(c_ptev[gpar], ptev0[gpar] + gnum // 2)
                    t_.wait(c_y)
                    d = t_.transpose(ptp[gpar][0:D, 0:128], ytmp[:, ypar, :], idb[:])
                    c_tp.inc(d)
                    s.wait(c_tp)
                    d = s.activation(ycm[(h % 2)*64:(h % 2)*64+64, h // 2,
                                         128*i:128*(i+1)],
                                     ptp[gpar][0:D, 0:128], AF.Copy)
                    c_ptev[gpar].inc(d)
                    gnum += 1
                    ynum += 1
                d = t_.nop()
                c_hd.inc(d)
        cx.barrier()
        if "ycm_dbg" in cx.dram:
            for ci in range(NCH):
                sp.inc(q.dma_start(out=cx.dram["ycm_dbg"][ci, :, :],
                                   in_=ycm[:, ci, :]))
            cx.barrier()
        with cx.scope():
            wps = cx.sbuf("b_wp", [128, NCH, C], BF16)
            bprow = cx.sbuf("b_bprow", [1, C], F32)
            bppp = cx.sbuf("b_bppp", [128, NCH], F32)
            asb = cx.sbuf("b_asb", [128, 2, 512], F32)
            pcol3 = cx.psum("b_pcol3", [128, 8])
            pa = [cx.psum(f"b_pa{i}", [128, 512]) for i in range(2)]
            c_pp = cx.fine[8]
            c_mm = cx.fine[9]
            c_ev = cx.fine[10]
            lgw3 = cx.dmac[14]
            ldr3 = cx.dmac[9]
            spb3 = [cx.dmac[13], cx.dmac[8]]
            for ci in range(NCH):
                lgw3.inc(g.dma_start(out=wps[:, ci, :],
                                     in_=cx.dram["L0_attn_Wp"][128*ci:128*(ci+1), :]))
            emit_row_load(cx, cx.dram["L0_attn_bp"][0:1, :], bprow[:], ldr3)
            t_.wait(ldr3)
            t_.wait(lgw3)
            emit_pp_from_row(cx, bprow[:], C, bppp, pcol3, c_pp)
            s.wait(c_pp)
            gi = 0
            mm0, ev0 = c_mm.n, c_ev.n
            for it in range(NT):
                t0 = 512 * it
                for ob in range(NCH):
                    slot = gi % 2
                    if gi >= 2:
                        t_.wait(c_ev, ev0 + gi - 1)
                    for ci in range(NCH):
                        d = t_.matmul(pa[slot][:], wps[:, ci, 128*ob:128*(ob+1)],
                                      ycm[:, ci, t0:t0+512], start=(ci == 0),
                                      stop=(ci == NCH - 1))
                    c_mm.inc(d)
                    s.wait(c_mm)
                    if gi >= 2:
                        s.wait(spb3[slot])
                    d = s.activation(asb[:, slot, :], pa[slot][:], AF.Identity,
                                     bias=bppp[:, ob:ob+1])
                    c_ev.inc(d)
                    q.wait(c_ev, ev0 + gi + 1)
                    spb3[slot].inc(q.dma_start(
                        out=cx.dram["A_cm"][128*ob:128*(ob+1), t0:t0+512],
                        in_=asb[:, slot, :]))
                    gi += 1
    cx.barrier()


def _emit_rope_tables(cx, tabs, sc, pq0, pq1, xpos=True):
    t_, v, s, g = cx.t, cx.v, cx.s, cx.g
    cosS, sinS, cosI, sinI = tabs
    irow, r32, wk, halfpi = sc["irow"], sc["r32"], sc["wk"], sc["halfpi"]
    cvv = cx.fine[4]
    cvg = cx.fine[24]
    cvs = cx.fine[25]
    cvt = cx.fine[18]
    MG = MAGIC
    d = g.iota(r32[0:1, 0:32], pattern=[[1, 32]], base=0, channel_multiplier=0,
               allow_small_or_imprecise_dtypes=True)
    cvg.inc(d)
    s.wait(cvg)
    d = s.activation(r32[:, 32:64], r32[:, 0:32], AF.Exp,
                     scale=-float(np.log(10000.0) / 32.0))
    cvs.inc(d)
    v.wait(cvg)
    if xpos:
        d = v.tensor_scalar(r32[:, 64:96], r32[:, 0:32], float(2.0 / 89.6),
                            float(25.6 / 89.6), op0=ALU.mult, op1=ALU.add)
        cvv.inc(d)
        s.wait(cvv)
        d = s.activation(r32[:, 64:96], r32[:, 64:96], AF.Ln)
        cvs.inc(d)
    else:
        d = v.memset(r32[:, 64:96], 0.0)
        cvv.inc(d)
        s.wait(cvv)
    t_.wait(cvs)
    for half, pq in ((0, pq0), (1, pq1)):
        for phase in (0, 1):
            for j in range(16):
                tile = 16 * half + j
                g.wait(cvt)
                d = g.iota(irow[0:1, 0:128], pattern=[[1, 128]], base=128 * tile,
                           channel_multiplier=0,
                           allow_small_or_imprecise_dtypes=True)
                cvg.inc(d)
                if phase == 1:
                    v.wait(cvg)
                    d = v.tensor_scalar(irow[0:1, 128:256], irow[0:1, 0:128],
                                        float(1.0 / 128.0), -16.0,
                                        op0=ALU.mult, op1=ALU.add)
                    cvv.inc(d)
                    t_.wait(cvv)
                else:
                    t_.wait(cvg)
                rr = r32[0:1, 32:64] if phase == 0 else r32[0:1, 64:96]
                ri = irow[0:1, 0:128] if phase == 0 else irow[0:1, 128:256]
                d = t_.matmul(pq[:, 32*j:32*(j+1)], ri, rr, start=True, stop=True)
                cvt.inc(d)
                if phase == 0 and j == 15:
                    v.wait(cvt)
                    d = v.tensor_scalar(wk[:, 0, :], pq[:], float(1 / (2 * np.pi)),
                                        MG, op0=ALU.mult, op1=ALU.add)
                    cvv.inc(d)
                    d = v.tensor_scalar(wk[:, 1, :], pq[:], float(1 / (2 * np.pi)),
                                        0.25, op0=ALU.mult, op1=ALU.add)
                    cvv.inc(d)
                    v.wait(cvv)
                    d = v.tensor_scalar(wk[:, 1, :], wk[:, 1, :], MG, None,
                                        op0=ALU.add)
                    cvv.inc(d)
                    v.wait(cvv)
                    d = v.tensor_scalar(wk[:, 2, :], wk[:, 0, :], MG, -TPI_HI,
                                        op0=ALU.subtract, op1=ALU.mult)
                    cvv.inc(d)
                    d = v.tensor_scalar(wk[:, 0, :], wk[:, 0, :], MG, -TPI_LO,
                                        op0=ALU.subtract, op1=ALU.mult)
                    cvv.inc(d)
                    d = v.tensor_scalar(wk[:, 3, :], wk[:, 1, :], MG, -TPI_HI,
                                        op0=ALU.subtract, op1=ALU.mult)
                    cvv.inc(d)
                    d = v.tensor_scalar(wk[:, 1, :], wk[:, 1, :], MG, -TPI_LO,
                                        op0=ALU.subtract, op1=ALU.mult)
                    cvv.inc(d)
                    v.wait(cvv)
                    d = v.scalar_tensor_tensor(wk[:, 2, :], pq[:], 1.0, wk[:, 2, :],
                                               op0=ALU.mult, op1=ALU.add)
                    cvv.inc(d)
                    v.wait(cvv)
                    d = v.tensor_add(wk[:, 0, :], wk[:, 0, :], wk[:, 2, :])
                    cvv.inc(d)
                    v.wait(cvv)
                    d = v.scalar_tensor_tensor(wk[:, 2, :], pq[:], 1.0, wk[:, 3, :],
                                               op0=ALU.mult, op1=ALU.add)
                    cvv.inc(d)
                    v.wait(cvv)
                    d = v.scalar_tensor_tensor(wk[:, 1, :], wk[:, 1, :],
                                               float(np.pi / 2), wk[:, 2, :],
                                               op0=ALU.add, op1=ALU.add)
                    cvv.inc(d)
                    v.wait(cvv)
                    d = v.tensor_scalar(wk[:, 0, :], wk[:, 0, :],
                                        -float(np.pi), float(np.pi),
                                        op0=ALU.max, op1=ALU.min)
                    cvv.inc(d)
                    d = v.tensor_scalar(wk[:, 1, :], wk[:, 1, :],
                                        -float(np.pi), float(np.pi),
                                        op0=ALU.max, op1=ALU.min)
                    cvv.inc(d)
                    s.wait(cvv)
                    d = s.activation(wk[:, 2, :], wk[:, 0, :], AF.Sin)
                    cvs.inc(d)
                    d = s.activation(wk[:, 3, :], wk[:, 1, :], AF.Sin)
                    cvs.inc(d)
                    t_.wait(cvs)
        s.wait(cvt)
        d = s.activation(wk[:, 0, :], pq[:], AF.Exp)
        cvs.inc(d)
        d = s.activation(wk[:, 1, :], pq[:], AF.Exp, scale=-1.0)
        cvs.inc(d)
        v.wait(cvs)
        def tv(tab, dup):
            return tab.ap().rearrange("p (tile j) -> p tile j", tile=32, j=64)[
                :, 16*half:16*(half+1), 32*dup:32*(dup+1)]
        wv3 = wk.ap().rearrange("p a (tile j) -> p a tile j", tile=16, j=32)
        for dup in (0, 1):
            sgn = -1.0 if dup == 0 else 1.0
            for dd in (v.tensor_tensor(tv(cosS, dup), wv3[:, 3, :, :],
                                       wv3[:, 0, :, :], op=ALU.mult),
                       v.scalar_tensor_tensor(tv(sinS, dup), wv3[:, 2, :, :],
                                              sgn, wv3[:, 0, :, :],
                                              op0=ALU.mult, op1=ALU.mult),
                       v.tensor_tensor(tv(cosI, dup), wv3[:, 3, :, :],
                                       wv3[:, 1, :, :], op=ALU.mult),
                       v.scalar_tensor_tensor(tv(sinI, dup), wv3[:, 2, :, :],
                                              sgn, wv3[:, 1, :, :],
                                              op0=ALU.mult, op1=ALU.mult)):
                cvv.inc(dd)
        t_.wait(cvv)


def _emit_rope_apply(cx, eng, cc, src_ap, dst_bf_ap, rstd_ap, scale_ap, ta, tb,
                     tmp, prior=None):
    Hh = 8
    if prior is not None:
        eng.wait_ge(prior[0].sem, prior[1])
    skip_scale = rstd_ap is None
    q3 = src_ap.rearrange("p (h j) -> p h j", h=Hh, j=64)
    qh3 = tmp[:, 0, :].rearrange("p (h j) -> p h j", h=Hh, j=64)
    t13 = tmp[:, 1, :].rearrange("p (h j) -> p h j", h=Hh, j=64)
    ta_b = ta.unsqueeze(1).broadcast_to([128, Hh, 64])
    eng.wait(cc)
    if not skip_scale:
        rstd_b = rstd_ap.unsqueeze(2).broadcast_to([128, Hh, 64])
        qs_b = scale_ap.unsqueeze(1).broadcast_to([128, Hh, 64])
        cc.inc(eng.tensor_tensor(qh3, q3, rstd_b, op=ALU.mult))
        eng.wait(cc)
        cc.inc(eng.tensor_tensor(qh3, qh3, qs_b, op=ALU.mult))
        eng.wait(cc)
        cc.inc(eng.tensor_tensor(t13, qh3, ta_b, op=ALU.mult))
    else:
        cc.inc(eng.tensor_copy(tmp[:, 0, :], src_ap))
        eng.wait(cc)
        cc.inc(eng.tensor_tensor(t13, qh3, ta_b, op=ALU.mult))
    qh4 = tmp[:, 0, :].rearrange("p (h hf r) -> p h hf r", h=Hh, hf=2, r=32)
    t24 = tmp[:, 2, :].rearrange("p (h hf r) -> p h hf r", h=Hh, hf=2, r=32)
    tb4 = tb.rearrange("p (hf r) -> p hf r", hf=2, r=32)
    for hf in range(2):
        tb_b = tb4[:, hf, :].unsqueeze(1).broadcast_to([128, Hh, 32])
        cc.inc(eng.tensor_tensor(t24[:, :, hf, :], qh4[:, :, 1 - hf, :], tb_b,
                                 op=ALU.mult))
    eng.wait(cc)
    dst3 = dst_bf_ap.rearrange("p (h j) -> p h j", h=Hh, j=64)
    t23 = tmp[:, 2, :].rearrange("p (h j) -> p h j", h=Hh, j=64)
    return eng.tensor_tensor(dst3, t13, t23, op=ALU.add)


def emit_l0_ffn(cx: Ctx):
    t_, v, s, g, q = cx.t, cx.v, cx.s, cx.g, cx.q
    Hh, D = 16, 64
    W = 256
    NW = T // W
    ld, lg, sp = cx.ld, cx.lg, cx.sp
    with cx.scope():
        idb = cx.sbuf("c_idb", [128, 128], BF16)
        mk = cx.sbuf("c_mk", [128, 4, 512], BF16)
        c_ini = cx.fine[4]
        with cx.scope():
            iot = cx.sbuf("c_iot", [128, 512], F32)
            vcol = cx.sbuf("c_vcol", [128, 512], F32)
            w2 = cx.sbuf("c_mw", [128, 2, 512], F32)
            d = g.iota(iot[:], pattern=[[1, 512]], base=0, channel_multiplier=0,
                       allow_small_or_imprecise_dtypes=True)
            cx.fine[24].inc(d)
            d = g.iota(vcol[:], pattern=[[0, 512]], base=0, channel_multiplier=1,
                       allow_small_or_imprecise_dtypes=True)
            cx.fine[24].inc(d)
            v.wait(cx.fine[24])
            d = v.tensor_tensor(idb[:], iot[:, 0:128], vcol[:, 0:128],
                                op=ALU.is_equal)
            c_ini.inc(d)
            for sub in range(2):
                qi = 128.0 * sub
                d = v.tensor_scalar(w2[:, 0, :], vcol[:], qi, None, op0=ALU.add)
                c_ini.inc(d)
                v.wait(c_ini)
                d = v.tensor_tensor(w2[:, 1, :], iot[:], w2[:, 0, :], op=ALU.is_lt)
                c_ini.inc(d)
                d = v.tensor_scalar(w2[:, 0, :], w2[:, 0, :], 256.0, None,
                                    op0=ALU.add)
                c_ini.inc(d)
                v.wait(c_ini)
                d = v.tensor_tensor(w2[:, 0, :], iot[:], w2[:, 0, :], op=ALU.is_gt)
                c_ini.inc(d)
                v.wait(c_ini)
                d = v.tensor_add(w2[:, 0, :], w2[:, 0, :], w2[:, 1, :])
                c_ini.inc(d)
                v.wait(c_ini)
                d = v.tensor_scalar(mk[:, sub, :], w2[:, 0, :], -1e30, None,
                                    op0=ALU.mult)
                c_ini.inc(d)
                v.wait(c_ini)
                d = v.tensor_scalar(w2[:, 0, 0:256], vcol[:, 0:256], qi, None,
                                    op0=ALU.add)
                c_ini.inc(d)
                v.wait(c_ini)
                d = v.tensor_tensor(w2[:, 1, 0:256], iot[:, 0:256], w2[:, 0, 0:256],
                                    op=ALU.is_gt)
                c_ini.inc(d)
                v.wait(c_ini)
                d = v.tensor_scalar(mk[:, 2 + sub, 0:256], w2[:, 1, 0:256], -1e30,
                                    None, op0=ALU.mult)
                c_ini.inc(d)
            t_.wait(c_ini)
            s.wait(c_ini)
            g.wait(c_ini)
        cx.barrier()
        import os
        nlay = int(os.environ.get("C_LAYERS", "6"))
        parts = os.environ.get("C_PARTS", "af")
        for layer in range(nlay):
            if "a" in parts:
                _emit_c_attn_sweep(cx, layer, idb, mk)
                cx.barrier()
            if "f" in parts:
                _emit_c_ffn_sweep(cx, layer)
                cx.barrier()
                if "zlay_dbg" in cx.dram:
                    for ci in range(NCH):
                        cx.sp.inc(cx.q.dma_start(
                            out=cx.dram["zlay_dbg"][layer, 128*ci:128*(ci+1), :],
                            in_=cx.dram["z_cm"][128*ci:128*(ci+1), :]))
                    cx.barrier()


def _emit_c_attn_sweep(cx: Ctx, layer, idb, mk):
    t_, v, s, g, q = cx.t, cx.v, cx.s, cx.g, cx.q
    Hh, D = 16, 64
    W = 256
    NW = T // W
    ld, lg, sp = cx.ld, cx.lg, cx.sp
    QK_SCALE = 8.0
    with cx.scope():
        wqkv = cx.sbuf("ca_wqkv", [128, NCH, 3072], BF16)
        wo = cx.sbuf("ca_wo", [128, NCH, C], BF16)
        cosS = cx.sbuf("ca_cosS", [128, 2048], F32)
        sinS = cx.sbuf("ca_sinS", [128, 2048], F32)
        cosI = cx.sbuf("ca_cosI", [128, 2048], F32)
        sinI = cx.sbuf("ca_sinI", [128, 2048], F32)
        tbsc = {
            "irow": cx.sbuf("ca_irow", [1, 384], F32),
            "r32": cx.sbuf("ca_r32", [1, 96], F32),
            "wk": cx.sbuf("ca_wk", [128, 4, 512], F32),
            "halfpi": cx.sbuf("ca_halfpi", [128, 1], F32),
        }
        qsrow = cx.sbuf("ca_qsrow", [1, 128], F32)
        qs_b = cx.sbuf("ca_qsb", [128, 128], F32)
        zf = cx.sbuf("ca_zf", [128, 2, NCH, 256], F32)
        zb = cx.sbuf("ca_zb", [128, 2, NCH, 257], BF16)
        qk_tm = cx.sbuf("ca_qktm", [128, 2048], F32)
        qsq = cx.sbuf("ca_qsq", [128, 1024], F32)
        rss = cx.sbuf("ca_rss", [128, 2, 32], F32)
        ropet_v = cx.sbuf("ca_rtv", [128, 3, 512], F32)
        ropet_g = cx.sbuf("ca_rtg", [128, 3, 512], F32)
        qro = cx.sbuf("ca_qro", [128, 2, 2048], BF16)
        vring = cx.sbuf("ca_vring", [128, 2, 2, 1024], BF16)
        qcm = cx.sbuf("ca_qcm", [128, NCH, 256], BF16)
        kcm = cx.sbuf("ca_kcm", [128, 2, NCH, 256], BF16)
        P_sb = cx.sbuf("ca_psb", [128, 2, 512], BF16)
        PT_sb = cx.sbuf("ca_ptsb", [128, 2, 512], BF16)
        ytmp = cx.sbuf("ca_ytmp", [128, 2, D], BF16)
        ycm_w = cx.sbuf("ca_ycmw", [128, 2, NCH, 256], BF16)
        zout = cx.sbuf("ca_zout", [128, 2, 256], F32)
        lrow = cx.sbuf("ca_lrow", [128, 2, 4], F32)
        psim = [cx.psum(f"ca_psim{i}", [128, 512]) for i in range(2)]
        ptp = [cx.psum(f"ca_ptp{i}", [128, 1024], BF16) for i in range(2)]
        pav = [cx.psum(f"ca_pav{i}", [128, 512]) for i in range(2)]
        pq = [cx.psum(f"ca_pq{i}", [128, 512]) for i in range(2)]
        cv = cx.fine[4]
        c_qs = cx.fine[5]
        c_ev = [cx.fine[6], cx.fine[7]]
        c_mm = cx.fine[8]
        c_rp = cx.fine[9]
        c_rpg = cx.fine[10]
        c_tp2 = cx.fine[12]
        c_sim2 = cx.fine[13]
        c_red2 = cx.fine[14]
        c_exp2 = cx.fine[15]
        c_avc = cx.fine[16]
        c_yv = cx.fine[17]
        c_wo = cx.fine[20]
        c_zo = cx.fine[21]
        c_ptev = [cx.fine[22], cx.fine[23]]
        lgw = cx.dmac[14]
        ldr = cx.dmac[9]
        ldz = [cx.dmac[11], cx.dmac[12]]
        lgz = [cx.dmac[15], cx.dmac[16]]
        spz = cx.dmac[13]
        for ci in range(NCH):
            r = slice(128 * ci, 128 * (ci + 1))
            lgw.inc(g.dma_start(out=wqkv[:, ci, :],
                                in_=cx.dram["L0_ffn_Wqkv"][layer, r, :]))
            lgw.inc(g.dma_start(out=wo[:, ci, :],
                                in_=cx.dram["L0_ffn_Wo"][layer, r, :]))
        ldr.inc(q.dma_start(out=qsrow[0:1, 0:64],
                            in_=cx.dram["L0_ffn_q_scale"][layer:layer+1, :]))
        ldr.inc(q.dma_start(out=qsrow[0:1, 64:128],
                            in_=cx.dram["L0_ffn_k_scale"][layer:layer+1, :]))
        d = v.memset(tbsc["halfpi"][:], float(np.pi / 2))
        cv.inc(d)
        s.wait(cv)
        g.wait(cv)
        _emit_rope_tables(cx, (cosS, sinS, cosI, sinI), tbsc, pq[0], pq[1])
        t_.wait(ldr)
        t_.wait(cv)
        t_.wait(lgw)
        t_.matmul(pq[0][:, 0:64], cx.sb["ones_row"][0:1, :], qsrow[0:1, 0:64],
                  start=True, stop=True)
        d = t_.matmul(pq[0][:, 64:128], cx.sb["ones_row"][0:1, :],
                      qsrow[0:1, 64:128], start=True, stop=True)
        c_qs.inc(d)
        v.wait(c_qs)
        d = v.tensor_copy(qs_b[:], pq[0][:, 0:128])
        c_qs.inc(d)
        v.wait(c_qs)
        d = v.tensor_scalar(qs_b[:, 0:64], qs_b[:, 0:64], QK_SCALE, None,
                            op0=ALU.mult)
        c_qs.inc(d)

        gq = 0
        gt = 0
        ug_g = 0
        mm0 = c_mm.n
        ev0 = [c_ev[0].n, c_ev[1].n]
        rp0, rpg0, tp20 = c_rp.n, c_rpg.n, c_tp2.n
        sim20, red20, exp20, avc0, yv0 = (c_sim2.n, c_red2.n, c_exp2.n,
                                          c_avc.n, c_yv.n)
        wo0, zo0 = c_wo.n, c_zo.n
        ptev0 = [c_ptev[0].n, c_ptev[1].n]
        zsrc = cx.dram["xn_cm"] if layer == 0 else cx.dram["z_cm"]
        ldm, lgm = {}, {}
        tpP = {}
        ytpm = {}
        simmark = {}
        avmark = {}
        yemark = {}
        womark = {}
        rpmark = {}
        tpqmark = {}
        for w in range(NW + 1):
            buf = w % 2
            t0 = W * w
            if w < NW:
                if w >= 2:
                    q.wait(c_mm, mm0 + 12 * (w - 1))
                    q.wait(c_zo, womark.get(w - 2, zo0))
                    g.wait(c_mm, mm0 + 12 * (w - 1))
                    g.wait(cx.fine[24], cx.fine[24].n - 1)
                for ci in range(NCH):
                    ldz[buf].inc(q.dma_start(
                        out=zf[:, buf, ci, :],
                        in_=zsrc[128*ci:128*(ci+1), t0:t0 + W]))
                czb = cx.fine[11]
                g.wait_ge(ldz[buf].sem, ldz[buf].n)
                for ci in range(NCH):
                    d = g.tensor_copy(zb[:, buf, ci, 1:257], zf[:, buf, ci, :])
                    czb.inc(d)
                g.wait(czb)
                if w == 0:
                    d = g.memset(zb[:, 0, :, 0:1], 0.0)
                else:
                    d = g.tensor_copy(zb[:, buf, :, 0:1], zb[:, 1 - buf, :, 256:257])
                cx.fine[24].inc(d)
                ldm[w], lgm[w] = ldz[buf].n, czb.n
            if w >= 1:
                pw = w - 1
                pbuf = pw % 2
                for ob in range(NCH):
                    osl = ob % 2
                    t_.wait(c_ptev[0], yemark[pw][0])
                    t_.wait(c_ptev[1], yemark[pw][1])
                    if ob >= 2:
                        t_.wait(c_zo, c_zo.n - 1)
                    for ci in range(NCH):
                        d = t_.matmul(psim[osl][:, 0:256],
                                      wo[:, ci, 128*ob:128*(ob+1)],
                                      ycm_w[:, pbuf, ci, :],
                                      start=(ci == 0), stop=(ci == NCH - 1))
                    c_wo.inc(d)
                    v.wait(c_wo)
                    if pw >= 1 or ob >= 2:
                        v.wait(spz)
                    d = v.scalar_tensor_tensor(zout[:, osl, :], psim[osl][:, 0:256],
                                               1.0, zf[:, pbuf, ob, :],
                                               op0=ALU.mult, op1=ALU.add)
                    c_zo.inc(d)
                    q.wait(c_zo)
                    spz.inc(q.dma_start(
                        out=cx.dram["z_cm"][128*ob:128*(ob+1), W*pw:W*(pw+1)],
                        in_=zout[:, osl, :]))
                womark[pw] = c_zo.n
            if w >= NW:
                continue
            t_.wait_ge(ldz[buf].sem, ldm[w])
            t_.wait(cx.fine[11], lgm[w])
            t_.wait(c_qs)
            t_.wait(cx.fine[24])
            for tt in range(2):
                for nb in range(6):
                    slot = gq % 2
                    if gq >= 2:
                        t_.wait(c_ev[slot], ev0[slot] + gq // 2)
                    for ci in range(NCH):
                        off = (1 if ci < 4 else 0) + 128 * tt
                        d = t_.matmul(pq[slot][:],
                                      zb[:, buf, ci, off:off + 128],
                                      wqkv[:, ci, 512*nb:512*(nb+1)],
                                      start=(ci == 0), stop=(ci == NCH - 1))
                    c_mm.inc(d)
                    eng = v if slot == 0 else s
                    eng.wait(c_mm)
                    if nb < 4:
                        pm = rpmark.get((w, tt - 1)) or rpmark.get((w - 1, 1))
                        if pm:
                            eng.wait(c_rp, pm[0])
                            eng.wait(c_rpg, pm[1])
                        if eng is s:
                            d = s.activation(qk_tm[:, 512*nb:512*(nb+1)],
                                             pq[slot][:], AF.Copy)
                        else:
                            d = v.tensor_copy(qk_tm[:, 512*nb:512*(nb+1)],
                                              pq[slot][:])
                    else:
                        if w >= 1:
                            eng.wait(c_avc, avmark[w - 1])
                        if eng is s:
                            d = s.activation(vring[:, buf, tt, 512*(nb-4):512*(nb-3)],
                                             pq[slot][:], AF.Copy)
                        else:
                            d = v.tensor_copy(vring[:, buf, tt, 512*(nb-4):512*(nb-3)],
                                              pq[slot][:])
                    c_ev[slot].inc(d)
                    gq += 1
                s.wait(c_ev[0], c_ev[0].n)
                s.wait(c_ev[1], c_ev[1].n)
                pm = rpmark.get((w, tt - 1)) or rpmark.get((w - 1, 1))
                if pm:
                    s.wait(c_rp, pm[0])
                for qk in range(2):
                    if qk == 1:
                        s.wait(cv)
                    d = s.activation(qsq[:, 0:1024],
                                     qk_tm[:, 1024*qk:1024*(qk+1)], AF.Square)
                    c_rp.inc(d)
                    v.wait(c_rp)
                    d = v.reduce_sum(rss[:, tt, 16*qk:16*(qk+1)],
                                     qsq.ap().rearrange("p (h j) -> p h j", h=Hh, j=64),
                                     axis=AX.X)
                    cv.inc(d)
                    v.wait(cv)
                    d = v.tensor_scalar_max(rss[:, tt, 16*qk:16*(qk+1)],
                                            rss[:, tt, 16*qk:16*(qk+1)], 1e-24)
                    c_rp.inc(d)
                s.wait(c_rp)
                d = s.activation(rss[:, tt, :], rss[:, tt, :], AF.Ln)
                cx.fine[25].inc(d)
                s.wait(cx.fine[25])
                d = s.activation(rss[:, tt, :], rss[:, tt, :], AF.Exp, scale=-0.5)
                c_rp.inc(d)
                rstd_mark = c_rp.n
                tile = 2 * w + tt
                v.wait(c_rp, rstd_mark)
                for hg in range(2):
                    d = _emit_rope_apply(cx, v, cv,
                                         qk_tm.ap()[:, 512*hg:512*(hg+1)],
                                         qro.ap()[:, tt, 512*hg:512*(hg+1)],
                                         rss.ap()[:, tt, 8*hg:8*(hg+1)],
                                         qs_b.ap()[:, 0:64],
                                         cosS[:, 64*tile:64*(tile+1)],
                                         sinS[:, 64*tile:64*(tile+1)], ropet_v,
                                         prior=(c_rp, c_rp.n))
                    c_rp.inc(d)
                g.wait(c_rp, rstd_mark)
                for hg in range(2):
                    d = _emit_rope_apply(cx, g, cx.fine[24],
                                         qk_tm.ap()[:, 1024 + 512*hg:1024 + 512*(hg+1)],
                                         qro.ap()[:, tt, 1024 + 512*hg:1024 + 512*(hg+1)],
                                         rss.ap()[:, tt, 16 + 8*hg:16 + 8*(hg+1)],
                                         qs_b.ap()[:, 64:128],
                                         cosI[:, 64*tile:64*(tile+1)],
                                         sinI[:, 64*tile:64*(tile+1)], ropet_g,
                                         prior=(c_rpg, c_rpg.n))
                    c_rpg.inc(d)
                rpmark[(w, tt)] = (c_rp.n, c_rpg.n)
                for qk in range(2):
                    gpar = gt % 2
                    t_.wait(c_rp if qk == 0 else c_rpg)
                    if gt >= 2:
                        t_.wait(c_ptev[gpar], ptev0[gpar] + gt // 2)
                    for hh in range(Hh):
                        d = t_.transpose(
                            ptp[gpar][(hh % 2)*64:(hh % 2)*64+64,
                                      128*(hh//2):128*(hh//2)+128],
                            qro[:, tt, 1024*qk + 64*hh:1024*qk + 64*(hh+1)],
                            idb[:])
                    c_tp2.inc(d)
                    eng = v if gpar == 0 else s
                    eng.wait(c_tp2)
                    dst = (qcm.ap()[:, :, 128*tt:128*(tt+1)] if qk == 0 else
                           kcm.ap()[:, buf, :, 128*tt:128*(tt+1)])
                    src3 = ptp[gpar][:, :].rearrange("p (a b) -> p a b", a=NCH, b=128)
                    if eng is s:
                        d = s.activation(dst, src3, AF.Copy)
                    else:
                        d = v.tensor_copy(dst, src3)
                    c_ptev[gpar].inc(d)
                    gt += 1
            tpqmark[w] = (c_ptev[0].n, c_ptev[1].n)
            for h in range(Hh):
                pb = (h % 2) * 64
                hp = h // 2
                for tqt in range(2):
                    upar = ug_g % 2
                    sl = ug_g % 2
                    if ug_g >= 2:
                        t_.wait(c_exp2, exp20 + ug_g - 1)
                    if h == 0 and tqt == 0:
                        t_.wait(c_ptev[0], tpqmark[w][0])
                        t_.wait(c_ptev[1], tpqmark[w][1])
                        if w >= 1:
                            t_.wait(c_zo, womark[w - 1])
                    mki = (2 + tqt) if w == 0 else tqt
                    kw = 256 if w == 0 else 512
                    t_.matmul(psim[sl][:, 0:kw], idb[:], mk[:, mki, 0:kw],
                              start=True, stop=False)
                    if w == 0:
                        d = t_.matmul(psim[sl][:, 0:256],
                                      qcm[pb:pb+64, hp, 128*tqt:128*(tqt+1)],
                                      kcm[pb:pb+64, buf, hp, :],
                                      start=False, stop=True)
                    else:
                        t_.matmul(psim[sl][:, 0:256],
                                  qcm[pb:pb+64, hp, 128*tqt:128*(tqt+1)],
                                  kcm[pb:pb+64, 1 - buf, hp, :],
                                  start=False, stop=False)
                        d = t_.matmul(psim[sl][:, 256:512],
                                      qcm[pb:pb+64, hp, 128*tqt:128*(tqt+1)],
                                      kcm[pb:pb+64, buf, hp, :],
                                      start=False, stop=True)
                    c_sim2.inc(d)
                    v.wait(c_sim2, sim20 + ug_g + 1)
                    d = v.reduce_max(lrow[:, upar, 0:1], psim[sl][:, 0:kw],
                                     axis=AX.X, negate=True)
                    c_red2.inc(d)
                    s.wait(c_red2, red20 + ug_g + 1)
                    if ug_g >= 2:
                        s.wait(c_tp2, tpP[ug_g - 2])
                        s.wait(c_yv, yv0 + ug_g - 1)
                    d = s.activation(P_sb[:, upar, 0:kw], psim[sl][:, 0:kw],
                                     AF.Exp, bias=lrow[:, upar, 0:1],
                                     accum_out=lrow[:, upar, 1:2])
                    c_exp2.inc(d)
                    v.wait(c_exp2, exp20 + ug_g + 1)
                    d = v.reciprocal(lrow[:, upar, 2:3], lrow[:, upar, 1:2])
                    cv.inc(d)
                    v.wait(cv)
                    gpar = gt % 2
                    if gt >= 2:
                        t_.wait(c_ptev[gpar], ptev0[gpar] + gt // 2)
                    t_.wait(c_exp2, exp20 + ug_g + 1)
                    ntp = kw // 128
                    for jt in range(ntp):
                        d = t_.transpose(ptp[gpar][:, 128*jt:128*(jt+1)],
                                         P_sb[:, upar, 128*jt:128*(jt+1)], idb[:])
                    c_tp2.inc(d)
                    tpP[ug_g] = c_tp2.n
                    eng = v if gpar == 0 else s
                    eng.wait(c_tp2)
                    if eng is s:
                        d = s.activation(PT_sb[:, upar, 0:kw], ptp[gpar][:, 0:kw],
                                         AF.Copy)
                    else:
                        d = v.tensor_copy(PT_sb[:, upar, 0:kw], ptp[gpar][:, 0:kw])
                    c_ptev[gpar].inc(d)
                    gt += 1
                    t_.wait(c_ptev[(gt - 1) % 2], ptev0[(gt - 1) % 2] + (gt - 1) // 2 + 1)
                    if ug_g >= 2:
                        t_.wait(c_yv, yv0 + ug_g - 1)
                    for jt in range(ntp):
                        if w == 0:
                            wp, vtt = buf, jt
                        else:
                            wp, vtt = (1 - buf, jt) if jt < 2 else (buf, jt - 2)
                        d = t_.matmul(pav[sl][:, 0:D],
                                      PT_sb[:, upar, 128*jt:128*(jt+1)],
                                      vring[:, wp, vtt, 64*h:64*(h+1)],
                                      start=(jt == 0), stop=(jt == ntp - 1))
                    c_avc.inc(d)
                    v.wait(c_avc, avc0 + ug_g + 1)
                    if ug_g >= 2:
                        v.wait(c_tp2, ytpm[ug_g - 2])
                    d = v.tensor_scalar(ytmp[:, upar, :], pav[sl][:, 0:D],
                                        lrow[:, upar, 2:3], None, op0=ALU.mult)
                    c_yv.inc(d)
                    gpar = gt % 2
                    if gt >= 2:
                        t_.wait(c_ptev[gpar], ptev0[gpar] + gt // 2)
                    t_.wait(c_yv, yv0 + ug_g + 1)
                    d = t_.transpose(ptp[gpar][0:D, 0:128], ytmp[:, upar, :], idb[:])
                    c_tp2.inc(d)
                    ytpm[ug_g] = c_tp2.n
                    eng = v if gpar == 0 else s
                    eng.wait(c_tp2)
                    if w >= 2:
                        eng.wait(c_zo, womark.get(w - 1, zo0))
                    d_ = (s.activation(ycm_w[pb:pb+64, buf, hp, 128*tqt:128*(tqt+1)],
                                       ptp[gpar][0:D, 0:128], AF.Copy)
                          if eng is s else
                          v.tensor_copy(ycm_w[pb:pb+64, buf, hp, 128*tqt:128*(tqt+1)],
                                        ptp[gpar][0:D, 0:128]))
                    c_ptev[gpar].inc(d_)
                    gt += 1
                    ug_g += 1
            simmark[w] = c_sim2.n
            avmark[w] = c_avc.n
            yemark[w] = (c_ptev[0].n, c_ptev[1].n)
        _ = simmark


def _emit_c_ffn_sweep(cx: Ctx, layer):
    t_, v, s, g, q = cx.t, cx.v, cx.s, cx.g, cx.q
    ld, lg, sp = cx.ld, cx.lg, cx.sp
    import os
    zsrc2 = (cx.dram["xn_cm"] if (layer == 0 and os.environ.get("C_FFN_SRC_XN"))
             else cx.dram["z_cm"])
    NJA = 22
    with cx.scope():
        gpp = cx.sbuf("cf_gpp", [128, NCH], F32)
        bpp = cx.sbuf("cf_bpp", [128, NCH], F32)
        b2pp = cx.sbuf("cf_b2pp", [128, NCH], F32)
        b1app = cx.sbuf("cf_b1app", [128, NJA], F32)
        b1gpp = cx.sbuf("cf_b1gpp", [128, NJA], F32)
        pa = [cx.psum(f"cf_pa{i}", [128, 512]) for i in range(2)]
        pgt = [cx.psum(f"cf_pg{i}", [128, 512]) for i in range(2)]
        pst = [cx.psum(f"cf_pst{i}", [1, 512]) for i in range(2)]
        pb = [cx.psum(f"cf_pb{i}", [128, 512]) for i in range(2)]
        cv = cx.fine[4]
        c_pp = cx.fine[5]
        c_sq = cx.fine[6]
        c_st = cx.fine[7]
        c_var = cx.fine[8]
        c_rstd = cx.fine[9]
        c_mr = cx.fine[10]
        c_bc = cx.fine[11]
        c_ap2 = cx.fine[12]
        c_w1 = cx.fine[13]
        c_gel = cx.fine[14]
        c_hh = cx.fine[15]
        c_w2 = cx.fine[16]
        c_w2e = cx.fine[17]
        eps11 = cx.sb["eps11"]
        with cx.scope():
            rows = cx.sbuf("cf_rows", [1, 2 * C + 5460], F32)
            b2row = cx.sbuf("cf_b2row", [1, C], F32)
            ldr = cx.dmac[9]
            ldr.inc(q.dma_start(out=rows[0:1, 0:C],
                                in_=cx.dram["L0_ffn_ln_g"][layer:layer+1, :]))
            ldr.inc(q.dma_start(out=rows[0:1, C:2*C],
                                in_=cx.dram["L0_ffn_ln_b"][layer:layer+1, :]))
            ldr.inc(q.dma_start(out=rows[0:1, 2*C:2*C+5460],
                                in_=cx.dram["L0_ffn_b1"][layer:layer+1, :]))
            ldr.inc(q.dma_start(out=b2row[0:1, :],
                                in_=cx.dram["L0_ffn_b2"][layer:layer+1, :]))
            t_.wait(ldr)
            emit_pp_from_row(cx, rows[:, 0:C], C, gpp, pb[0], c_pp)
            emit_pp_from_row(cx, rows[:, C:2*C], C, bpp, pb[0], c_pp)
            emit_pp_from_row(cx, b2row[:, 0:C], C, b2pp, pb[0], c_pp)
            emit_pp_from_row(cx, rows[:, 2*C:2*C+2730], 2730, b1app, pb[0], c_pp)
            emit_pp_from_row(cx, rows[:, 2*C+2730:2*C+5460], 2730, b1gpp, pb[0],
                             c_pp)
        s.wait(c_pp)
        g.wait(c_pp)
        q.wait(c_pp)
        w1 = cx.sbuf("cf_w1", [128, NCH, 5460], BF16)
        w2 = cx.sbuf("cf_w2", [128, NJA, C], BF16)
        zf2 = cx.sbuf("cf_zf", [128, NCH, 513], F32)
        hcol = cx.sbuf("cf_hcol", [128, NCH], F32)
        zsq = cx.sbuf("cf_zsq", [128, 2, 512], F32)
        sm = cx.sbuf("cf_sm", [1, 2048], F32)
        xn2 = cx.sbuf("cf_xn2", [128, NCH, 512], BF16)
        gel = cx.sbuf("cf_gel", [128, 2, 512], F32)
        h_sb = cx.sbuf("cf_h", [128, NJA, 512], BF16)
        zo2 = cx.sbuf("cf_zo2", [128, 2, 512], F32)
        lgw = cx.dmac[14]
        ldz2 = cx.dmac[11]
        spf = cx.dmac[13]
        for ci in range(NCH):
            r = slice(128 * ci, 128 * (ci + 1))
            lgw.inc(g.dma_start(out=w1[:, ci, :],
                                in_=cx.dram["L0_ffn_W1"][layer, r, :]))
        for jc in range(NJA):
            cnt = 128 if jc < 21 else 42
            lgw.inc(g.dma_start(out=w2[0:cnt, jc, :],
                                in_=cx.dram["L0_ffn_W2"][layer,
                                                         128*jc:128*jc+cnt, :]))
        t_.wait(lgw)
        d = v.memset(zf2[:, :, 0:1], 0.0)
        cv.inc(d)
        q.wait(cv)
        sq0, st0, w10, gel0, hh0, w20, w2e0 = (c_sq.n, c_st.n, c_w1.n, c_gel.n,
                                               c_hh.n, c_w2.n, c_w2e.n)
        ap20 = c_ap2.n
        gw = 0
        go = 0
        for tc in range(NT):
            t0 = 512 * tc
            c_hc = cx.fine[18]
            if tc >= 1:
                d = v.tensor_copy(hcol[:, :], zf2[:, :, 512:513].rearrange(
                    "p a o -> p (a o)"))
                c_hc.inc(d)
                q.wait(c_hc)
                q.wait(c_w2e, w2e0 + NCH * tc)
                q.wait(c_ap2, ap20 + NCH * tc)
            for ci in range(NCH):
                ldz2.inc(q.dma_start(out=zf2[:, ci, 1:513],
                                     in_=zsrc2[128*ci:128*(ci+1), t0:t0 + 512]))
            if tc >= 1:
                v.wait(ldz2)
                d = v.tensor_copy(zf2[:, :, 0:1].rearrange("p a o -> p (a o)"),
                                  hcol[:, :])
                c_hc.inc(d)
            s.wait(ldz2)
            s.wait(c_hc)
            t_.wait(ldz2)
            t_.wait(c_hc)
            if tc >= 1:
                t_.wait(c_var, c_var.n)
            for ci in range(NCH):
                off = 1 if ci < 4 else 0
                t_.matmul(pst[0][:], cx.sb["ones_col"][:, 0:1],
                          zf2[:, ci, off:off+512], start=(ci == 0),
                          stop=(ci == NCH - 1))
            for ci in range(NCH):
                off = 1 if ci < 4 else 0
                if c_sq.n - sq0 >= 2:
                    s.wait(c_st, c_st.n - 1)
                d = s.activation(zsq[:, ci % 2, :], zf2[:, ci, off:off+512],
                                 AF.Square)
                c_sq.inc(d)
                t_.wait(c_sq)
                d = t_.matmul(pst[1][:], cx.sb["ones_col"][:, 0:1],
                              zsq[:, ci % 2, :], start=(ci == 0),
                              stop=(ci == NCH - 1))
                c_st.inc(d)
            v.wait(c_st)
            d = v.tensor_scalar(sm[:, 0:512], pst[0][:], 1.0 / C, None, op0=ALU.mult)
            cv.inc(d)
            d = v.tensor_scalar(sm[:, 512:1024], pst[1][:], 1.0 / C, None,
                                op0=ALU.mult)
            cv.inc(d)
            v.wait(cv)
            d = v.tensor_mul(sm[:, 1024:1536], sm[:, 0:512], sm[:, 0:512])
            cv.inc(d)
            v.wait(cv)
            d = v.tensor_sub(sm[:, 512:1024], sm[:, 512:1024], sm[:, 1024:1536])
            c_var.inc(d)
            s.wait(c_var)
            d = s.activation(sm[:, 1024:1536], sm[:, 512:1024], AF.Ln,
                             bias=eps11[0:1, 0:1])
            cx.fine[25].inc(d)
            s.wait(cx.fine[25])
            d = s.activation(sm[:, 1024:1536], sm[:, 1024:1536], AF.Exp, scale=-0.5)
            c_rstd.inc(d)
            v.wait(c_rstd)
            d = v.scalar_tensor_tensor(sm[:, 1536:2048], sm[:, 0:512], -1.0,
                                       sm[:, 1024:1536], op0=ALU.mult, op1=ALU.mult)
            c_mr.inc(d)
            t_.wait(c_mr)
            if tc >= 1:
                t_.wait(c_ap2, ap20 + NCH * tc)
            t_.matmul(pb[0][:], cx.sb["ones_row"][0:1, :], sm[:, 1024:1536],
                      start=True, stop=True)
            d = t_.matmul(pb[1][:], cx.sb["ones_row"][0:1, :], sm[:, 1536:2048],
                          start=True, stop=True)
            c_bc.inc(d)
            v.wait(c_bc)
            if tc >= 1:
                v.wait(c_w1, w10 + NJA * tc)
            for ci in range(NCH):
                off = 1 if ci < 4 else 0
                tvv = zsq[:, 0, :]
                d = v.tensor_mul(tvv, zf2[:, ci, off:off+512], pb[0][:])
                cv.inc(d)
                v.wait(cv)
                d = v.tensor_add(tvv, tvv, pb[1][:])
                cv.inc(d)
                v.wait(cv)
                d = v.tensor_scalar(xn2[:, ci, :], tvv, gpp[:, ci:ci+1],
                                    bpp[:, ci:ci+1], op0=ALU.mult, op1=ALU.add)
                if ci < NCH - 1:
                    cv.inc(d)
                    v.wait(cv)
            c_ap2.inc(d, NCH)
            t_.wait(c_ap2)
            for jc in range(NJA):
                cnt = 128 if jc < 21 else 42
                slw = gw % 2
                if gw >= 2:
                    t_.wait(c_gel, gel0 + gw - 1)
                    t_.wait(c_hh, hh0 + gw - 1)
                for ci in range(NCH):
                    t_.matmul(pa[slw][0:cnt, :], w1[:, ci, 128*jc:128*jc+cnt],
                              xn2[:, ci, :], start=(ci == 0), stop=(ci == NCH - 1))
                for ci in range(NCH):
                    d = t_.matmul(pgt[slw][0:cnt, :],
                                  w1[:, ci, 2730 + 128*jc:2730 + 128*jc + cnt],
                                  xn2[:, ci, :], start=(ci == 0),
                                  stop=(ci == NCH - 1))
                c_w1.inc(d)
                s.wait(c_w1)
                if gw >= 2:
                    s.wait(c_hh, hh0 + gw - 1)
                d = s.activation(gel[0:cnt, slw, :], pgt[slw][0:cnt, :], AF.Gelu,
                                 bias=b1gpp[0:cnt, jc:jc+1])
                c_gel.inc(d)
                v.wait(c_gel)
                if tc >= 1 and jc == 0:
                    v.wait(c_w2, w20 + NCH * tc)
                d = v.scalar_tensor_tensor(h_sb[0:cnt, jc, :], pa[slw][0:cnt, :],
                                           b1app[0:cnt, jc:jc+1], gel[0:cnt, slw, :],
                                           op0=ALU.add, op1=ALU.mult)
                c_hh.inc(d)
                gw += 1
            for ob in range(NCH):
                so = go % 2
                t_.wait(c_hh, hh0 + NJA * (tc + 1))
                if go >= 2:
                    t_.wait(c_w2e, w2e0 + go - 1)
                for jc in range(NJA):
                    cnt = 128 if jc < 21 else 42
                    d = t_.matmul(pgt[so][:],
                                  w2[0:cnt, jc, 128*ob:128*(ob+1)],
                                  h_sb[0:cnt, jc, :], start=(jc == 0),
                                  stop=(jc == NJA - 1))
                c_w2.inc(d)
                v.wait(c_w2)
                if go >= 2:
                    v.wait(spf)
                d = v.scalar_tensor_tensor(zo2[:, so, :], pgt[so][:],
                                           b2pp[:, ob:ob+1], zf2[:, ob, 1:513],
                                           op0=ALU.add, op1=ALU.add)
                c_w2e.inc(d)
                q.wait(c_w2e)
                spf.inc(q.dma_start(out=cx.dram["z_cm"][128*ob:128*(ob+1),
                                                        t0:t0+512],
                                    in_=zo2[:, so, :]))
                go += 1



def _emit_l1_proj(cx: Ctx, idx, tabs):
    t_, v, s, g, q = cx.t, cx.v, cx.s, cx.g, cx.q
    GM = 256
    GAM = (0.5, 1.0, 2.0)
    cT, cV, cS, cG = cx.fine[20], cx.fine[21], cx.fine[22], cx.fine[23]
    with cx.scope():
        ckw = cx.sbuf("f_ckw", [128, 6, C], BF16)
        rfw = cx.sbuf("f_rfw", [128, NCH, 3 * GM], BF16)
        pjw = cx.sbuf("f_pjw", [128, 6, C], BF16)
        brow = cx.sbuf("f_brow", [1, C + 3 * GM + C], F32)
        cbpp = cx.sbuf("f_cbpp", [128, NCH], F32)
        rbpp = cx.sbuf("f_rbpp", [128, 6], F32)
        rb2pp = cx.sbuf("f_rb2pp", [128, 6], F32)
        pbrow_bf = cx.sbuf("f_pbbf", [1, C], BF16)
        xin = cx.sbuf("f_xin", [128, 2, NCH, 514], BF16)
        hsb = cx.sbuf("f_hsb", [128, NCH, 512], BF16)
        zsb = cx.sbuf("f_zsb", [128, 6, 512], BF16)
        wk2 = cx.sbuf("f_wk2", [128, 4, 512], F32)
        otm = cx.sbuf("f_otm", [128, 2, 1024], F32)
        obf = cx.sbuf("f_obf", [128, 2, 1024], BF16)
        n2t = cx.sbuf("f_n2t", [128, 16], F32)
        n2bf = cx.sbuf("f_n2bf", [128, 16], BF16)
        ocm = cx.sbuf("f_ocm", [128, 2, 1024], BF16)
        n2cm = cx.sbuf("f_n2cm", [16, 2, 128], F32)
        ptmp = cx.sbuf("f_ptmp", [128, 3, 512], F32)
        pcm = [cx.psum(f"f_pcm{i}", [128, 512]) for i in range(2)]
        ptm = [cx.psum(f"f_ptm{i}", [128, 512]) for i in range(2)]
        ptpn = cx.psum("f_ptpn", [128, 1024], BF16)
        pnr = cx.psum("f_pnr", [128, 512])
        lgw = cx.dmac[14]
        ldr = cx.dmac[9]
        ldx = [cx.dmac[15], cx.dmac[16]]
        spq = cx.dmac[13]
        c_pp = cx.fine[5]
        for cc2 in range(2):
            for w in range(3):
                lgw.inc(g.dma_start(
                    out=ckw[:, cc2 * 3 + w, :],
                    in_=cx.dram["L1_attn_conv_k"][idx, w, 128*cc2:128*(cc2+1), :]))
        for ci in range(NCH):
            r = slice(128 * ci, 128 * (ci + 1))
            for gi2 in range(3):
                lgw.inc(g.dma_start(
                    out=rfw[:, ci, GM*gi2:GM*(gi2+1)],
                    in_=cx.dram["L1_attn_rff_w"][idx, gi2, r, :]))
        for fi in range(6):
            lgw.inc(g.dma_start(
                out=pjw[:, fi, :],
                in_=cx.dram["L1_attn_proj_w"][idx, 128*fi:128*(fi+1), :]))
        ldr.inc(q.dma_start(out=brow[0:1, 0:C],
                            in_=cx.dram["L1_attn_conv_b"][idx:idx+1, :]))
        ldr.inc(q.dma_start(
            out=brow[0:1, C:C+3*GM],
            in_=cx.dram["L1_attn_rff_b"][idx, :, :].rearrange(
                "g m -> (g m)").unsqueeze(0)))
        ldr.inc(q.dma_start(out=brow[0:1, C+3*GM:C+3*GM+C],
                            in_=cx.dram["L1_attn_proj_b"][idx:idx+1, :]))
        t_.wait(ldr)
        t_.wait(lgw)
        emit_pp_from_row(cx, brow[:, 0:C], C, cbpp, pnr, c_pp)
        emit_pp_from_row(cx, brow[:, C:C+3*GM], 3 * GM, rbpp, pnr, c_pp)
        v.wait(c_pp)
        cV.inc(v.tensor_copy(pbrow_bf[:], brow[:, C+3*GM:C+3*GM+C]))
        v.wait(cV)
        cV.inc(v.tensor_scalar(rb2pp[:, :], rbpp[:, :], float(1.0 / (2 * np.pi)),
                               0.25, op0=ALU.mult, op1=ALU.add))
        g.wait(c_pp)
        g.wait(lgw)
        cG.inc(g.tensor_scalar(pjw.ap().rearrange("p a b -> p (a b)"),
                               pjw.ap().rearrange("p a b -> p (a b)"),
                               float(np.sqrt(2.0 / GM)), None, op0=ALU.mult))
        t_.wait(cV)
        t_.wait(cG)
        s.wait(cV)

        SP = 512
        for sp_i in range(T // SP):
            buf = sp_i % 2
            t0 = SP * sp_i
            g.wait(cT)
            lo = max(t0 - 1, 0)
            hi = min(t0 + SP + 1, T)
            woff = 1 - (t0 - lo)
            for ci in range(NCH):
                ldx[buf].inc(g.dma_start(
                    out=xin[:, buf, ci, woff:woff + (hi - lo)],
                    in_=cx.dram["xn1_cm"][128*ci:128*(ci+1), lo:hi]))
            v.wait(ldx[buf])
            if sp_i == 0:
                cV.inc(v.memset(xin[:, 0, :, 0:1], 0.0))
            if sp_i == T // SP - 1:
                cV.inc(v.memset(xin[:, buf, :, 513:514], 0.0))
            t_.wait_ge(ldx[buf].sem, ldx[buf].n)
            t_.wait(cV)
            t_.wait(cS)
            for ob in range(NCH):
                slot = ob % 2
                grp = ob // 2
                first = True
                for cc2 in range(2):
                    for w in range(3):
                        d = t_.matmul(pcm[slot][:],
                                      ckw[:, cc2 * 3 + w, 128*ob:128*(ob+1)],
                                      xin[:, buf, grp * 2 + cc2, w:w + 512],
                                      start=first, stop=(cc2 == 1 and w == 2))
                        first = False
                cT.inc(d)
                s.wait(cT)
                d = s.activation(hsb[:, ob, :], pcm[slot][:], AF.Identity,
                                 bias=cbpp[:, ob:ob+1])
                cS.inc(d)
                if ob >= 1:
                    t_.wait(cS, cS.n - 1)
            t_.wait(cS)
            for mcol in range(6):
                gi2, mc = mcol // 2, mcol % 2
                sgam = float(np.sqrt(2.0 * GAM[gi2]))
                slot = mcol % 2
                if mcol >= 2:
                    t_.wait(cS, cS.n)
                for ci in range(NCH):
                    d = t_.matmul(pcm[slot][:],
                                  rfw[:, ci, GM*gi2 + 128*mc:GM*gi2 + 128*(mc+1)],
                                  hsb[:, ci, :], start=(ci == 0),
                                  stop=(ci == NCH - 1))
                cT.inc(d)
                v.wait(cT)
                v.wait(cS, cS.n)
                cV.inc(v.tensor_scalar(wk2[:, 0, :], pcm[slot][:],
                                       float(sgam / (2 * np.pi)),
                                       rb2pp[:, mcol:mcol+1],
                                       op0=ALU.mult, op1=ALU.add))
                v.wait(cV)
                cV.inc(v.tensor_scalar(wk2[:, 0, :], wk2[:, 0, :], MAGIC, None,
                                       op0=ALU.add))
                v.wait(cV)
                cV.inc(v.tensor_scalar(wk2[:, 1, :], wk2[:, 0, :], MAGIC, -TPI_HI,
                                       op0=ALU.subtract, op1=ALU.mult))
                cV.inc(v.tensor_scalar(wk2[:, 2, :], wk2[:, 0, :], MAGIC, -TPI_LO,
                                       op0=ALU.subtract, op1=ALU.mult))
                v.wait(cV)
                cV.inc(v.tensor_scalar(wk2[:, 3, :], pcm[slot][:], sgam,
                                       rbpp[:, mcol:mcol+1],
                                       op0=ALU.mult, op1=ALU.add))
                v.wait(cV)
                cV.inc(v.tensor_add(wk2[:, 3, :], wk2[:, 3, :], wk2[:, 1, :]))
                v.wait(cV)
                cV.inc(v.scalar_tensor_tensor(wk2[:, 3, :], wk2[:, 3, :],
                                              float(np.pi / 2), wk2[:, 2, :],
                                              op0=ALU.add, op1=ALU.add))
                v.wait(cV)
                cV.inc(v.tensor_scalar(wk2[:, 3, :], wk2[:, 3, :],
                                       -float(np.pi), float(np.pi),
                                       op0=ALU.max, op1=ALU.min))
                s.wait(cV)
                d = s.activation(zsb[:, mcol, :], wk2[:, 3, :], AF.Sin)
                cS.inc(d)
            for tt in range(4):
                tb2 = tt % 2
                tile = sp_i * 4 + tt
                for oc in range(2):
                    slot = oc
                    t_.wait(cS)
                    t_.wait(cV, cV.n)
                    t_.matmul(ptm[slot][:], cx.sb["ones_row_bf"][0:1, :],
                              pbrow_bf[0:1, 512*oc:512*(oc+1)],
                              start=True, stop=False)
                    for fi in range(6):
                        d = t_.matmul(ptm[slot][:],
                                      zsb[:, fi, 128*tt:128*(tt+1)],
                                      pjw[:, fi, 512*oc:512*(oc+1)],
                                      start=False, stop=(fi == 5))
                    cT.inc(d)
                    v.wait(cT)
                    cV.inc(v.tensor_copy(otm[:, tb2, 512*oc:512*(oc+1)],
                                         ptm[slot][:]))
                if idx == 2:
                    v.wait(cV)
                    if tile >= 2:
                        v.wait(spq)
                    cV.inc(v.tensor_copy(obf[:, tb2, :], otm[:, tb2, :]))
                    q.wait(cV)
                    for h in range(16):
                        spq.inc(q.dma_start(
                            out=cx.dram["v1_tm"][h, 128*tile:128*(tile+1), :],
                            in_=obf[:, tb2, 64*h:64*(h+1)]))
                else:
                    v.wait(cV)
                    if tile >= 2:
                        v.wait(spq)
                    for hg in range(2):
                        d = _emit_rope_apply(
                            cx, v, cV,
                            otm.ap()[:, tb2, 512*hg:512*(hg+1)],
                            obf.ap()[:, tb2, 512*hg:512*(hg+1)],
                            None, None,
                            tabs[0][:, 64*tile:64*(tile+1)],
                            tabs[1][:, 64*tile:64*(tile+1)], ptmp)
                        cV.inc(d)
                    s.wait(cV)
                    cS.inc(s.activation(ptmp[:, 0, :], obf[:, tb2, 0:512],
                                        AF.Square))
                    cS.inc(s.activation(ptmp[:, 1, :], obf[:, tb2, 512:1024],
                                        AF.Square))
                    v.wait(cS)
                    cV.inc(v.reduce_sum(
                        n2t[:, 0:16],
                        ptmp.ap()[:, 0:2, :].rearrange(
                            "p a (h j) -> p (a h) j", h=8, j=64),
                        axis=AX.X, negate=(idx == 0)))
                    v.wait(cV)
                    if idx == 0:
                        cV.inc(v.tensor_scalar(n2bf[:, 0:16], n2t[:, 0:16], 0.5,
                                               None, op0=ALU.mult))
                    else:
                        cV.inc(v.tensor_copy(n2bf[:, 0:16], n2t[:, 0:16]))
                    t_.wait(cV)
                    t_.wait(cS, cS.n)
                    d = t_.transpose(ptpn[0:16, 0:128], n2bf[:, 0:16],
                                     cx.sb["f_idb"][:])
                    cT.inc(d)
                    s.wait(cT)
                    cS.inc(s.activation(n2cm[0:16, tb2, :], ptpn[0:16, 0:128],
                                        AF.Copy))
                    t_.wait(cS, cS.n)
                    for hh in range(16):
                        d = t_.transpose(
                            ptpn[(hh % 2)*64:(hh % 2)*64+64,
                                 128*(hh//2):128*(hh//2)+128],
                            obf[:, tb2, 64*hh:64*(hh+1)], cx.sb["f_idb"][:])
                    cT.inc(d)
                    s.wait(cT)
                    cS.inc(s.activation(ocm[:, tb2, :], ptpn[:, 0:1024], AF.Copy))
                    q.wait(cS)
                    dstq = cx.dram["q1_cm"] if idx == 0 else cx.dram["k1r_cm"]
                    dn2 = cx.dram["qn2_d"] if idx == 0 else cx.dram["kr2_d"]
                    spq.inc(q.dma_start(
                        out=dstq.rearrange("(hp r) t -> r hp t", hp=8, r=128)[
                            :, :, 128*tile:128*(tile+1)],
                        in_=ocm[:, tb2, :].rearrange("p (hp b) -> p hp b",
                                                     hp=8, b=128)))
                    spq.inc(q.dma_start(out=dn2[:, 128*tile:128*(tile+1)],
                                        in_=n2cm[0:16, tb2, :]))
    cx.barrier()


def emit_l1_attn(cx: Ctx):
    t_, v, s, g, q = cx.t, cx.v, cx.s, cx.g, cx.q
    H, D = 16, 64
    cT, cV, cS, cG = cx.fine[20], cx.fine[21], cx.fine[22], cx.fine[23]
    with cx.scope():
        cx.sb["f_idb"] = cx.sbuf("f_idbt", [128, 128], BF16)
        cosT = cx.sbuf("f_cosT", [128, 2048], F32)
        sinT = cx.sbuf("f_sinT", [128, 2048], F32)
        cosI2 = cx.sbuf("f_cosI2", [128, 2048], F32)
        sinI2 = cx.sbuf("f_sinI2", [128, 2048], F32)
        with cx.scope():
            iot = cx.sbuf("f_iot", [128, 128], F32)
            vcol = cx.sbuf("f_vcol", [128, 128], F32)
            hpi = cx.sbuf("f_hpi", [128, 1], F32)
            d = g.iota(iot[:], pattern=[[1, 128]], base=0, channel_multiplier=0,
                       allow_small_or_imprecise_dtypes=True)
            cG.inc(d)
            d = g.iota(vcol[:], pattern=[[0, 128]], base=0, channel_multiplier=1,
                       allow_small_or_imprecise_dtypes=True)
            cG.inc(d)
            v.wait(cG)
            d = v.tensor_tensor(cx.sb["f_idb"][:], iot[:], vcol[:],
                                op=ALU.is_equal)
            cV.inc(d)
            d = v.memset(hpi[:], float(np.pi / 2))
            cV.inc(d)
            t_.wait(cV)
            s.wait(cV)
            g.wait(cV)
            tbsc = {
                "irow": cx.sbuf("f_irow", [1, 384], F32),
                "r32": cx.sbuf("f_r32", [1, 96], F32),
                "wk": cx.sbuf("f_wkt", [128, 4, 512], F32),
                "halfpi": hpi,
            }
            pq0 = cx.psum("f_pq0", [128, 512])
            pq1 = cx.psum("f_pq1", [128, 512])
            _emit_rope_tables(cx, (cosT, sinT, cosI2, sinI2), tbsc, pq0, pq1,
                              xpos=False)
        cx.barrier()
        _emit_l1_proj(cx, 0, (cosT, sinT))
        _emit_l1_proj(cx, 1, (cosT, sinT))
        _emit_l1_proj(cx, 2, (cosT, sinT))
        with cx.scope():
            lamt = cx.sbuf("f_lamt", [64, 64], BF16)
            lamn = cx.sbuf("f_lamn", [64, 1], F32)
            lsq = cx.sbuf("f_lsq", [64, 64], F32)
            kin = cx.sbuf("f_kin", [64, 2, 16, 512], BF16)
            krows = cx.sbuf("f_krows", [1, 2, 16, 512], BF16)
            knyo = cx.sbuf("f_knyo", [64, 2, 16, 512], BF16)
            ksq = cx.sbuf("f_ksq", [64, 512], F32)
            k2o = cx.sbuf("f_k2o", [1, 2, 16, 512], F32)
            pny = [cx.psum(f"f_pny{i}", [128, 512]) for i in range(2)]
            pn1 = cx.psum("f_pn1", [1, 512])
            ldk = [cx.dmac[11], cx.dmac[12]]
            lgk = [cx.dmac[15], cx.dmac[16]]
            spk = cx.dmac[13]
            ldr = cx.dmac[9]
            lgw2 = cx.dmac[14]
            ldr.inc(q.dma_start(out=lsq[:, :],
                                in_=cx.dram["L1_attn_landmarks"][:, :]))
            s.wait(ldr)
            cS.inc(s.activation(lsq[:, :], lsq[:, :], AF.Square,
                                accum_out=lamn[:, 0:1]))
            s.wait(cS)
            cS.inc(s.activation(lamn[:, 0:1], lamn[:, 0:1], AF.Identity,
                                scale=float(-1.0 / 64.0)))
            lgw2.inc(g.dma_start(out=lamt[:, :], in_=cx.dram["L1_attn_lamT"][:, :]))
            g.wait(lgw2)
            cG.inc(g.tensor_scalar(lamt[:, :], lamt[:, :], -2.0, None,
                                   op0=ALU.mult))
            t_.wait(cG)
            t_.wait(cS)
            for sp_i in range(T // 512):
                buf = sp_i % 2
                t0 = 512 * sp_i
                q.wait(cS)
                for h in range(H):
                    ldk[buf].inc(q.dma_start(
                        out=kin[0:64, buf, h, :],
                        in_=cx.dram["k1r_cm"][64*h:64*(h+1), t0:t0+512]))
                g.wait(cT)
                for h in range(H):
                    lgk[buf].inc(g.dma_start(
                        out=krows[0:1, buf, h, :],
                        in_=cx.dram["kr2_d"][h:h+1, t0:t0+512]))
                t_.wait_ge(ldk[buf].sem, ldk[buf].n)
                t_.wait_ge(lgk[buf].sem, lgk[buf].n)
                if sp_i >= 2:
                    s.wait(spk)
                    v.wait(spk)
                for h in range(H):
                    slot = h % 2
                    if h >= 2 or sp_i >= 1:
                        t_.wait(cS, cS.n)
                    t_.matmul(pny[slot][0:64, :], lamt[:, :],
                              kin[0:64, buf, h, :], start=True, stop=False)
                    d = t_.matmul(pny[slot][0:64, :],
                                  cx.sb["ones_row_bf"][0:1, 0:64],
                                  krows[0:1, buf, h, :], start=False, stop=True)
                    cT.inc(d)
                    s.wait(cT)
                    d = s.activation(knyo[0:64, buf, h, :],
                                     pny[slot][0:64, :], AF.Exp,
                                     scale=float(-1.0 / 64.0), bias=lamn[:, 0:1])
                    cS.inc(d)
                for h in range(H):
                    s.wait(cS)
                    if h >= 1:
                        s.wait(cT, cT.n)
                    cS.inc(s.activation(ksq[0:64, :], knyo[0:64, buf, h, :],
                                        AF.Square))
                    t_.wait(cS)
                    if h >= 1 or sp_i >= 1:
                        t_.wait(cV, cV.n)
                    d = t_.matmul(pn1[0:1, :], cx.sb["ones_col"][0:64, 0:1],
                                  ksq[0:64, :], start=True, stop=True)
                    cT.inc(d)
                    v.wait(cT)
                    cV.inc(v.tensor_scalar(k2o[0:1, buf, h, :], pn1[0:1, :],
                                           -0.5, None, op0=ALU.mult))
                q.wait(cS)
                q.wait(cV)
                for h in range(H):
                    spk.inc(q.dma_start(
                        out=cx.dram["kny_cm"][64*h:64*(h+1), t0:t0+512],
                        in_=knyo[0:64, buf, h, :]))
                spk.inc(q.dma_start(
                    out=cx.dram["kn2_d"][:, t0:t0+512].unsqueeze(0),
                    in_=k2o[:, buf, :, :]))
        cx.barrier()
        with cx.scope():
            omg = cx.sbuf("f_omg", [65, 256], BF16)
            mln16 = cx.sbuf("f_mln16", [128, 1], F32)
            ktil = cx.sbuf("f_ktil", [65, 2, T], BF16)
            qtil = cx.sbuf("f_qtil", [65, 2, T], BF16)
            vhd = cx.sbuf("f_vhd", [128, 2, 32, D], BF16)
            kp_tm = cx.sbuf("f_kptm", [128, 32, 256], BF16)
            qp_cm = cx.sbuf("f_qpcm", [128, 2, T], BF16)
            ksrow = cx.sbuf("f_ksrow", [1, 256], F32)
            ksumc = cx.sbuf("f_ksumc", [128, 2], BF16)
            kvsb = cx.sbuf("f_kvsb", [128, 2, D], BF16)
            dreq = cx.sbuf("f_dreq", [1, T], F32)
            rinvc = cx.sbuf("f_rinvc", [128, 32], F32)
            numt = cx.sbuf("f_numt", [128, 2, D], BF16)
            ncm = cx.sbuf("f_ncm", [64, 2, 128], BF16)
            pk5 = [cx.psum(f"f_pk5{i}", [128, 512]) for i in range(2)]
            pd5 = cx.psum("f_pd5", [128, 512])
            pt5 = cx.psum("f_pt5", [128, 1024], BF16)
            ldh = [cx.dmac[11], cx.dmac[12]]
            lgh = [cx.dmac[15], cx.dmac[16]]
            spn = cx.dmac[13]
            lgo = cx.dmac[14]
            lgo.inc(g.dma_start(out=omg[0:64, :],
                                in_=cx.dram["L1_attn_omega"][:, :]))
            v.wait(lgo)
            cV.inc(v.memset(omg[64:65, :], 1.0))
            cV.inc(v.memset(mln16[:, :], -float(np.log(16.0))))
            t_.wait(cV)
            s.wait(cV)
            for h in range(H):
                hb = h % 2
                t_.wait(cS, cS.n)
                t_.wait(cV, cV.n)
                if h >= 2:
                    q.wait(cT)
                    g.wait(cT)
                ldh[hb].inc(q.dma_start(out=ktil[0:64, hb, :],
                                        in_=cx.dram["kny_cm"][64*h:64*(h+1), :]))
                ldh[hb].inc(q.dma_start(out=qtil[0:64, hb, :],
                                        in_=cx.dram["q1_cm"][64*h:64*(h+1), :]))
                ldh[hb].inc(q.dma_start(
                    out=vhd[:, hb, :, :],
                    in_=cx.dram["v1_tm"][h, :, :].rearrange("(a p) d -> p a d",
                                                            p=128)))
                lgh[hb].inc(g.dma_start(out=ktil[64:65, hb, :],
                                        in_=cx.dram["kn2_d"][h:h+1, :]))
                lgh[hb].inc(g.dma_start(out=qtil[64:65, hb, :],
                                        in_=cx.dram["qn2_d"][h:h+1, :]))
                t_.wait_ge(ldh[hb].sem, ldh[hb].n)
                t_.wait_ge(lgh[hb].sem, lgh[hb].n)
                for tc in range(32):
                    slot = tc % 2
                    if tc >= 2:
                        t_.wait(cS, cS.n)
                    d = t_.matmul(pk5[slot][:, 0:256],
                                  ktil[:, hb, 128*tc:128*(tc+1)],
                                  omg[:, :], start=True, stop=True)
                    cT.inc(d)
                    s.wait(cT)
                    cS.inc(s.activation(kp_tm[:, tc, :], pk5[slot][:, 0:256],
                                        AF.Exp, bias=mln16[:, 0:1]))
                t_.wait(cS)
                for tc in range(32):
                    d = t_.matmul(pd5[0:1, 0:256], cx.sb["ones_col_bf"][:, 0:1],
                                  kp_tm[:, tc, :], start=(tc == 0),
                                  stop=(tc == 31))
                cT.inc(d)
                v.wait(cT)
                cV.inc(v.tensor_copy(ksrow[0:1, :], pd5[0:1, 0:256]))
                t_.wait(cV)
                for mc in range(2):
                    d = t_.matmul(pd5[0:128, 300+mc:301+mc],
                                  ksrow[0:1, 128*mc:128*(mc+1)],
                                  cx.sb["ones_row"][0:1, 0:1],
                                  start=True, stop=True)
                cT.inc(d)
                v.wait(cT)
                cV.inc(v.tensor_copy(ksumc[:, 0:2], pd5[0:128, 300:302]))
                for mc in range(2):
                    for tc in range(32):
                        d = t_.matmul(pk5[mc][:, 0:D],
                                      kp_tm[:, tc, 128*mc:128*(mc+1)],
                                      vhd[:, hb, tc, :], start=(tc == 0),
                                      stop=(tc == 31))
                    cT.inc(d)
                    v.wait(cT)
                    cV.inc(v.tensor_copy(kvsb[:, mc, :], pk5[mc][:, 0:D]))
                t_.wait(cV)
                for mc in range(2):
                    for tc2 in range(8):
                        slot = tc2 % 2
                        t_.wait(cS, cS.n)
                        d = t_.matmul(pk5[slot][:],
                                      omg[:, 128*mc:128*(mc+1)],
                                      qtil[:, hb, 512*tc2:512*(tc2+1)],
                                      start=True, stop=True)
                        cT.inc(d)
                        s.wait(cT)
                        cS.inc(s.activation(qp_cm[:, mc, 512*tc2:512*(tc2+1)],
                                            pk5[slot][:], AF.Exp,
                                            bias=mln16[:, 0:1]))
                t_.wait(cS)
                for tc2 in range(8):
                    if tc2 >= 1:
                        t_.wait(cV, cV.n)
                    for mc in range(2):
                        d = t_.matmul(pd5[0:1, 0:512], ksumc[:, mc:mc+1],
                                      qp_cm[:, mc, 512*tc2:512*(tc2+1)],
                                      start=(mc == 0), stop=(mc == 1))
                    cT.inc(d)
                    v.wait(cT)
                    cV.inc(v.tensor_copy(dreq[0:1, 512*tc2:512*(tc2+1)],
                                         pd5[0:1, 0:512]))
                v.wait(cV)
                cV.inc(v.tensor_scalar(dreq[0:1, :], dreq[0:1, :], 1e-6, None,
                                       op0=ALU.add))
                v.wait(cV)
                cV.inc(v.reciprocal(dreq[0:1, :], dreq[0:1, :]))
                t_.wait(cV)
                for tc in range(32):
                    d = t_.matmul(pd5[0:128, tc:tc+1],
                                  dreq[0:1, 128*tc:128*(tc+1)],
                                  cx.sb["ones_row"][0:1, 0:1],
                                  start=True, stop=True)
                cT.inc(d)
                v.wait(cT)
                cV.inc(v.tensor_copy(rinvc[:, 0:32], pd5[0:128, 0:32]))
                t_.wait(cV)
                t_.wait(cS, cS.n)
                for tc in range(32):
                    tpar = tc % 2
                    if tc >= 2:
                        t_.wait(cV, cV.n)
                    for mc in range(2):
                        d = t_.matmul(pk5[tpar][:, 0:D],
                                      qp_cm[:, mc, 128*tc:128*(tc+1)],
                                      kvsb[:, mc, :], start=(mc == 0),
                                      stop=(mc == 1))
                    cT.inc(d)
                    v.wait(cT)
                    cV.inc(v.tensor_scalar(numt[:, tpar, :], pk5[tpar][:, 0:D],
                                           rinvc[:, tc:tc+1], None, op0=ALU.mult))
                    t_.wait(cV)
                    if tc >= 2 or h >= 1:
                        t_.wait(cS, cS.n)
                    d = t_.transpose(pt5[0:D, 128*tpar:128*(tpar+1)],
                                     numt[:, tpar, :], cx.sb["f_idb"][:])
                    cT.inc(d)
                    s.wait(cT)
                    if tc >= 2 or h >= 1:
                        s.wait(spn)
                    cS.inc(s.activation(ncm[0:D, tpar, :],
                                        pt5[0:D, 128*tpar:128*(tpar+1)], AF.Copy))
                    q.wait(cS)
                    if tc >= 2 or h >= 1:
                        q.wait(spn)
                    spn.inc(q.dma_start(
                        out=cx.dram["num_cm"][64*h:64*(h+1), 128*tc:128*(tc+1)],
                        in_=ncm[0:D, tpar, :]))
        cx.barrier()
        with cx.scope():
            wps = cx.sbuf("f_wow", [128, NCH, C], BF16)
            bprow = cx.sbuf("f_bprow", [1, C], F32)
            bppp = cx.sbuf("f_bppp", [128, NCH], F32)
            nin = cx.sbuf("f_nin", [128, 2, NCH, 512], BF16)
            asb = cx.sbuf("f_asb", [128, 2, 512], F32)
            pcol6 = cx.psum("f_pcol6", [128, 8])
            pa6 = [cx.psum(f"f_pa6{i}", [128, 512]) for i in range(2)]
            c_pp = cx.fine[5]
            lgw6 = cx.dmac[14]
            ldr6 = cx.dmac[9]
            ldn = [cx.dmac[11], cx.dmac[12]]
            spp = [cx.dmac[13], cx.dmac[8]]
            for ci in range(NCH):
                lgw6.inc(g.dma_start(
                    out=wps[:, ci, :],
                    in_=cx.dram["L1_attn_out_w"][128*ci:128*(ci+1), :]))
            ldr6.inc(q.dma_start(out=bprow[0:1, :],
                                 in_=cx.dram["L1_attn_out_b"][0:1, :]))
            t_.wait(ldr6)
            t_.wait(lgw6)
            emit_pp_from_row(cx, bprow[:], C, bppp, pcol6, c_pp)
            s.wait(c_pp)
            for it in range(NT):
                t0 = 512 * it
                buf = it % 2
                q.wait(cT)
                for ci in range(NCH):
                    ldn[buf].inc(q.dma_start(
                        out=nin[:, buf, ci, :],
                        in_=cx.dram["num_cm"][128*ci:128*(ci+1), t0:t0+512]))
                t_.wait_ge(ldn[buf].sem, ldn[buf].n)
                for ob in range(NCH):
                    slot = ob % 2
                    if it > 0 or ob >= 2:
                        t_.wait(cS, cS.n)
                    for ci in range(NCH):
                        d = t_.matmul(pa6[slot][:], wps[:, ci, 128*ob:128*(ob+1)],
                                      nin[:, buf, ci, :], start=(ci == 0),
                                      stop=(ci == NCH - 1))
                    cT.inc(d)
                    s.wait(cT)
                    if it > 0 or ob >= 2:
                        s.wait(spp[slot])
                    d = s.activation(asb[:, slot, :], pa6[slot][:], AF.Identity,
                                     bias=bppp[:, ob:ob+1])
                    cS.inc(d)
                    q.wait(cS)
                    spp[slot].inc(q.dma_start(
                        out=cx.dram["P_cm"][128*ob:128*(ob+1), t0:t0+512],
                        in_=asb[:, slot, :]))
    cx.barrier()


_DRAM_TABLE = [
    ("x_cm", [C, T], "I"),
    ("L0_attn_Wq", [C, C], "I"), ("L0_attn_bq", [1, C], "I"),
    ("L0_attn_Wk", [C, C], "I"), ("L0_attn_bk", [1, C], "I"),
    ("L0_attn_Wv", [C, C], "I"), ("L0_attn_bv", [1, C], "I"),
    ("L0_attn_Wp", [C, C], "I"), ("L0_attn_bp", [1, C], "I"),
    ("L0_ffn_Wqkv", [6, C, 3072], "I"),
    ("L0_ffn_q_scale", [6, 64], "I"), ("L0_ffn_k_scale", [6, 64], "I"),
    ("L0_ffn_Wo", [6, C, C], "I"),
    ("L0_ffn_ln_g", [6, C], "I"), ("L0_ffn_ln_b", [6, C], "I"),
    ("L0_ffn_W1", [6, C, 5460], "I"), ("L0_ffn_b1", [6, 5460], "I"),
    ("L0_ffn_W2", [6, 2730, C], "I"), ("L0_ffn_b2", [6, C], "I"),
    ("L1_attn_conv_k", [3, 3, 256, C], "I"), ("L1_attn_conv_b", [3, C], "I"),
    ("L1_attn_rff_w", [3, 3, C, 256], "I"), ("L1_attn_rff_b", [3, 3, 256], "I"),
    ("L1_attn_proj_w", [3, 768, C], "I"), ("L1_attn_proj_b", [3, C], "I"),
    ("L1_attn_omega", [64, 256], "I"), ("L1_attn_landmarks", [64, 64], "I"),
    ("L1_attn_lamT", [64, 64], "I"),
    ("L1_attn_out_w", [C, C], "I"), ("L1_attn_out_b", [1, C], "I"),
    ("L1_ffn_W_rT", [C, C], "I"), ("L1_ffn_W_kT", [C, C], "I"),
    ("L1_ffn_W_vT", [C, C], "I"),
    ("L1_ffn_mu_r", [1, C], "I"), ("L1_ffn_mu_k", [1, C], "I"),
    ("xn_cm", [C, T], "S"), ("A_cm", [C, T], "S"), ("z_cm", [C, T], "S"),
    ("x1_cm", [C, T], "S"), ("xn1_cm", [C, T], "S"), ("P_cm", [C, T], "S"),
    ("q0_cm", [C, T], "Sb"), ("k0_cm", [C, T], "Sb"), ("v0_tm", [16, T, 64], "Sb"),
    ("q1_cm", [C, T], "Sb"), ("v1_tm", [16, T, 64], "Sb"),
    ("k1r_cm", [C, T], "Sb"), ("kny_cm", [C, T], "Sb"),
    ("qn2_d", [16, T], "S"), ("kr2_d", [16, T], "S"),
    ("kn2_d", [16, T], "S"),
    ("num_cm", [C, T], "Sb"),
    ("out_cm", [C, T], "O"),
]
_DBG_TABLE = [("ycm_dbg", [NCH, 128, T], "Sb"),
              ("zlay_dbg", [6, C, T], "S")]

PHASES_ALL = ("A", "B", "C", "D", "E", "F", "G")


def build_program(phases=PHASES_ALL, debug_inputs=(), debug_outputs=()):
    nc = bass.Bass("TRN2", target_bir_lowering=False)
    es = ExitStack()
    cx = Ctx(nc, es)
    table = list(_DRAM_TABLE) + [x for x in _DBG_TABLE if x[0] in debug_outputs]
    for name, shape, kind in table:
        dt = BF16 if kind == "Sb" else F32
        k = {"I": "ExternalInput", "S": "Internal", "Sb": "Internal",
             "O": "ExternalOutput"}[kind]
        if name in debug_inputs:
            k = "ExternalInput"
        if name in debug_outputs:
            k = "ExternalOutput"
        cx.dram[name] = nc.dram_tensor(name, shape, dt, kind=k).ap()

    phase_consts(cx)
    if "A" in phases:
        with cx.scope():
            sc = alloc_ln_scratch(cx)
            emit_ln_cm(cx, "x_cm", "xn_cm", sc)
        cx.barrier()
    if "B" in phases:
        emit_l0_attn(cx)
    if "C" in phases:
        emit_l0_ffn(cx)
    if "D" in phases:
        emit_add3(cx, "x_cm", "A_cm", "z_cm", "x1_cm")
    if "E" in phases:
        with cx.scope():
            sc = alloc_ln_scratch(cx)
            emit_ln_cm(cx, "x1_cm", "xn1_cm", sc)
        cx.barrier()
    if "F" in phases:
        emit_l1_attn(cx)
    if "G" in phases:
        emit_l1ffn_combine(cx)

    for c in cx.all_dma_ctrs():
        if c.n:
            cx.q.wait(c)
    cx.g.wait(cx.lg)

    block = es.enter_context(nc.Block())
    block.tensor(_replay(cx.streams["tensor"]))
    block.vector(_replay(cx.streams["vector"]))
    block.scalar(_replay(cx.streams["scalar"]))
    block.gpsimd(_replay(cx.streams["gpsimd"]))
    block.sync(_replay(cx.streams["sync"]))
    es.close()
    return nc


def make_in_maps(inputs, extra_per_core=None):
    x = np.asarray(inputs["L0_attn_x"], dtype=np.float32)
    shared = {}
    for name, shape, kind in _DRAM_TABLE:
        if kind != "I" or name in ("x_cm", "L1_attn_lamT"):
            continue
        if name in ("L1_ffn_W_rT", "L1_ffn_W_kT", "L1_ffn_W_vT"):
            base = name.replace("T", "")
            shared[name] = np.ascontiguousarray(
                np.asarray(inputs[base], np.float32).T)
        else:
            arr = np.asarray(inputs[name], np.float32)
            shared[name] = np.ascontiguousarray(arr.reshape(shape))
    shared["L1_attn_lamT"] = np.ascontiguousarray(
        np.asarray(inputs["L1_attn_landmarks"], np.float32).T)
    in_maps = []
    for b in range(8):
        m = dict(shared)
        m["x_cm"] = np.ascontiguousarray(x[b].T)
        if extra_per_core:
            for k, v in extra_per_core.items():
                m[k] = v[b]
        in_maps.append(m)
    return in_maps


def kernel(**inputs):
    in_maps = make_in_maps(inputs)
    nc = build_program()
    res = run_bass_kernel_spmd(nc, in_maps, core_ids=list(range(8)))
    out = np.stack([np.asarray(res.results[b]["out_cm"], np.float32).T
                    for b in range(8)])
    return np.ascontiguousarray(out.astype(np.float32))


# version 04 (final)
# speedup vs baseline: 1.0002x; 1.0002x over previous
import numpy as np
import concourse.bass as bass
import concourse.mybir as mybir
from concourse.bass_utils import run_bass_kernel_spmd
from contextlib import ExitStack

F32 = mybir.dt.float32
BF16 = mybir.dt.bfloat16
AF = mybir.ActivationFunctionType
ALU = mybir.AluOpType
AX = mybir.AxisListType

T = 4096
C = 1024
NCH = C // 128
NT = T // 512
MAGIC = float(1.5 * 2 ** 23)
TPI_HI = 12868.0 / 2048.0
TPI_LO = float(2.0 * np.pi - 12868.0 / 2048.0)
LNEPS = 1e-5

ENGINES = ("tensor", "vector", "scalar", "gpsimd", "sync")


class DInstr:
    __slots__ = ("calls",)

    def __init__(self):
        self.calls = []

    def then_inc(self, sem, n):
        self.calls.append(("then_inc", (sem, n), {}))
        return self


class EngRec:

    def __init__(self, ctx, name):
        object.__setattr__(self, "_ctx", ctx)
        object.__setattr__(self, "_name", name)

    def __getattr__(self, attr):
        ctx, name = self._ctx, self._name
        def call(*a, **k):
            d = DInstr()
            ctx.streams[name].append((attr, a, k, d))
            if attr != "wait_ge":
                ctx.last_real[name] = d
            return d
        return call

    def wait(self, ctr, val=None):
        self.wait_ge(ctr.sem, ctr.n if val is None else val)


class Ctr:

    def __init__(self, sem, step=1):
        self.sem = sem
        self.step = step
        self.n = 0

    def inc(self, dinstr, k=1):
        amt = self.step * k
        self.n += amt
        dinstr.then_inc(self.sem, amt)
        return self.n


class Ctx:
    def __init__(self, nc, es):
        self.nc = nc
        self.es = es
        self.streams = {e: [] for e in ENGINES}
        self.last_real = {e: None for e in ENGINES}
        self._bar_mark = {e: 0 for e in ENGINES}
        self.t = EngRec(self, "tensor")
        self.v = EngRec(self, "vector")
        self.s = EngRec(self, "scalar")
        self.g = EngRec(self, "gpsimd")
        self.q = EngRec(self, "sync")
        self.ld = Ctr(es.enter_context(nc.semaphore("c_ld")), 16)
        self.lg = Ctr(es.enter_context(nc.semaphore("c_lg")), 16)
        self.sp = Ctr(es.enter_context(nc.semaphore("c_sp")), 16)
        self.bar = Ctr(es.enter_context(nc.semaphore("c_bar")), 1)
        self.fine = [Ctr(es.enter_context(nc.semaphore(f"c_f{i}")), 1)
                     for i in range(26)]
        self.dmac = [Ctr(es.enter_context(nc.semaphore(f"c_d{i}")), 16)
                     for i in range(20)]
        self._bar_dmam = {}
        self.sb = {}
        self._names = {}
        self.dram = {}
        self._scopes = []

    def _uniq(self, name):
        n = self._names.get(name, 0)
        self._names[name] = n + 1
        return name if n == 0 else f"{name}__{n}"

    def sbuf(self, name, shape, dtype):
        st = self._scopes[-1] if self._scopes else self.es
        return st.enter_context(self.nc.sbuf_tensor(self._uniq(name), shape, dtype))

    def psum(self, name, shape, dtype=F32):
        st = self._scopes[-1] if self._scopes else self.es
        return st.enter_context(self.nc.psum_tensor(self._uniq(name), shape, dtype))

    def scope(self):
        from contextlib import contextmanager

        @contextmanager
        def _s():
            st = ExitStack()
            self._scopes.append(st)
            try:
                yield
            finally:
                self._scopes.pop()
                st.close()
        return _s()

    def all_dma_ctrs(self):
        return [self.ld, self.lg, self.sp] + self.dmac

    def barrier(self):
        used = []
        if len(self.streams["sync"]) > self._bar_mark["sync"]:
            for c in self.all_dma_ctrs():
                if c.n > self._bar_dmam.get(id(c), 0):
                    self.q.wait(c)
                    self._bar_dmam[id(c)] = c.n
            self.q.nop()
        if len(self.streams["gpsimd"]) > self._bar_mark["gpsimd"]:
            self.g.wait(self.lg)
            self.g.nop()
        for e in ENGINES:
            if len(self.streams[e]) > self._bar_mark[e] and self.last_real[e] is not None:
                used.append(e)
        proxies = {"tensor": self.t, "vector": self.v, "scalar": self.s,
                   "gpsimd": self.g, "sync": self.q}
        for e in used:
            d = self.last_real[e]
            if any(c[0] == "then_inc" for c in d.calls):
                alt = None
                for (attr, a, k, dd) in reversed(self.streams[e][-8:]):
                    if dd is not None and attr != "wait_ge" and not dd.calls:
                        alt = dd
                        break
                d = alt if alt is not None else proxies[e].nop()
            self.bar.inc(d)
        tot = self.bar.n
        for e in ENGINES:
            self.streams[e].append(("wait_ge", (self.bar.sem, tot), {}, None))
            self._bar_mark[e] = len(self.streams[e])
        for e in ENGINES:
            self.last_real[e] = None


def _replay(stream):
    def run(eng):
        for (attr, a, k, d) in stream:
            ins = getattr(eng, attr)(*a, **k)
            if d is not None:
                for (m, ma, mk) in d.calls:
                    ins = getattr(ins, m)(*ma, **mk)
    return run


def phase_consts(cx: Ctx):
    cx.sb["ones_col"] = cx.sbuf("ones_col", [128, 1], F32)
    cx.sb["ones_row"] = cx.sbuf("ones_row", [1, 128], F32)
    cx.sb["ones_row_bf"] = cx.sbuf("ones_row_bf", [1, 128], BF16)
    cx.sb["ones_col_bf"] = cx.sbuf("ones_col_bf", [128, 1], BF16)
    cx.sb["eps11"] = cx.sbuf("eps11", [1, 1], F32)
    v = cx.v
    v.memset(cx.sb["ones_col"][:], 1.0)
    v.memset(cx.sb["ones_row"][:], 1.0)
    v.memset(cx.sb["ones_row_bf"][:], 1.0)
    v.memset(cx.sb["ones_col_bf"][:], 1.0)
    d = v.memset(cx.sb["eps11"][:], float(LNEPS))
    c0 = cx.fine[0]
    c0.inc(d)
    for e in ("tensor", "scalar", "gpsimd", "sync"):
        cx.streams[e].append(("wait_ge", (c0.sem, c0.n), {}, None))
    cx._bar_mark = {e: len(cx.streams[e]) for e in ENGINES}


def emit_ln_cm(cx: Ctx, src_name, dst_name, sc):
    t_, v, s, q = cx.t, cx.v, cx.s, cx.q
    src, dst = cx.dram[src_name], cx.dram[dst_name]
    xin, xsq = sc["ln_xin"], sc["ln_xsq"]
    stat_m, stat_v = sc["ln_stat_m"], sc["ln_stat_v"]
    bca, bcb = sc["ln_bca"], sc["ln_bcb"]
    sm, xout = sc["ln_small"], sc["ln_xout"]
    ones_col, ones_row = cx.sb["ones_col"], cx.sb["ones_row"]
    eps11 = cx.sb["eps11"]
    lda = [cx.dmac[0], cx.dmac[1]]
    spa = [cx.dmac[2], cx.dmac[3]]
    c_sq, c_st, c_var, c_rstd, c_mr, c_bc, c_ap = (cx.fine[i] for i in range(1, 8))

    for it in range(NT):
        buf = it % 2
        t0 = 512 * it
        if it >= 2:
            q.wait(c_ap, c_ap.n - 1)
        for ci in range(NCH):
            d = q.dma_start(out=xin[:, buf, ci, :],
                            in_=src[128 * ci:128 * (ci + 1), t0:t0 + 512])
            lda[buf].inc(d)
        s.wait(lda[buf])
        if it >= 1:
            s.wait(c_st)
        d = s.activation(xsq[:], xin.ap()[:, buf, :, :].rearrange("p a b -> p (a b)"),
                         AF.Square)
        c_sq.inc(d)
        t_.wait(lda[buf])
        if it >= 1:
            t_.wait(c_var)
        for ci in range(NCH):
            t_.matmul(stat_m[0:1, :], ones_col[:, 0:1], xin[:, buf, ci, :],
                      start=(ci == 0), stop=(ci == NCH - 1))
        t_.wait(c_sq)
        for ci in range(NCH):
            d = t_.matmul(stat_v[0:1, :], ones_col[:, 0:1],
                          xsq[:, 512 * ci:512 * (ci + 1)],
                          start=(ci == 0), stop=(ci == NCH - 1))
        c_st.inc(d)
        v.wait(c_st)
        cva = cx.fine[8]
        d = v.tensor_scalar(sm[:, 0:512], stat_m[0:1, :], 1.0 / C, None,
                            op0=ALU.mult)
        cva.inc(d)
        d = v.tensor_scalar(sm[:, 512:1024], stat_v[0:1, :], 1.0 / C, None,
                            op0=ALU.mult)
        cva.inc(d)
        v.wait(cva)
        d = v.tensor_mul(sm[:, 1024:1536], sm[:, 0:512], sm[:, 0:512])
        cva.inc(d)
        v.wait(cva)
        d = v.tensor_sub(sm[:, 512:1024], sm[:, 512:1024], sm[:, 1024:1536])
        c_var.inc(d)
        s.wait(c_var)
        d = s.activation(sm[:, 1024:1536], sm[:, 1024:1536] if False else sm[:, 512:1024],
                         AF.Ln, bias=eps11[0:1, 0:1])
        cx.fine[9].inc(d)
        s.wait(cx.fine[9])
        d = s.activation(sm[:, 1024:1536], sm[:, 1024:1536], AF.Exp, scale=-0.5)
        c_rstd.inc(d)
        v.wait(c_rstd)
        d = v.scalar_tensor_tensor(sm[:, 1536:2048], sm[:, 0:512], -1.0,
                                   sm[:, 1024:1536], op0=ALU.mult, op1=ALU.mult)
        c_mr.inc(d)
        t_.wait(c_mr)
        if it >= 1:
            t_.wait(c_ap)
        t_.matmul(bca[:], ones_row[0:1, :], sm[:, 1024:1536], start=True, stop=True)
        d = t_.matmul(bcb[:], ones_row[0:1, :], sm[:, 1536:2048], start=True, stop=True)
        c_bc.inc(d)
        v.wait(c_bc)
        if it >= 2:
            v.wait(spa[buf])
        for ci in range(NCH):
            d = v.tensor_mul(xout[:, buf, ci, :], xin[:, buf, ci, :], bca[:])
            cva.inc(d)
            v.wait(cva)
            d = v.tensor_add(xout[:, buf, ci, :], xout[:, buf, ci, :], bcb[:])
            if ci < NCH - 1:
                cva.inc(d)
                v.wait(cva)
        c_ap.inc(d)
        q.wait(c_ap)
        for ci in range(NCH):
            d = q.dma_start(out=dst[128 * ci:128 * (ci + 1), t0:t0 + 512],
                            in_=xout[:, buf, ci, :])
            spa[buf].inc(d)


def alloc_ln_scratch(cx: Ctx):
    return {
        "ln_xin": cx.sbuf("ln_xin", [128, 2, NCH, 512], F32),
        "ln_xsq": cx.sbuf("ln_xsq", [128, NCH * 512], F32),
        "ln_stat_m": cx.psum("ln_stat_m", [1, 512]),
        "ln_stat_v": cx.psum("ln_stat_v", [1, 512]),
        "ln_bca": cx.psum("ln_bca", [128, 512]),
        "ln_bcb": cx.psum("ln_bcb", [128, 512]),
        "ln_small": cx.sbuf("ln_small", [1, 2048], F32),
        "ln_xout": cx.sbuf("ln_xout", [128, 2, NCH, 512], F32),
    }


def emit_row_load(cx: Ctx, dram_ap, row_sb, ctr=None):
    d = cx.q.dma_start(out=row_sb[:], in_=dram_ap)
    (ctr or cx.ld).inc(d)
    return d


def emit_pp_from_row(cx: Ctx, row_sb_ap, L, out_sb, pcol, c_done):
    ncol = (L + 127) // 128
    cx.t.wait(c_done)
    for j in range(ncol):
        w = min(128, L - 128 * j)
        d = cx.t.matmul(pcol[0:w, j:j + 1], row_sb_ap[0:1, 128 * j:128 * j + w],
                        cx.sb["ones_row"][0:1, 0:1], start=True, stop=True)
    c_done.inc(d)
    cx.v.wait(c_done)
    wlast = L - 128 * (ncol - 1)
    if wlast == 128:
        d = cx.v.tensor_copy(out_sb[0:128, 0:ncol], pcol[0:128, 0:ncol])
    else:
        if ncol > 1:
            cx.v.tensor_copy(out_sb[0:128, 0:ncol-1], pcol[0:128, 0:ncol-1])
        d = cx.v.tensor_copy(out_sb[0:wlast, ncol-1:ncol],
                             pcol[0:wlast, ncol-1:ncol])
    c_done.inc(d)


def emit_add3(cx: Ctx, a_name, b_name, c_name, dst_name):
    v, g, q = cx.v, cx.g, cx.q
    A, B, Cc, D = (cx.dram[n] for n in (a_name, b_name, c_name, dst_name))
    with cx.scope():
        xa = cx.sbuf("d3_a", [128, 2, NCH, 512], F32)
        xb = cx.sbuf("d3_b", [128, 2, NCH, 512], F32)
        xc = cx.sbuf("d3_c", [128, 2, NCH, 512], F32)
        xo = cx.sbuf("d3_o", [128, 2, NCH, 512], F32)
        lda = [cx.dmac[0], cx.dmac[1]]
        spa = [cx.dmac[2], cx.dmac[3]]
        c_ap = cx.fine[8]
        ap0 = c_ap.n
        for it in range(NT):
            buf = it % 2
            t0 = 512 * it
            if it >= 2:
                q.wait(c_ap, ap0 + it - 1)
            for ci in range(NCH):
                sl = (slice(128 * ci, 128 * (ci + 1)), slice(t0, t0 + 512))
                lda[buf].inc(q.dma_start(out=xa[:, buf, ci, :], in_=A[sl]))
                lda[buf].inc(q.dma_start(out=xb[:, buf, ci, :], in_=B[sl]))
                lda[buf].inc(q.dma_start(out=xc[:, buf, ci, :], in_=Cc[sl]))
            v.wait(lda[buf])
            if it >= 2:
                v.wait(spa[buf])
            cvd = cx.fine[9]
            for ci in range(NCH):
                d = v.tensor_add(xo[:, buf, ci, :], xa[:, buf, ci, :],
                                 xb[:, buf, ci, :])
                cvd.inc(d)
                v.wait(cvd)
                d = v.tensor_add(xo[:, buf, ci, :], xo[:, buf, ci, :],
                                 xc[:, buf, ci, :])
                if ci < NCH - 1:
                    cvd.inc(d)
                    v.wait(cvd)
            c_ap.inc(d)
            q.wait(c_ap)
            for ci in range(NCH):
                sl = (slice(128 * ci, 128 * (ci + 1)), slice(t0, t0 + 512))
                spa[buf].inc(q.dma_start(out=D[sl], in_=xo[:, buf, ci, :]))
    cx.barrier()


def emit_l1ffn_combine(cx: Ctx):
    t_, v, s, g, q = cx.t, cx.v, cx.s, cx.g, cx.q
    xn1, x1, P, out = (cx.dram[n] for n in ("xn1_cm", "x1_cm", "P_cm", "out_cm"))
    TCH = 256
    NIT = T // TCH
    with cx.scope():
        wr = cx.sbuf("g_wr", [128, NCH, C], BF16)
        wk = cx.sbuf("g_wk", [128, NCH, C], BF16)
        wv = cx.sbuf("g_wv", [128, NCH, C], BF16)
        mu_row = cx.sbuf("g_murow", [1, 2 * C], F32)
        mu_pp = cx.sbuf("g_mupp", [128, 2 * NCH], F32)
        xin = cx.sbuf("g_xin", [128, 2, NCH, TCH + 1], F32)
        dtmp = cx.sbuf("g_dtmp", [128, NCH, TCH], F32)
        xr = cx.sbuf("g_xr", [128, 2, NCH, TCH], BF16)
        xk = cx.sbuf("g_xk", [128, 2, NCH, TCH], BF16)
        r_sb = cx.sbuf("g_r", [128, 2, NCH, TCH], F32)
        krelu = cx.sbuf("g_krelu", [128, 2, TCH], F32)
        kact = cx.sbuf("g_kact", [128, 2, NCH, TCH], BF16)
        x1in = cx.sbuf("g_x1", [128, 2, NCH, TCH], F32)
        pin = cx.sbuf("g_p", [128, 2, NCH, TCH], F32)
        osb = cx.sbuf("g_o", [128, 2, NCH, TCH], F32)
        pcol = cx.psum("g_pcol", [128, 16])
        pr = [cx.psum(f"g_pr{i}", [128, 512]) for i in range(2)]
        pk = [cx.psum(f"g_pk{i}", [128, 512]) for i in range(2)]
        pvp = [cx.psum(f"g_pv{i}", [128, 512]) for i in range(2)]
        ld, lg, sp = cx.ld, cx.lg, cx.sp
        lda = [cx.dmac[4], cx.dmac[5]]
        spg = [cx.dmac[6], cx.dmac[7]]
        c_pp = cx.fine[8]
        c_xr = cx.fine[9]
        c_pr = cx.fine[10]
        c_pk = cx.fine[11]
        c_sig = cx.fine[12]
        c_kact = cx.fine[13]
        c_pv = cx.fine[14]
        c_comb = cx.fine[15]
        c_ini = cx.fine[7]

        lgw = cx.dmac[14]
        for ci in range(NCH):
            lgw.inc(g.dma_start(out=wr[:, ci, :], in_=cx.dram["L1_ffn_W_rT"][128*ci:128*(ci+1), :]))
            lgw.inc(g.dma_start(out=wk[:, ci, :], in_=cx.dram["L1_ffn_W_kT"][128*ci:128*(ci+1), :]))
            lgw.inc(g.dma_start(out=wv[:, ci, :], in_=cx.dram["L1_ffn_W_vT"][128*ci:128*(ci+1), :]))
        ldr = cx.dmac[9]
        emit_row_load(cx, cx.dram["L1_ffn_mu_r"][0:1, :], mu_row[:, 0:C], ldr)
        emit_row_load(cx, cx.dram["L1_ffn_mu_k"][0:1, :], mu_row[:, C:2*C], ldr)
        cx.t.wait(ldr)
        emit_pp_from_row(cx, mu_row[:, 0:2*C], 2 * C, mu_pp, pcol, c_pp)
        d = v.memset(xin[:, 0, :, 0:1], 0.0)
        c_ini.inc(d)
        q.wait(c_ini)
        t_.wait(lgw)

        xr0, pr0, pk0, sig0, kact0, pv0, comb0 = (c.n for c in
            (c_xr, c_pr, c_pk, c_sig, c_kact, c_pv, c_comb))
        for it in range(NIT + 1):
            buf = it % 2
            t0 = TCH * it
            if it < NIT:
                if it >= 2:
                    q.wait(c_xr, xr0 + it - 1)
                    q.wait(c_comb, comb0 + NCH * (it - 1))
                for ci in range(NCH):
                    lo = max(t0 - 1, 0)
                    w = t0 + TCH - lo
                    lda[buf].inc(q.dma_start(out=xin[:, buf, ci, (TCH + 1 - w):],
                                             in_=xn1[128*ci:128*(ci+1), lo:t0 + TCH]))
                for ci in range(NCH):
                    lda[buf].inc(q.dma_start(out=x1in[:, buf, ci, :],
                                             in_=x1[128*ci:128*(ci+1), t0:t0+TCH]))
                    lda[buf].inc(q.dma_start(out=pin[:, buf, ci, :],
                                             in_=P[128*ci:128*(ci+1), t0:t0+TCH]))
                v.wait(lda[buf])
                v.wait(c_pp)
                if it >= 2:
                    v.wait(c_pk, pk0 + NCH * (it - 1))
                cvg2 = cx.fine[6]
                for ci in range(NCH):
                    xv = xin[:, buf, ci, 1:TCH+1]
                    xp = xin[:, buf, ci, 0:TCH]
                    d = v.tensor_sub(dtmp[:, ci, :], xv, xp)
                    cvg2.inc(d)
                    v.wait(cvg2)
                    v.scalar_tensor_tensor(xr[:, buf, ci, :], dtmp[:, ci, :],
                                           mu_pp[:, ci:ci+1], xp,
                                           op0=ALU.mult, op1=ALU.add)
                    d = v.scalar_tensor_tensor(xk[:, buf, ci, :], dtmp[:, ci, :],
                                               mu_pp[:, NCH+ci:NCH+ci+1], xp,
                                               op0=ALU.mult, op1=ALU.add)
                c_xr.inc(d)
                for dc in range(NCH):
                    gidx = it * NCH + dc
                    slot = gidx % 2
                    t_.wait(c_xr)
                    if gidx >= 2:
                        t_.wait(c_sig, sig0 + gidx - 1)
                    for ci in range(NCH):
                        d = t_.matmul(pr[slot][:, 0:TCH], wr[:, ci, 128*dc:128*(dc+1)],
                                      xr[:, buf, ci, :], start=(ci == 0), stop=(ci == NCH-1))
                    c_pr.inc(d)
                    if gidx >= 2:
                        t_.wait(c_kact, kact0 + gidx - 1)
                    for ci in range(NCH):
                        d = t_.matmul(pk[slot][:, 0:TCH], wk[:, ci, 128*dc:128*(dc+1)],
                                      xk[:, buf, ci, :], start=(ci == 0), stop=(ci == NCH-1))
                    c_pk.inc(d)
                    s.wait(c_pr, pr0 + gidx + 1)
                    if it >= 2:
                        s.wait(c_comb, comb0 + NCH * (it - 1))
                    d = s.activation(r_sb[:, buf, dc, :], pr[slot][:, 0:TCH], AF.Sigmoid)
                    c_sig.inc(d)
                    s.wait(c_pk, pk0 + gidx + 1)
                    crl = cx.fine[16]
                    d = s.activation(krelu[:, slot, :], pk[slot][:, 0:TCH], AF.Relu)
                    crl.inc(d)
                    s.wait(crl)
                    d = s.activation(kact[:, buf, dc, :], krelu[:, slot, :], AF.Square)
                    c_kact.inc(d)
            if it >= 1:
                jt = it - 1
                jbuf = jt % 2
                for dc in range(NCH):
                    gidx = jt * NCH + dc
                    slot = gidx % 2
                    t_.wait(c_kact, kact0 + NCH * it)
                    if gidx >= 2:
                        t_.wait(c_comb, comb0 + gidx - 1)
                    for ci in range(NCH):
                        d = t_.matmul(pvp[slot][:, 0:TCH], wv[:, ci, 128*dc:128*(dc+1)],
                                      kact[:, jbuf, ci, :], start=(ci == 0), stop=(ci == NCH-1))
                    c_pv.inc(d)
                    v.wait(c_pv, pv0 + gidx + 1)
                    if jt >= 2 and dc == 0:
                        v.wait(spg[jbuf])
                    cvg2 = cx.fine[6]
                    d = v.scalar_tensor_tensor(osb[:, jbuf, dc, :],
                                               pvp[slot][:, 0:TCH], 1.0,
                                               r_sb[:, jbuf, dc, :],
                                               op0=ALU.mult, op1=ALU.mult)
                    cvg2.inc(d)
                    v.wait(cvg2)
                    d = v.tensor_add(osb[:, jbuf, dc, :], osb[:, jbuf, dc, :],
                                     x1in[:, jbuf, dc, :])
                    cvg2.inc(d)
                    v.wait(cvg2)
                    d = v.tensor_add(osb[:, jbuf, dc, :], osb[:, jbuf, dc, :],
                                     pin[:, jbuf, dc, :])
                    c_comb.inc(d)
                q.wait(c_comb)
                for ci in range(NCH):
                    spg[jbuf].inc(q.dma_start(
                        out=out[128*ci:128*(ci+1), TCH*jt:TCH*(jt+1)],
                        in_=osb[:, jbuf, ci, :]))
    cx.barrier()


def emit_l0_attn(cx: Ctx):
    t_, v, s, g, q = cx.t, cx.v, cx.s, cx.g, cx.q
    H, D = 16, 64
    ld, lg, sp = cx.ld, cx.lg, cx.sp
    with cx.scope():
        ycm = cx.sbuf("b_ycm", [128, NCH, T], BF16)
        idb = cx.sbuf("b_idb", [128, 128], BF16)
        madd = cx.sbuf("b_madd", [128, 128], F32)
        iot = cx.sbuf("b_iot", [128, 128], F32)
        vcol = cx.sbuf("b_vcol", [128, 128], F32)
        c_ini = cx.fine[7]
        g.iota(iot[:], pattern=[[1, 128]], base=0, channel_multiplier=0,
               allow_small_or_imprecise_dtypes=True)
        d = g.iota(vcol[:], pattern=[[0, 128]], base=0, channel_multiplier=1,
                   allow_small_or_imprecise_dtypes=True)
        c_ini.inc(d)
        v.wait(c_ini)
        v.tensor_tensor(idb[:], iot[:], vcol[:], op=ALU.is_equal)
        d = v.tensor_tensor(madd[:], iot[:], vcol[:], op=ALU.is_gt)
        c_ini.inc(d)
        v.wait(c_ini)
        d = v.tensor_scalar(madd[:], madd[:], -1e30, None, op0=ALU.mult)
        c_ini.inc(d)
        t_.wait(c_ini)

        with cx.scope():
            wqs = cx.sbuf("b_wq", [128, NCH, C], BF16)
            wks = cx.sbuf("b_wk", [128, NCH, C], BF16)
            wvs = cx.sbuf("b_wv", [128, NCH, C], BF16)
            brow = cx.sbuf("b_brow", [1, 2 * C], F32)
            bvrow = cx.sbuf("b_bvrow", [1, C], BF16)
            bpp = cx.sbuf("b_bpp", [128, 2 * NCH], F32)
            xbf = cx.sbuf("b_xbf", [128, 2, NCH, 512], BF16)
            qko = cx.sbuf("b_qko", [128, 2, 512], BF16)
            vo = cx.sbuf("b_vo", [128, 2, 512], BF16)
            pcol = cx.psum("b_pcol", [128, 16])
            pqk = [cx.psum(f"b_pqk{i}", [128, 512]) for i in range(2)]
            pv = [cx.psum(f"b_pv{i}", [128, 512]) for i in range(2)]
            c_pp = cx.fine[8]
            c_mm = cx.fine[9]
            c_ev = cx.fine[10]
            lgw = cx.dmac[14]
            ldr = cx.dmac[9]
            lgx = [cx.dmac[17], cx.dmac[18]]
            spb = [cx.dmac[13], cx.dmac[8]]
            for ci in range(NCH):
                r = slice(128 * ci, 128 * (ci + 1))
                lgw.inc(g.dma_start(out=wqs[:, ci, :], in_=cx.dram["L0_attn_Wq"][r, :]))
                lgw.inc(g.dma_start(out=wks[:, ci, :], in_=cx.dram["L0_attn_Wk"][r, :]))
                lgw.inc(g.dma_start(out=wvs[:, ci, :], in_=cx.dram["L0_attn_Wv"][r, :]))
            emit_row_load(cx, cx.dram["L0_attn_bq"][0:1, :], brow[:, 0:C], ldr)
            emit_row_load(cx, cx.dram["L0_attn_bk"][0:1, :], brow[:, C:2*C], ldr)
            lgw.inc(g.dma_start(out=bvrow[:], in_=cx.dram["L0_attn_bv"][0:1, :]))
            t_.wait(ldr)
            t_.wait(lgw)
            emit_pp_from_row(cx, brow[:, 0:2*C], 2 * C, bpp, pcol, c_pp)
            v.wait(c_pp)
            d = v.tensor_scalar(bpp[:, 0:NCH], bpp[:, 0:NCH], 0.125, None,
                                op0=ALU.mult)
            c_pp.inc(d)
            s.wait(c_pp)
            mm0, ev0 = c_mm.n, c_ev.n
            gi = 0
            for it in range(NT):
                buf = it % 2
                t0 = 512 * it
                if it >= 2:
                    g.wait(c_mm, mm0 + 24 * (it - 1))
                for ci in range(NCH):
                    lgx[buf].inc(g.dma_start(out=xbf[:, buf, ci, :],
                                             in_=cx.dram["xn_cm"][128*ci:128*(ci+1), t0:t0+512]))
                t_.wait(lgx[buf])
                for wsb, boff, dst in ((wqs, 0, "q0_cm"), (wks, NCH, "k0_cm")):
                    for nb in range(NCH):
                        slot = gi % 2
                        if gi >= 2:
                            t_.wait(c_ev, ev0 + gi - 1)
                        for ci in range(NCH):
                            d = t_.matmul(pqk[slot][:], wsb[:, ci, 128*nb:128*(nb+1)],
                                          xbf[:, buf, ci, :], start=(ci == 0),
                                          stop=(ci == NCH - 1))
                        c_mm.inc(d)
                        s.wait(c_mm)
                        if gi >= 2:
                            s.wait(spb[slot])
                        sc = 0.125 if boff == 0 else 1.0
                        d = s.activation(qko[:, slot, :], pqk[slot][:], AF.Identity,
                                         scale=sc, bias=bpp[:, boff+nb:boff+nb+1])
                        c_ev.inc(d)
                        q.wait(c_ev, ev0 + gi + 1)
                        spb[slot].inc(q.dma_start(
                            out=cx.dram[dst][128*nb:128*(nb+1), t0:t0+512],
                            in_=qko[:, slot, :]))
                        gi += 1
                for tt in range(4):
                    for nb2 in range(2):
                        slot = gi % 2
                        if gi >= 2:
                            t_.wait(c_ev, ev0 + gi - 1)
                        nsl = slice(512 * nb2, 512 * (nb2 + 1))
                        t_.matmul(pv[slot][:], cx.sb["ones_row_bf"][0:1, :],
                                  bvrow[0:1, nsl], start=True, stop=False)
                        for ci in range(NCH):
                            d = t_.matmul(pv[slot][:], xbf[:, buf, ci, 128*tt:128*(tt+1)],
                                          wvs[:, ci, nsl], start=False,
                                          stop=(ci == NCH - 1))
                        c_mm.inc(d)
                        s.wait(c_mm)
                        if gi >= 2:
                            s.wait(spb[slot])
                        d = s.activation(vo[:, slot, :], pv[slot][:], AF.Copy)
                        c_ev.inc(d)
                        q.wait(c_ev, ev0 + gi + 1)
                        tr = slice(t0 + 128 * tt, t0 + 128 * (tt + 1))
                        for hh in range(8):
                            h = nb2 * 8 + hh
                            spb[slot].inc(q.dma_start(out=cx.dram["v0_tm"][h, tr, :],
                                                      in_=vo[:, slot, 64*hh:64*(hh+1)]))
                        gi += 1
        cx.barrier()
        with cx.scope():
            q_sb = cx.sbuf("b_qsb", [64, 2, T], BF16)
            k_sb = cx.sbuf("b_ksb", [64, 2, T], BF16)
            v_sb = cx.sbuf("b_vsb", [128, 2, 32, D], BF16)
            P_sb = cx.sbuf("b_psb", [128, 2, 1024], BF16)
            PT_sb = cx.sbuf("b_ptsb", [128, 2, 512], BF16)
            ytmp = cx.sbuf("b_ytmp", [128, 2, D], BF16)
            mrow = cx.sbuf("b_mrow", [128, 10], F32)
            psB = [cx.psum(f"b_psB{i}", [128, 1024]) for i in range(2)]
            ptp = [cx.psum(f"b_ptp{i}", [128, 1024], BF16) for i in range(2)]
            pav = [cx.psum(f"b_pav{i}", [128, 512]) for i in range(2)]
            yacc = cx.sbuf("b_yacc", [128, 2, 64], F32)
            c_fold = cx.fine[19]
            fold0 = c_fold.n
            c_hd = cx.fine[6]
            cx.v.memset(ptp[0].ap().bitcast(F32), 0.0)
            d = cx.v.memset(ptp[1].ap().bitcast(F32), 0.0)
            cx.fine[17].inc(d)
            t_.wait(cx.fine[17])
            c_sim = cx.fine[8]
            c_red = cx.fine[9]
            c_exp = cx.fine[10]
            c_lup = cx.fine[11]
            c_tp = cx.fine[12]
            c_ptev = [cx.fine[13], cx.fine[16]]
            c_av = cx.fine[14]
            c_y = cx.fine[15]
            sim0, red0, exp0, lup0, tp0, av0, y0, hd0 = (c.n for c in
                (c_sim, c_red, c_exp, c_lup, c_tp, c_av, c_y, c_hd))
            ptev0 = [c_ptev[0].n, c_ptev[1].n]
            cnum = 0
            gnum = 0
            ynum = 0
            tpmark = []
            avmark = []
            ldmark = {}

            ldh = [cx.dmac[11], cx.dmac[12]]

            def load_head(h):
                hb = h % 2
                if h >= 2:
                    q.wait(c_hd, hd0 + h - 1)
                r = slice(64 * h, 64 * (h + 1))
                ldh[hb].inc(q.dma_start(out=q_sb[:, hb, :], in_=cx.dram["q0_cm"][r, :]))
                ldh[hb].inc(q.dma_start(out=k_sb[:, hb, :], in_=cx.dram["k0_cm"][r, :]))
                ldh[hb].inc(q.dma_start(
                    out=v_sb[:, hb, :, :],
                    in_=cx.dram["v0_tm"][h, :, :].rearrange("(a p) d -> p a d", p=128)))
                ldmark[h] = ldh[hb].n
            load_head(0)
            load_head(1)
            for h in range(H):
                hb = h % 2
                if h + 2 < H:
                    load_head(h + 2)
                t_.wait_ge(ldh[h % 2].sem, ldmark[h])
                for i in range(32):
                    rows = (i + 1) * 128
                    nch = (rows + 1023) // 1024
                    ypar = ynum % 2
                    for c in range(nch):
                        jlo = 1024 * c
                        jw = min(1024, rows - jlo)
                        par = cnum % 2
                        if cnum >= 2:
                            t_.wait(c_exp, exp0 + cnum - 1)
                        for sl in range(0, jw, 512):
                            sw = min(512, jw - sl)
                            d = t_.matmul(psB[par][:, sl:sl+sw],
                                          q_sb[:, hb, 128*i:128*(i+1)],
                                          k_sb[:, hb, jlo+sl:jlo+sl+sw],
                                          start=True, stop=True)
                        c_sim.inc(d)
                        cv = cx.fine[5]
                        v.wait(c_sim, sim0 + cnum + 1)
                        if c == nch - 1:
                            d = v.tensor_add(psB[par][:, jw-128:jw],
                                             psB[par][:, jw-128:jw], madd[:])
                            cv.inc(d)
                            v.wait(cv)
                        if c == 0:
                            d = v.reduce_max(mrow[:, 3:4], psB[par][:, 0:jw],
                                             axis=AX.X, negate=True)
                            c_red.inc(d)
                        else:
                            d = v.reduce_max(mrow[:, 1:2], psB[par][:, 0:jw],
                                             axis=AX.X, negate=True)
                            cv.inc(d)
                            v.wait(cv)
                            d = v.tensor_tensor(mrow[:, 1:2], mrow[:, 3:4],
                                                mrow[:, 1:2], op=ALU.min)
                            cv.inc(d)
                            v.wait(cv)
                            d = v.tensor_sub(mrow[:, 2:3], mrow[:, 1:2], mrow[:, 3:4])
                            cv.inc(d)
                            v.wait(cv)
                            d = v.tensor_copy(mrow[:, 3:4], mrow[:, 1:2])
                            c_red.inc(d)
                        s.wait(c_red, red0 + cnum + 1)
                        if cnum >= 1:
                            s.wait(c_lup, lup0 + cnum)
                            s.wait(c_fold, fold0 + cnum)
                        if cnum >= 2:
                            s.wait(c_tp, tpmark[cnum - 2])
                        if c > 0:
                            s.activation(mrow[:, 6:7], mrow[:, 2:3], AF.Exp)
                        lcc = 4 if cnum % 2 == 0 else 8
                        d = s.activation(P_sb[:, par, 0:jw], psB[par][:, 0:jw], AF.Exp,
                                         bias=mrow[:, 3:4],
                                         accum_out=mrow[:, lcc:lcc+1])
                        c_exp.inc(d)
                        v.wait(c_exp, exp0 + cnum + 1)
                        if c == 0:
                            d = v.tensor_copy(mrow[:, 5:6], mrow[:, lcc:lcc+1])
                            c_lup.inc(d)
                        else:
                            d = v.scalar_tensor_tensor(mrow[:, 5:6], mrow[:, 5:6],
                                                       mrow[:, 6:7], mrow[:, lcc:lcc+1],
                                                       op0=ALU.mult, op1=ALU.add)
                            c_lup.inc(d)
                        for sl in range(0, jw, 512):
                            sw = min(512, jw - sl)
                            gpar = gnum % 2
                            ntp = (sw + 127) // 128
                            if gnum >= 2:
                                t_.wait(c_ptev[gpar], ptev0[gpar] + gnum // 2)
                            t_.wait(c_exp, exp0 + cnum + 1)
                            for jt in range(ntp):
                                w2 = min(128, sw - 128 * jt)
                                d = t_.transpose(ptp[gpar][0:w2, 128*jt:128*jt+128],
                                                 P_sb[:, par, sl+128*jt:sl+128*jt+w2],
                                                 idb[:])
                            c_tp.inc(d)
                            eng = v if (gnum % 2 == 0) else s
                            eng.wait(c_tp)
                            if eng is s:
                                d = s.activation(PT_sb[:, gpar, :], ptp[gpar][:, 0:512],
                                                 AF.Copy)
                            else:
                                d = v.tensor_copy(PT_sb[:, gpar, :], ptp[gpar][:, 0:512])
                            c_ptev[gpar].inc(d)
                            t_.wait(c_ptev[gpar], ptev0[gpar] + gnum // 2 + 1)
                            if sl == 0 and cnum >= 2:
                                t_.wait(c_fold, fold0 + cnum - 1)
                            cpar = cnum % 2
                            for jt in range(ntp):
                                w2 = min(128, sw - 128 * jt)
                                kt = (jlo + sl + 128 * jt) // 128
                                d = t_.matmul(pav[cpar][:, 0:D],
                                              PT_sb[0:w2, gpar, 128*jt:128*jt+128],
                                              v_sb[0:w2, hb, kt, :],
                                              start=(sl == 0 and jt == 0),
                                              stop=(sl + 512 >= jw
                                                    and jt == ntp - 1))
                            c_av.inc(d)
                            gnum += 1
                        v.wait(c_av)
                        if c == 0:
                            d = v.tensor_copy(yacc[:, ypar, :], pav[cnum % 2][:, 0:D])
                        else:
                            v.wait(c_fold)
                            d = v.scalar_tensor_tensor(yacc[:, ypar, :],
                                                       yacc[:, ypar, :], mrow[:, 6:7],
                                                       pav[cnum % 2][:, 0:D],
                                                       op0=ALU.mult, op1=ALU.add)
                        c_fold.inc(d)
                        tpmark.append(c_tp.n)
                        avmark.append(c_av.n)
                        cnum += 1
                    v.wait(c_fold)
                    cv = cx.fine[5]
                    d = v.reciprocal(mrow[:, 7:8], mrow[:, 5:6])
                    cv.inc(d)
                    v.wait(cv)
                    d = v.tensor_scalar(ytmp[:, ypar, :], yacc[:, ypar, :],
                                        mrow[:, 7:8], None, op0=ALU.mult)
                    c_y.inc(d)
                    gpar = gnum % 2
                    if gnum >= 2:
                        t_.wait(c_ptev[gpar], ptev0[gpar] + gnum // 2)
                    t_.wait(c_y)
                    d = t_.transpose(ptp[gpar][0:D, 0:128], ytmp[:, ypar, :], idb[:])
                    c_tp.inc(d)
                    s.wait(c_tp)
                    d = s.activation(ycm[(h % 2)*64:(h % 2)*64+64, h // 2,
                                         128*i:128*(i+1)],
                                     ptp[gpar][0:D, 0:128], AF.Copy)
                    c_ptev[gpar].inc(d)
                    gnum += 1
                    ynum += 1
                d = t_.nop()
                c_hd.inc(d)
        cx.barrier()
        if "ycm_dbg" in cx.dram:
            for ci in range(NCH):
                sp.inc(q.dma_start(out=cx.dram["ycm_dbg"][ci, :, :],
                                   in_=ycm[:, ci, :]))
            cx.barrier()
        with cx.scope():
            wps = cx.sbuf("b_wp", [128, NCH, C], BF16)
            bprow = cx.sbuf("b_bprow", [1, C], F32)
            bppp = cx.sbuf("b_bppp", [128, NCH], F32)
            asb = cx.sbuf("b_asb", [128, 2, 512], F32)
            pcol3 = cx.psum("b_pcol3", [128, 8])
            pa = [cx.psum(f"b_pa{i}", [128, 512]) for i in range(2)]
            c_pp = cx.fine[8]
            c_mm = cx.fine[9]
            c_ev = cx.fine[10]
            lgw3 = cx.dmac[14]
            ldr3 = cx.dmac[9]
            spb3 = [cx.dmac[13], cx.dmac[8]]
            for ci in range(NCH):
                lgw3.inc(g.dma_start(out=wps[:, ci, :],
                                     in_=cx.dram["L0_attn_Wp"][128*ci:128*(ci+1), :]))
            emit_row_load(cx, cx.dram["L0_attn_bp"][0:1, :], bprow[:], ldr3)
            t_.wait(ldr3)
            t_.wait(lgw3)
            emit_pp_from_row(cx, bprow[:], C, bppp, pcol3, c_pp)
            s.wait(c_pp)
            gi = 0
            mm0, ev0 = c_mm.n, c_ev.n
            for it in range(NT):
                t0 = 512 * it
                for ob in range(NCH):
                    slot = gi % 2
                    if gi >= 2:
                        t_.wait(c_ev, ev0 + gi - 1)
                    for ci in range(NCH):
                        d = t_.matmul(pa[slot][:], wps[:, ci, 128*ob:128*(ob+1)],
                                      ycm[:, ci, t0:t0+512], start=(ci == 0),
                                      stop=(ci == NCH - 1))
                    c_mm.inc(d)
                    s.wait(c_mm)
                    if gi >= 2:
                        s.wait(spb3[slot])
                    d = s.activation(asb[:, slot, :], pa[slot][:], AF.Identity,
                                     bias=bppp[:, ob:ob+1])
                    c_ev.inc(d)
                    q.wait(c_ev, ev0 + gi + 1)
                    spb3[slot].inc(q.dma_start(
                        out=cx.dram["A_cm"][128*ob:128*(ob+1), t0:t0+512],
                        in_=asb[:, slot, :]))
                    gi += 1
    cx.barrier()


def _emit_rope_tables(cx, tabs, sc, pq0, pq1, xpos=True):
    t_, v, s, g = cx.t, cx.v, cx.s, cx.g
    cosS, sinS, cosI, sinI = tabs
    irow, r32, wk, halfpi = sc["irow"], sc["r32"], sc["wk"], sc["halfpi"]
    cvv = cx.fine[4]
    cvg = cx.fine[24]
    cvs = cx.fine[25]
    cvt = cx.fine[18]
    MG = MAGIC
    d = g.iota(r32[0:1, 0:32], pattern=[[1, 32]], base=0, channel_multiplier=0,
               allow_small_or_imprecise_dtypes=True)
    cvg.inc(d)
    s.wait(cvg)
    d = s.activation(r32[:, 32:64], r32[:, 0:32], AF.Exp,
                     scale=-float(np.log(10000.0) / 32.0))
    cvs.inc(d)
    v.wait(cvg)
    if xpos:
        d = v.tensor_scalar(r32[:, 64:96], r32[:, 0:32], float(2.0 / 89.6),
                            float(25.6 / 89.6), op0=ALU.mult, op1=ALU.add)
        cvv.inc(d)
        s.wait(cvv)
        d = s.activation(r32[:, 64:96], r32[:, 64:96], AF.Ln)
        cvs.inc(d)
    else:
        d = v.memset(r32[:, 64:96], 0.0)
        cvv.inc(d)
        s.wait(cvv)
    t_.wait(cvs)
    for half, pq in ((0, pq0), (1, pq1)):
        for phase in (0, 1):
            for j in range(16):
                tile = 16 * half + j
                g.wait(cvt)
                d = g.iota(irow[0:1, 0:128], pattern=[[1, 128]], base=128 * tile,
                           channel_multiplier=0,
                           allow_small_or_imprecise_dtypes=True)
                cvg.inc(d)
                if phase == 1:
                    v.wait(cvg)
                    d = v.tensor_scalar(irow[0:1, 128:256], irow[0:1, 0:128],
                                        float(1.0 / 128.0), -16.0,
                                        op0=ALU.mult, op1=ALU.add)
                    cvv.inc(d)
                    t_.wait(cvv)
                else:
                    t_.wait(cvg)
                rr = r32[0:1, 32:64] if phase == 0 else r32[0:1, 64:96]
                ri = irow[0:1, 0:128] if phase == 0 else irow[0:1, 128:256]
                d = t_.matmul(pq[:, 32*j:32*(j+1)], ri, rr, start=True, stop=True)
                cvt.inc(d)
                if phase == 0 and j == 15:
                    v.wait(cvt)
                    d = v.tensor_scalar(wk[:, 0, :], pq[:], float(1 / (2 * np.pi)),
                                        MG, op0=ALU.mult, op1=ALU.add)
                    cvv.inc(d)
                    d = v.tensor_scalar(wk[:, 1, :], pq[:], float(1 / (2 * np.pi)),
                                        0.25, op0=ALU.mult, op1=ALU.add)
                    cvv.inc(d)
                    v.wait(cvv)
                    d = v.tensor_scalar(wk[:, 1, :], wk[:, 1, :], MG, None,
                                        op0=ALU.add)
                    cvv.inc(d)
                    v.wait(cvv)
                    d = v.tensor_scalar(wk[:, 2, :], wk[:, 0, :], MG, -TPI_HI,
                                        op0=ALU.subtract, op1=ALU.mult)
                    cvv.inc(d)
                    d = v.tensor_scalar(wk[:, 0, :], wk[:, 0, :], MG, -TPI_LO,
                                        op0=ALU.subtract, op1=ALU.mult)
                    cvv.inc(d)
                    d = v.tensor_scalar(wk[:, 3, :], wk[:, 1, :], MG, -TPI_HI,
                                        op0=ALU.subtract, op1=ALU.mult)
                    cvv.inc(d)
                    d = v.tensor_scalar(wk[:, 1, :], wk[:, 1, :], MG, -TPI_LO,
                                        op0=ALU.subtract, op1=ALU.mult)
                    cvv.inc(d)
                    v.wait(cvv)
                    d = v.scalar_tensor_tensor(wk[:, 2, :], pq[:], 1.0, wk[:, 2, :],
                                               op0=ALU.mult, op1=ALU.add)
                    cvv.inc(d)
                    v.wait(cvv)
                    d = v.tensor_add(wk[:, 0, :], wk[:, 0, :], wk[:, 2, :])
                    cvv.inc(d)
                    v.wait(cvv)
                    d = v.scalar_tensor_tensor(wk[:, 2, :], pq[:], 1.0, wk[:, 3, :],
                                               op0=ALU.mult, op1=ALU.add)
                    cvv.inc(d)
                    v.wait(cvv)
                    d = v.scalar_tensor_tensor(wk[:, 1, :], wk[:, 1, :],
                                               float(np.pi / 2), wk[:, 2, :],
                                               op0=ALU.add, op1=ALU.add)
                    cvv.inc(d)
                    v.wait(cvv)
                    d = v.tensor_scalar(wk[:, 0, :], wk[:, 0, :],
                                        -float(np.pi), float(np.pi),
                                        op0=ALU.max, op1=ALU.min)
                    cvv.inc(d)
                    d = v.tensor_scalar(wk[:, 1, :], wk[:, 1, :],
                                        -float(np.pi), float(np.pi),
                                        op0=ALU.max, op1=ALU.min)
                    cvv.inc(d)
                    s.wait(cvv)
                    d = s.activation(wk[:, 2, :], wk[:, 0, :], AF.Sin)
                    cvs.inc(d)
                    d = s.activation(wk[:, 3, :], wk[:, 1, :], AF.Sin)
                    cvs.inc(d)
                    t_.wait(cvs)
        s.wait(cvt)
        d = s.activation(wk[:, 0, :], pq[:], AF.Exp)
        cvs.inc(d)
        d = s.activation(wk[:, 1, :], pq[:], AF.Exp, scale=-1.0)
        cvs.inc(d)
        v.wait(cvs)
        def tv(tab, dup):
            return tab.ap().rearrange("p (tile j) -> p tile j", tile=32, j=64)[
                :, 16*half:16*(half+1), 32*dup:32*(dup+1)]
        wv3 = wk.ap().rearrange("p a (tile j) -> p a tile j", tile=16, j=32)
        for dup in (0, 1):
            sgn = -1.0 if dup == 0 else 1.0
            for dd in (v.tensor_tensor(tv(cosS, dup), wv3[:, 3, :, :],
                                       wv3[:, 0, :, :], op=ALU.mult),
                       v.scalar_tensor_tensor(tv(sinS, dup), wv3[:, 2, :, :],
                                              sgn, wv3[:, 0, :, :],
                                              op0=ALU.mult, op1=ALU.mult),
                       v.tensor_tensor(tv(cosI, dup), wv3[:, 3, :, :],
                                       wv3[:, 1, :, :], op=ALU.mult),
                       v.scalar_tensor_tensor(tv(sinI, dup), wv3[:, 2, :, :],
                                              sgn, wv3[:, 1, :, :],
                                              op0=ALU.mult, op1=ALU.mult)):
                cvv.inc(dd)
        t_.wait(cvv)


def _emit_rope_apply(cx, eng, cc, src_ap, dst_bf_ap, rstd_ap, scale_ap, ta, tb,
                     tmp, prior=None):
    Hh = 8
    if prior is not None:
        eng.wait_ge(prior[0].sem, prior[1])
    skip_scale = rstd_ap is None
    q3 = src_ap.rearrange("p (h j) -> p h j", h=Hh, j=64)
    qh3 = tmp[:, 0, :].rearrange("p (h j) -> p h j", h=Hh, j=64)
    t13 = tmp[:, 1, :].rearrange("p (h j) -> p h j", h=Hh, j=64)
    ta_b = ta.unsqueeze(1).broadcast_to([128, Hh, 64])
    eng.wait(cc)
    if not skip_scale:
        rstd_b = rstd_ap.unsqueeze(2).broadcast_to([128, Hh, 64])
        qs_b = scale_ap.unsqueeze(1).broadcast_to([128, Hh, 64])
        cc.inc(eng.tensor_tensor(qh3, q3, rstd_b, op=ALU.mult))
        eng.wait(cc)
        cc.inc(eng.tensor_tensor(qh3, qh3, qs_b, op=ALU.mult))
        eng.wait(cc)
        cc.inc(eng.tensor_tensor(t13, qh3, ta_b, op=ALU.mult))
    else:
        cc.inc(eng.tensor_copy(tmp[:, 0, :], src_ap))
        eng.wait(cc)
        cc.inc(eng.tensor_tensor(t13, qh3, ta_b, op=ALU.mult))
    qh4 = tmp[:, 0, :].rearrange("p (h hf r) -> p h hf r", h=Hh, hf=2, r=32)
    t24 = tmp[:, 2, :].rearrange("p (h hf r) -> p h hf r", h=Hh, hf=2, r=32)
    tb4 = tb.rearrange("p (hf r) -> p hf r", hf=2, r=32)
    for hf in range(2):
        tb_b = tb4[:, hf, :].unsqueeze(1).broadcast_to([128, Hh, 32])
        cc.inc(eng.tensor_tensor(t24[:, :, hf, :], qh4[:, :, 1 - hf, :], tb_b,
                                 op=ALU.mult))
    eng.wait(cc)
    dst3 = dst_bf_ap.rearrange("p (h j) -> p h j", h=Hh, j=64)
    t23 = tmp[:, 2, :].rearrange("p (h j) -> p h j", h=Hh, j=64)
    return eng.tensor_tensor(dst3, t13, t23, op=ALU.add)


def emit_l0_ffn(cx: Ctx):
    t_, v, s, g, q = cx.t, cx.v, cx.s, cx.g, cx.q
    Hh, D = 16, 64
    W = 256
    NW = T // W
    ld, lg, sp = cx.ld, cx.lg, cx.sp
    with cx.scope():
        idb = cx.sbuf("c_idb", [128, 128], BF16)
        mk = cx.sbuf("c_mk", [128, 4, 512], BF16)
        c_ini = cx.fine[4]
        with cx.scope():
            iot = cx.sbuf("c_iot", [128, 512], F32)
            vcol = cx.sbuf("c_vcol", [128, 512], F32)
            w2 = cx.sbuf("c_mw", [128, 2, 512], F32)
            d = g.iota(iot[:], pattern=[[1, 512]], base=0, channel_multiplier=0,
                       allow_small_or_imprecise_dtypes=True)
            cx.fine[24].inc(d)
            d = g.iota(vcol[:], pattern=[[0, 512]], base=0, channel_multiplier=1,
                       allow_small_or_imprecise_dtypes=True)
            cx.fine[24].inc(d)
            v.wait(cx.fine[24])
            d = v.tensor_tensor(idb[:], iot[:, 0:128], vcol[:, 0:128],
                                op=ALU.is_equal)
            c_ini.inc(d)
            for sub in range(2):
                qi = 128.0 * sub
                d = v.tensor_scalar(w2[:, 0, :], vcol[:], qi, None, op0=ALU.add)
                c_ini.inc(d)
                v.wait(c_ini)
                d = v.tensor_tensor(w2[:, 1, :], iot[:], w2[:, 0, :], op=ALU.is_lt)
                c_ini.inc(d)
                d = v.tensor_scalar(w2[:, 0, :], w2[:, 0, :], 256.0, None,
                                    op0=ALU.add)
                c_ini.inc(d)
                v.wait(c_ini)
                d = v.tensor_tensor(w2[:, 0, :], iot[:], w2[:, 0, :], op=ALU.is_gt)
                c_ini.inc(d)
                v.wait(c_ini)
                d = v.tensor_add(w2[:, 0, :], w2[:, 0, :], w2[:, 1, :])
                c_ini.inc(d)
                v.wait(c_ini)
                d = v.tensor_scalar(mk[:, sub, :], w2[:, 0, :], -1e30, None,
                                    op0=ALU.mult)
                c_ini.inc(d)
                v.wait(c_ini)
                d = v.tensor_scalar(w2[:, 0, 0:256], vcol[:, 0:256], qi, None,
                                    op0=ALU.add)
                c_ini.inc(d)
                v.wait(c_ini)
                d = v.tensor_tensor(w2[:, 1, 0:256], iot[:, 0:256], w2[:, 0, 0:256],
                                    op=ALU.is_gt)
                c_ini.inc(d)
                v.wait(c_ini)
                d = v.tensor_scalar(mk[:, 2 + sub, 0:256], w2[:, 1, 0:256], -1e30,
                                    None, op0=ALU.mult)
                c_ini.inc(d)
            t_.wait(c_ini)
            s.wait(c_ini)
            g.wait(c_ini)
        cx.barrier()
        import os
        nlay = int(os.environ.get("C_LAYERS", "6"))
        parts = os.environ.get("C_PARTS", "af")
        for layer in range(nlay):
            if "a" in parts:
                _emit_c_attn_sweep(cx, layer, idb, mk)
                cx.barrier()
            if "f" in parts:
                _emit_c_ffn_sweep(cx, layer)
                cx.barrier()
                if "zlay_dbg" in cx.dram:
                    for ci in range(NCH):
                        cx.sp.inc(cx.q.dma_start(
                            out=cx.dram["zlay_dbg"][layer, 128*ci:128*(ci+1), :],
                            in_=cx.dram["z_cm"][128*ci:128*(ci+1), :]))
                    cx.barrier()


def _emit_c_attn_sweep(cx: Ctx, layer, idb, mk):
    t_, v, s, g, q = cx.t, cx.v, cx.s, cx.g, cx.q
    Hh, D = 16, 64
    W = 256
    NW = T // W
    ld, lg, sp = cx.ld, cx.lg, cx.sp
    QK_SCALE = 8.0
    with cx.scope():
        wqkv = cx.sbuf("ca_wqkv", [128, NCH, 3072], BF16)
        wo = cx.sbuf("ca_wo", [128, NCH, C], BF16)
        cosS = cx.sbuf("ca_cosS", [128, 2048], F32)
        sinS = cx.sbuf("ca_sinS", [128, 2048], F32)
        cosI = cx.sbuf("ca_cosI", [128, 2048], F32)
        sinI = cx.sbuf("ca_sinI", [128, 2048], F32)
        tbsc = {
            "irow": cx.sbuf("ca_irow", [1, 384], F32),
            "r32": cx.sbuf("ca_r32", [1, 96], F32),
            "wk": cx.sbuf("ca_wk", [128, 4, 512], F32),
            "halfpi": cx.sbuf("ca_halfpi", [128, 1], F32),
        }
        qsrow = cx.sbuf("ca_qsrow", [1, 128], F32)
        qs_b = cx.sbuf("ca_qsb", [128, 128], F32)
        zf = cx.sbuf("ca_zf", [128, 2, NCH, 256], F32)
        zb = cx.sbuf("ca_zb", [128, 2, NCH, 257], BF16)
        qk_tm = cx.sbuf("ca_qktm", [128, 2048], F32)
        qsq = cx.sbuf("ca_qsq", [128, 1024], F32)
        rss = cx.sbuf("ca_rss", [128, 2, 32], F32)
        ropet_v = cx.sbuf("ca_rtv", [128, 3, 512], F32)
        ropet_g = cx.sbuf("ca_rtg", [128, 3, 512], F32)
        qro = cx.sbuf("ca_qro", [128, 2, 2048], BF16)
        vring = cx.sbuf("ca_vring", [128, 2, 2, 1024], BF16)
        qcm = cx.sbuf("ca_qcm", [128, NCH, 256], BF16)
        kcm = cx.sbuf("ca_kcm", [128, 2, NCH, 256], BF16)
        P_sb = cx.sbuf("ca_psb", [128, 2, 512], BF16)
        PT_sb = cx.sbuf("ca_ptsb", [128, 2, 512], BF16)
        ytmp = cx.sbuf("ca_ytmp", [128, 2, D], BF16)
        ycm_w = cx.sbuf("ca_ycmw", [128, 2, NCH, 256], BF16)
        zout = cx.sbuf("ca_zout", [128, 2, 256], F32)
        lrow = cx.sbuf("ca_lrow", [128, 2, 4], F32)
        psim = [cx.psum(f"ca_psim{i}", [128, 512]) for i in range(2)]
        ptp = [cx.psum(f"ca_ptp{i}", [128, 1024], BF16) for i in range(2)]
        pav = [cx.psum(f"ca_pav{i}", [128, 512]) for i in range(2)]
        pq = [cx.psum(f"ca_pq{i}", [128, 512]) for i in range(2)]
        cv = cx.fine[4]
        c_qs = cx.fine[5]
        c_ev = [cx.fine[6], cx.fine[7]]
        c_mm = cx.fine[8]
        c_rp = cx.fine[9]
        c_rpg = cx.fine[10]
        c_tp2 = cx.fine[12]
        c_sim2 = cx.fine[13]
        c_red2 = cx.fine[14]
        c_exp2 = cx.fine[15]
        c_avc = cx.fine[16]
        c_yv = cx.fine[17]
        c_wo = cx.fine[20]
        c_zo = cx.fine[21]
        c_ptev = [cx.fine[22], cx.fine[23]]
        lgw = cx.dmac[14]
        ldr = cx.dmac[9]
        ldz = [cx.dmac[11], cx.dmac[12]]
        lgz = [cx.dmac[15], cx.dmac[16]]
        spz = cx.dmac[13]
        for ci in range(NCH):
            r = slice(128 * ci, 128 * (ci + 1))
            lgw.inc(g.dma_start(out=wqkv[:, ci, :],
                                in_=cx.dram["L0_ffn_Wqkv"][layer, r, :]))
            lgw.inc(g.dma_start(out=wo[:, ci, :],
                                in_=cx.dram["L0_ffn_Wo"][layer, r, :]))
        ldr.inc(q.dma_start(out=qsrow[0:1, 0:64],
                            in_=cx.dram["L0_ffn_q_scale"][layer:layer+1, :]))
        ldr.inc(q.dma_start(out=qsrow[0:1, 64:128],
                            in_=cx.dram["L0_ffn_k_scale"][layer:layer+1, :]))
        d = v.memset(tbsc["halfpi"][:], float(np.pi / 2))
        cv.inc(d)
        s.wait(cv)
        g.wait(cv)
        _emit_rope_tables(cx, (cosS, sinS, cosI, sinI), tbsc, pq[0], pq[1])
        t_.wait(ldr)
        t_.wait(cv)
        t_.wait(lgw)
        t_.matmul(pq[0][:, 0:64], cx.sb["ones_row"][0:1, :], qsrow[0:1, 0:64],
                  start=True, stop=True)
        d = t_.matmul(pq[0][:, 64:128], cx.sb["ones_row"][0:1, :],
                      qsrow[0:1, 64:128], start=True, stop=True)
        c_qs.inc(d)
        v.wait(c_qs)
        d = v.tensor_copy(qs_b[:], pq[0][:, 0:128])
        c_qs.inc(d)
        v.wait(c_qs)
        d = v.tensor_scalar(qs_b[:, 0:64], qs_b[:, 0:64], QK_SCALE, None,
                            op0=ALU.mult)
        c_qs.inc(d)

        gq = 0
        gt = 0
        ug_g = 0
        mm0 = c_mm.n
        ev0 = [c_ev[0].n, c_ev[1].n]
        rp0, rpg0, tp20 = c_rp.n, c_rpg.n, c_tp2.n
        sim20, red20, exp20, avc0, yv0 = (c_sim2.n, c_red2.n, c_exp2.n,
                                          c_avc.n, c_yv.n)
        wo0, zo0 = c_wo.n, c_zo.n
        ptev0 = [c_ptev[0].n, c_ptev[1].n]
        zsrc = cx.dram["xn_cm"] if layer == 0 else cx.dram["z_cm"]
        ldm, lgm = {}, {}
        tpP = {}
        ytpm = {}
        simmark = {}
        avmark = {}
        yemark = {}
        womark = {}
        rpmark = {}
        tpqmark = {}
        for w in range(NW + 1):
            buf = w % 2
            t0 = W * w
            if w < NW:
                if w >= 2:
                    q.wait(c_mm, mm0 + 12 * (w - 1))
                    q.wait(c_zo, womark.get(w - 2, zo0))
                    g.wait(c_mm, mm0 + 12 * (w - 1))
                    g.wait(cx.fine[24], cx.fine[24].n - 1)
                for ci in range(NCH):
                    ldz[buf].inc(q.dma_start(
                        out=zf[:, buf, ci, :],
                        in_=zsrc[128*ci:128*(ci+1), t0:t0 + W]))
                czb = cx.fine[11]
                g.wait_ge(ldz[buf].sem, ldz[buf].n)
                for ci in range(NCH):
                    d = g.tensor_copy(zb[:, buf, ci, 1:257], zf[:, buf, ci, :])
                    czb.inc(d)
                g.wait(czb)
                if w == 0:
                    d = g.memset(zb[:, 0, :, 0:1], 0.0)
                else:
                    d = g.tensor_copy(zb[:, buf, :, 0:1], zb[:, 1 - buf, :, 256:257])
                cx.fine[24].inc(d)
                ldm[w], lgm[w] = ldz[buf].n, czb.n
            if w >= 1:
                pw = w - 1
                pbuf = pw % 2
                for ob in range(NCH):
                    osl = ob % 2
                    t_.wait(c_ptev[0], yemark[pw][0])
                    t_.wait(c_ptev[1], yemark[pw][1])
                    if ob >= 2:
                        t_.wait(c_zo, c_zo.n - 1)
                    for ci in range(NCH):
                        d = t_.matmul(psim[osl][:, 0:256],
                                      wo[:, ci, 128*ob:128*(ob+1)],
                                      ycm_w[:, pbuf, ci, :],
                                      start=(ci == 0), stop=(ci == NCH - 1))
                    c_wo.inc(d)
                    v.wait(c_wo)
                    if pw >= 1 or ob >= 2:
                        v.wait(spz)
                    d = v.scalar_tensor_tensor(zout[:, osl, :], psim[osl][:, 0:256],
                                               1.0, zf[:, pbuf, ob, :],
                                               op0=ALU.mult, op1=ALU.add)
                    c_zo.inc(d)
                    q.wait(c_zo)
                    spz.inc(q.dma_start(
                        out=cx.dram["z_cm"][128*ob:128*(ob+1), W*pw:W*(pw+1)],
                        in_=zout[:, osl, :]))
                womark[pw] = c_zo.n
            if w >= NW:
                continue
            t_.wait_ge(ldz[buf].sem, ldm[w])
            t_.wait(cx.fine[11], lgm[w])
            t_.wait(c_qs)
            t_.wait(cx.fine[24])
            for tt in range(2):
                for nb in range(6):
                    slot = gq % 2
                    if gq >= 2:
                        t_.wait(c_ev[slot], ev0[slot] + gq // 2)
                    for ci in range(NCH):
                        off = (1 if ci < 4 else 0) + 128 * tt
                        d = t_.matmul(pq[slot][:],
                                      zb[:, buf, ci, off:off + 128],
                                      wqkv[:, ci, 512*nb:512*(nb+1)],
                                      start=(ci == 0), stop=(ci == NCH - 1))
                    c_mm.inc(d)
                    eng = v if slot == 0 else s
                    eng.wait(c_mm)
                    if nb < 4:
                        pm = rpmark.get((w, tt - 1)) or rpmark.get((w - 1, 1))
                        if pm:
                            eng.wait(c_rp, pm[0])
                            eng.wait(c_rpg, pm[1])
                        if eng is s:
                            d = s.activation(qk_tm[:, 512*nb:512*(nb+1)],
                                             pq[slot][:], AF.Copy)
                        else:
                            d = v.tensor_copy(qk_tm[:, 512*nb:512*(nb+1)],
                                              pq[slot][:])
                    else:
                        if w >= 1:
                            eng.wait(c_avc, avmark[w - 1])
                        if eng is s:
                            d = s.activation(vring[:, buf, tt, 512*(nb-4):512*(nb-3)],
                                             pq[slot][:], AF.Copy)
                        else:
                            d = v.tensor_copy(vring[:, buf, tt, 512*(nb-4):512*(nb-3)],
                                              pq[slot][:])
                    c_ev[slot].inc(d)
                    gq += 1
                s.wait(c_ev[0], c_ev[0].n)
                s.wait(c_ev[1], c_ev[1].n)
                pm = rpmark.get((w, tt - 1)) or rpmark.get((w - 1, 1))
                if pm:
                    s.wait(c_rp, pm[0])
                for qk in range(2):
                    if qk == 1:
                        s.wait(cv)
                    d = s.activation(qsq[:, 0:1024],
                                     qk_tm[:, 1024*qk:1024*(qk+1)], AF.Square)
                    c_rp.inc(d)
                    v.wait(c_rp)
                    d = v.reduce_sum(rss[:, tt, 16*qk:16*(qk+1)],
                                     qsq.ap().rearrange("p (h j) -> p h j", h=Hh, j=64),
                                     axis=AX.X)
                    cv.inc(d)
                    v.wait(cv)
                    d = v.tensor_scalar_max(rss[:, tt, 16*qk:16*(qk+1)],
                                            rss[:, tt, 16*qk:16*(qk+1)], 1e-24)
                    c_rp.inc(d)
                s.wait(c_rp)
                d = s.activation(rss[:, tt, :], rss[:, tt, :], AF.Ln)
                cx.fine[25].inc(d)
                s.wait(cx.fine[25])
                d = s.activation(rss[:, tt, :], rss[:, tt, :], AF.Exp, scale=-0.5)
                c_rp.inc(d)
                rstd_mark = c_rp.n
                tile = 2 * w + tt
                v.wait(c_rp, rstd_mark)
                for hg in range(2):
                    d = _emit_rope_apply(cx, v, cv,
                                         qk_tm.ap()[:, 512*hg:512*(hg+1)],
                                         qro.ap()[:, tt, 512*hg:512*(hg+1)],
                                         rss.ap()[:, tt, 8*hg:8*(hg+1)],
                                         qs_b.ap()[:, 0:64],
                                         cosS[:, 64*tile:64*(tile+1)],
                                         sinS[:, 64*tile:64*(tile+1)], ropet_v,
                                         prior=(c_rp, c_rp.n))
                    c_rp.inc(d)
                g.wait(c_rp, rstd_mark)
                for hg in range(2):
                    d = _emit_rope_apply(cx, g, cx.fine[24],
                                         qk_tm.ap()[:, 1024 + 512*hg:1024 + 512*(hg+1)],
                                         qro.ap()[:, tt, 1024 + 512*hg:1024 + 512*(hg+1)],
                                         rss.ap()[:, tt, 16 + 8*hg:16 + 8*(hg+1)],
                                         qs_b.ap()[:, 64:128],
                                         cosI[:, 64*tile:64*(tile+1)],
                                         sinI[:, 64*tile:64*(tile+1)], ropet_g,
                                         prior=(c_rpg, c_rpg.n))
                    c_rpg.inc(d)
                rpmark[(w, tt)] = (c_rp.n, c_rpg.n)
                for qk in range(2):
                    gpar = gt % 2
                    t_.wait(c_rp if qk == 0 else c_rpg)
                    if gt >= 2:
                        t_.wait(c_ptev[gpar], ptev0[gpar] + gt // 2)
                    for hh in range(Hh):
                        d = t_.transpose(
                            ptp[gpar][(hh % 2)*64:(hh % 2)*64+64,
                                      128*(hh//2):128*(hh//2)+128],
                            qro[:, tt, 1024*qk + 64*hh:1024*qk + 64*(hh+1)],
                            idb[:])
                    c_tp2.inc(d)
                    eng = v if gpar == 0 else s
                    eng.wait(c_tp2)
                    dst = (qcm.ap()[:, :, 128*tt:128*(tt+1)] if qk == 0 else
                           kcm.ap()[:, buf, :, 128*tt:128*(tt+1)])
                    src3 = ptp[gpar][:, :].rearrange("p (a b) -> p a b", a=NCH, b=128)
                    if eng is s:
                        d = s.activation(dst, src3, AF.Copy)
                    else:
                        d = v.tensor_copy(dst, src3)
                    c_ptev[gpar].inc(d)
                    gt += 1
            tpqmark[w] = (c_ptev[0].n, c_ptev[1].n)
            for h in range(Hh):
                pb = (h % 2) * 64
                hp = h // 2
                for tqt in range(2):
                    upar = ug_g % 2
                    sl = ug_g % 2
                    if ug_g >= 2:
                        t_.wait(c_exp2, exp20 + ug_g - 1)
                    if h == 0 and tqt == 0:
                        t_.wait(c_ptev[0], tpqmark[w][0])
                        t_.wait(c_ptev[1], tpqmark[w][1])
                        if w >= 1:
                            t_.wait(c_zo, womark[w - 1])
                    mki = (2 + tqt) if w == 0 else tqt
                    kw = 256 if w == 0 else 512
                    t_.matmul(psim[sl][:, 0:kw], idb[:], mk[:, mki, 0:kw],
                              start=True, stop=False)
                    if w == 0:
                        d = t_.matmul(psim[sl][:, 0:256],
                                      qcm[pb:pb+64, hp, 128*tqt:128*(tqt+1)],
                                      kcm[pb:pb+64, buf, hp, :],
                                      start=False, stop=True)
                    else:
                        t_.matmul(psim[sl][:, 0:256],
                                  qcm[pb:pb+64, hp, 128*tqt:128*(tqt+1)],
                                  kcm[pb:pb+64, 1 - buf, hp, :],
                                  start=False, stop=False)
                        d = t_.matmul(psim[sl][:, 256:512],
                                      qcm[pb:pb+64, hp, 128*tqt:128*(tqt+1)],
                                      kcm[pb:pb+64, buf, hp, :],
                                      start=False, stop=True)
                    c_sim2.inc(d)
                    v.wait(c_sim2, sim20 + ug_g + 1)
                    d = v.reduce_max(lrow[:, upar, 0:1], psim[sl][:, 0:kw],
                                     axis=AX.X, negate=True)
                    c_red2.inc(d)
                    s.wait(c_red2, red20 + ug_g + 1)
                    if ug_g >= 2:
                        s.wait(c_tp2, tpP[ug_g - 2])
                        s.wait(c_yv, yv0 + ug_g - 1)
                    d = s.activation(P_sb[:, upar, 0:kw], psim[sl][:, 0:kw],
                                     AF.Exp, bias=lrow[:, upar, 0:1],
                                     accum_out=lrow[:, upar, 1:2])
                    c_exp2.inc(d)
                    v.wait(c_exp2, exp20 + ug_g + 1)
                    d = v.reciprocal(lrow[:, upar, 2:3], lrow[:, upar, 1:2])
                    cv.inc(d)
                    v.wait(cv)
                    gpar = gt % 2
                    if gt >= 2:
                        t_.wait(c_ptev[gpar], ptev0[gpar] + gt // 2)
                    t_.wait(c_exp2, exp20 + ug_g + 1)
                    ntp = kw // 128
                    for jt in range(ntp):
                        d = t_.transpose(ptp[gpar][:, 128*jt:128*(jt+1)],
                                         P_sb[:, upar, 128*jt:128*(jt+1)], idb[:])
                    c_tp2.inc(d)
                    tpP[ug_g] = c_tp2.n
                    eng = v if gpar == 0 else s
                    eng.wait(c_tp2)
                    if eng is s:
                        d = s.activation(PT_sb[:, upar, 0:kw], ptp[gpar][:, 0:kw],
                                         AF.Copy)
                    else:
                        d = v.tensor_copy(PT_sb[:, upar, 0:kw], ptp[gpar][:, 0:kw])
                    c_ptev[gpar].inc(d)
                    gt += 1
                    t_.wait(c_ptev[(gt - 1) % 2], ptev0[(gt - 1) % 2] + (gt - 1) // 2 + 1)
                    if ug_g >= 2:
                        t_.wait(c_yv, yv0 + ug_g - 1)
                    for jt in range(ntp):
                        if w == 0:
                            wp, vtt = buf, jt
                        else:
                            wp, vtt = (1 - buf, jt) if jt < 2 else (buf, jt - 2)
                        d = t_.matmul(pav[sl][:, 0:D],
                                      PT_sb[:, upar, 128*jt:128*(jt+1)],
                                      vring[:, wp, vtt, 64*h:64*(h+1)],
                                      start=(jt == 0), stop=(jt == ntp - 1))
                    c_avc.inc(d)
                    v.wait(c_avc, avc0 + ug_g + 1)
                    if ug_g >= 2:
                        v.wait(c_tp2, ytpm[ug_g - 2])
                    d = v.tensor_scalar(ytmp[:, upar, :], pav[sl][:, 0:D],
                                        lrow[:, upar, 2:3], None, op0=ALU.mult)
                    c_yv.inc(d)
                    gpar = gt % 2
                    if gt >= 2:
                        t_.wait(c_ptev[gpar], ptev0[gpar] + gt // 2)
                    t_.wait(c_yv, yv0 + ug_g + 1)
                    d = t_.transpose(ptp[gpar][0:D, 0:128], ytmp[:, upar, :], idb[:])
                    c_tp2.inc(d)
                    ytpm[ug_g] = c_tp2.n
                    eng = v if gpar == 0 else s
                    eng.wait(c_tp2)
                    if w >= 2:
                        eng.wait(c_zo, womark.get(w - 1, zo0))
                    d_ = (s.activation(ycm_w[pb:pb+64, buf, hp, 128*tqt:128*(tqt+1)],
                                       ptp[gpar][0:D, 0:128], AF.Copy)
                          if eng is s else
                          v.tensor_copy(ycm_w[pb:pb+64, buf, hp, 128*tqt:128*(tqt+1)],
                                        ptp[gpar][0:D, 0:128]))
                    c_ptev[gpar].inc(d_)
                    gt += 1
                    ug_g += 1
            simmark[w] = c_sim2.n
            avmark[w] = c_avc.n
            yemark[w] = (c_ptev[0].n, c_ptev[1].n)
        _ = simmark


def _emit_c_ffn_sweep(cx: Ctx, layer):
    t_, v, s, g, q = cx.t, cx.v, cx.s, cx.g, cx.q
    ld, lg, sp = cx.ld, cx.lg, cx.sp
    import os
    zsrc2 = (cx.dram["xn_cm"] if (layer == 0 and os.environ.get("C_FFN_SRC_XN"))
             else cx.dram["z_cm"])
    NJA = 22
    with cx.scope():
        gpp = cx.sbuf("cf_gpp", [128, NCH], F32)
        bpp = cx.sbuf("cf_bpp", [128, NCH], F32)
        b2pp = cx.sbuf("cf_b2pp", [128, NCH], F32)
        b1app = cx.sbuf("cf_b1app", [128, NJA], F32)
        b1gpp = cx.sbuf("cf_b1gpp", [128, NJA], F32)
        pa = [cx.psum(f"cf_pa{i}", [128, 512]) for i in range(2)]
        pgt = [cx.psum(f"cf_pg{i}", [128, 512]) for i in range(2)]
        pst = [cx.psum(f"cf_pst{i}", [1, 512]) for i in range(2)]
        pb = [cx.psum(f"cf_pb{i}", [128, 512]) for i in range(2)]
        cv = cx.fine[4]
        c_pp = cx.fine[5]
        c_sq = cx.fine[6]
        c_st = cx.fine[7]
        c_var = cx.fine[8]
        c_rstd = cx.fine[9]
        c_mr = cx.fine[10]
        c_bc = cx.fine[11]
        c_ap2 = cx.fine[12]
        c_w1 = cx.fine[13]
        c_gel = cx.fine[14]
        c_hh = cx.fine[15]
        c_w2 = cx.fine[16]
        c_w2e = cx.fine[17]
        eps11 = cx.sb["eps11"]
        with cx.scope():
            rows = cx.sbuf("cf_rows", [1, 2 * C + 5460], F32)
            b2row = cx.sbuf("cf_b2row", [1, C], F32)
            ldr = cx.dmac[9]
            ldr.inc(q.dma_start(out=rows[0:1, 0:C],
                                in_=cx.dram["L0_ffn_ln_g"][layer:layer+1, :]))
            ldr.inc(q.dma_start(out=rows[0:1, C:2*C],
                                in_=cx.dram["L0_ffn_ln_b"][layer:layer+1, :]))
            ldr.inc(q.dma_start(out=rows[0:1, 2*C:2*C+5460],
                                in_=cx.dram["L0_ffn_b1"][layer:layer+1, :]))
            ldr.inc(q.dma_start(out=b2row[0:1, :],
                                in_=cx.dram["L0_ffn_b2"][layer:layer+1, :]))
            t_.wait(ldr)
            emit_pp_from_row(cx, rows[:, 0:C], C, gpp, pb[0], c_pp)
            emit_pp_from_row(cx, rows[:, C:2*C], C, bpp, pb[0], c_pp)
            emit_pp_from_row(cx, b2row[:, 0:C], C, b2pp, pb[0], c_pp)
            emit_pp_from_row(cx, rows[:, 2*C:2*C+2730], 2730, b1app, pb[0], c_pp)
            emit_pp_from_row(cx, rows[:, 2*C+2730:2*C+5460], 2730, b1gpp, pb[0],
                             c_pp)
        s.wait(c_pp)
        g.wait(c_pp)
        q.wait(c_pp)
        w1 = cx.sbuf("cf_w1", [128, NCH, 5460], BF16)
        w2 = cx.sbuf("cf_w2", [128, NJA, C], BF16)
        zf2 = cx.sbuf("cf_zf", [128, NCH, 513], F32)
        hcol = cx.sbuf("cf_hcol", [128, NCH], F32)
        zsq = cx.sbuf("cf_zsq", [128, 2, 512], F32)
        sm = cx.sbuf("cf_sm", [1, 2048], F32)
        xn2 = cx.sbuf("cf_xn2", [128, NCH, 512], BF16)
        gel = cx.sbuf("cf_gel", [128, 2, 512], F32)
        h_sb = cx.sbuf("cf_h", [128, NJA, 512], BF16)
        zo2 = cx.sbuf("cf_zo2", [128, 2, 512], F32)
        lgw = cx.dmac[14]
        ldz2 = cx.dmac[11]
        spf = cx.dmac[13]
        for ci in range(NCH):
            r = slice(128 * ci, 128 * (ci + 1))
            lgw.inc(g.dma_start(out=w1[:, ci, :],
                                in_=cx.dram["L0_ffn_W1"][layer, r, :]))
        for jc in range(NJA):
            cnt = 128 if jc < 21 else 42
            lgw.inc(g.dma_start(out=w2[0:cnt, jc, :],
                                in_=cx.dram["L0_ffn_W2"][layer,
                                                         128*jc:128*jc+cnt, :]))
        t_.wait(lgw)
        d = v.memset(zf2[:, :, 0:1], 0.0)
        cv.inc(d)
        q.wait(cv)
        sq0, st0, w10, gel0, hh0, w20, w2e0 = (c_sq.n, c_st.n, c_w1.n, c_gel.n,
                                               c_hh.n, c_w2.n, c_w2e.n)
        ap20 = c_ap2.n
        gw = 0
        go = 0
        for tc in range(NT):
            t0 = 512 * tc
            c_hc = cx.fine[18]
            if tc >= 1:
                d = v.tensor_copy(hcol[:, :], zf2[:, :, 512:513].rearrange(
                    "p a o -> p (a o)"))
                c_hc.inc(d)
                q.wait(c_hc)
                q.wait(c_w2e, w2e0 + NCH * tc)
                q.wait(c_ap2, ap20 + NCH * tc)
            for ci in range(NCH):
                ldz2.inc(q.dma_start(out=zf2[:, ci, 1:513],
                                     in_=zsrc2[128*ci:128*(ci+1), t0:t0 + 512]))
            if tc >= 1:
                v.wait(ldz2)
                d = v.tensor_copy(zf2[:, :, 0:1].rearrange("p a o -> p (a o)"),
                                  hcol[:, :])
                c_hc.inc(d)
            s.wait(ldz2)
            s.wait(c_hc)
            t_.wait(ldz2)
            t_.wait(c_hc)
            if tc >= 1:
                t_.wait(c_var, c_var.n)
            for ci in range(NCH):
                off = 1 if ci < 4 else 0
                t_.matmul(pst[0][:], cx.sb["ones_col"][:, 0:1],
                          zf2[:, ci, off:off+512], start=(ci == 0),
                          stop=(ci == NCH - 1))
            for ci in range(NCH):
                off = 1 if ci < 4 else 0
                if c_sq.n - sq0 >= 2:
                    s.wait(c_st, c_st.n - 1)
                d = s.activation(zsq[:, ci % 2, :], zf2[:, ci, off:off+512],
                                 AF.Square)
                c_sq.inc(d)
                t_.wait(c_sq)
                d = t_.matmul(pst[1][:], cx.sb["ones_col"][:, 0:1],
                              zsq[:, ci % 2, :], start=(ci == 0),
                              stop=(ci == NCH - 1))
                c_st.inc(d)
            v.wait(c_st)
            d = v.tensor_scalar(sm[:, 0:512], pst[0][:], 1.0 / C, None, op0=ALU.mult)
            cv.inc(d)
            d = v.tensor_scalar(sm[:, 512:1024], pst[1][:], 1.0 / C, None,
                                op0=ALU.mult)
            cv.inc(d)
            v.wait(cv)
            d = v.tensor_mul(sm[:, 1024:1536], sm[:, 0:512], sm[:, 0:512])
            cv.inc(d)
            v.wait(cv)
            d = v.tensor_sub(sm[:, 512:1024], sm[:, 512:1024], sm[:, 1024:1536])
            c_var.inc(d)
            s.wait(c_var)
            d = s.activation(sm[:, 1024:1536], sm[:, 512:1024], AF.Ln,
                             bias=eps11[0:1, 0:1])
            cx.fine[25].inc(d)
            s.wait(cx.fine[25])
            d = s.activation(sm[:, 1024:1536], sm[:, 1024:1536], AF.Exp, scale=-0.5)
            c_rstd.inc(d)
            v.wait(c_rstd)
            d = v.scalar_tensor_tensor(sm[:, 1536:2048], sm[:, 0:512], -1.0,
                                       sm[:, 1024:1536], op0=ALU.mult, op1=ALU.mult)
            c_mr.inc(d)
            t_.wait(c_mr)
            if tc >= 1:
                t_.wait(c_ap2, ap20 + NCH * tc)
            t_.matmul(pb[0][:], cx.sb["ones_row"][0:1, :], sm[:, 1024:1536],
                      start=True, stop=True)
            d = t_.matmul(pb[1][:], cx.sb["ones_row"][0:1, :], sm[:, 1536:2048],
                          start=True, stop=True)
            c_bc.inc(d)
            v.wait(c_bc)
            if tc >= 1:
                v.wait(c_w1, w10 + NJA * tc)
            for ci in range(NCH):
                off = 1 if ci < 4 else 0
                tvv = zsq[:, 0, :]
                d = v.tensor_mul(tvv, zf2[:, ci, off:off+512], pb[0][:])
                cv.inc(d)
                v.wait(cv)
                d = v.tensor_add(tvv, tvv, pb[1][:])
                cv.inc(d)
                v.wait(cv)
                d = v.tensor_scalar(xn2[:, ci, :], tvv, gpp[:, ci:ci+1],
                                    bpp[:, ci:ci+1], op0=ALU.mult, op1=ALU.add)
                if ci < NCH - 1:
                    cv.inc(d)
                    v.wait(cv)
            c_ap2.inc(d, NCH)
            t_.wait(c_ap2)
            for jc in range(NJA):
                cnt = 128 if jc < 21 else 42
                slw = gw % 2
                if gw >= 2:
                    t_.wait(c_gel, gel0 + gw - 1)
                    t_.wait(c_hh, hh0 + gw - 1)
                for ci in range(NCH):
                    t_.matmul(pa[slw][0:cnt, :], w1[:, ci, 128*jc:128*jc+cnt],
                              xn2[:, ci, :], start=(ci == 0), stop=(ci == NCH - 1))
                for ci in range(NCH):
                    d = t_.matmul(pgt[slw][0:cnt, :],
                                  w1[:, ci, 2730 + 128*jc:2730 + 128*jc + cnt],
                                  xn2[:, ci, :], start=(ci == 0),
                                  stop=(ci == NCH - 1))
                c_w1.inc(d)
                s.wait(c_w1)
                if gw >= 2:
                    s.wait(c_hh, hh0 + gw - 1)
                d = s.activation(gel[0:cnt, slw, :], pgt[slw][0:cnt, :], AF.Gelu,
                                 bias=b1gpp[0:cnt, jc:jc+1])
                c_gel.inc(d)
                v.wait(c_gel)
                if tc >= 1 and jc == 0:
                    v.wait(c_w2, w20 + NCH * tc)
                d = v.scalar_tensor_tensor(h_sb[0:cnt, jc, :], pa[slw][0:cnt, :],
                                           b1app[0:cnt, jc:jc+1], gel[0:cnt, slw, :],
                                           op0=ALU.add, op1=ALU.mult)
                c_hh.inc(d)
                gw += 1
            for ob in range(NCH):
                so = go % 2
                t_.wait(c_hh, hh0 + NJA * (tc + 1))
                if go >= 2:
                    t_.wait(c_w2e, w2e0 + go - 1)
                for jc in range(NJA):
                    cnt = 128 if jc < 21 else 42
                    d = t_.matmul(pgt[so][:],
                                  w2[0:cnt, jc, 128*ob:128*(ob+1)],
                                  h_sb[0:cnt, jc, :], start=(jc == 0),
                                  stop=(jc == NJA - 1))
                c_w2.inc(d)
                v.wait(c_w2)
                if go >= 2:
                    v.wait(spf)
                d = v.scalar_tensor_tensor(zo2[:, so, :], pgt[so][:],
                                           b2pp[:, ob:ob+1], zf2[:, ob, 1:513],
                                           op0=ALU.add, op1=ALU.add)
                c_w2e.inc(d)
                q.wait(c_w2e)
                spf.inc(q.dma_start(out=cx.dram["z_cm"][128*ob:128*(ob+1),
                                                        t0:t0+512],
                                    in_=zo2[:, so, :]))
                go += 1



def _emit_l1_proj(cx: Ctx, idx, tabs):
    t_, v, s, g, q = cx.t, cx.v, cx.s, cx.g, cx.q
    GM = 256
    GAM = (0.5, 1.0, 2.0)
    cT, cV, cS, cG = cx.fine[20], cx.fine[21], cx.fine[22], cx.fine[23]
    with cx.scope():
        ckw = cx.sbuf("f_ckw", [128, 6, C], BF16)
        rfw = cx.sbuf("f_rfw", [128, NCH, 3 * GM], BF16)
        pjw = cx.sbuf("f_pjw", [128, 6, C], BF16)
        brow = cx.sbuf("f_brow", [1, C + 3 * GM + C], F32)
        cbpp = cx.sbuf("f_cbpp", [128, NCH], F32)
        rbpp = cx.sbuf("f_rbpp", [128, 6], F32)
        rb2pp = cx.sbuf("f_rb2pp", [128, 6], F32)
        pbrow_bf = cx.sbuf("f_pbbf", [1, C], BF16)
        xin = cx.sbuf("f_xin", [128, 2, NCH, 514], BF16)
        hsb = cx.sbuf("f_hsb", [128, NCH, 512], BF16)
        zsb = cx.sbuf("f_zsb", [128, 6, 512], BF16)
        wk2 = cx.sbuf("f_wk2", [128, 4, 512], F32)
        otm = cx.sbuf("f_otm", [128, 2, 1024], F32)
        obf = cx.sbuf("f_obf", [128, 2, 1024], BF16)
        n2t = cx.sbuf("f_n2t", [128, 16], F32)
        n2bf = cx.sbuf("f_n2bf", [128, 16], BF16)
        ocm = cx.sbuf("f_ocm", [128, 2, 1024], BF16)
        n2cm = cx.sbuf("f_n2cm", [16, 2, 128], F32)
        ptmp = cx.sbuf("f_ptmp", [128, 3, 512], F32)
        pcm = [cx.psum(f"f_pcm{i}", [128, 512]) for i in range(2)]
        ptm = [cx.psum(f"f_ptm{i}", [128, 512]) for i in range(2)]
        ptpn = cx.psum("f_ptpn", [128, 1024], BF16)
        pnr = cx.psum("f_pnr", [128, 512])
        lgw = cx.dmac[14]
        ldr = cx.dmac[9]
        ldx = [cx.dmac[15], cx.dmac[16]]
        spq = cx.dmac[13]
        c_pp = cx.fine[5]
        for cc2 in range(2):
            for w in range(3):
                lgw.inc(g.dma_start(
                    out=ckw[:, cc2 * 3 + w, :],
                    in_=cx.dram["L1_attn_conv_k"][idx, w, 128*cc2:128*(cc2+1), :]))
        for ci in range(NCH):
            r = slice(128 * ci, 128 * (ci + 1))
            for gi2 in range(3):
                lgw.inc(g.dma_start(
                    out=rfw[:, ci, GM*gi2:GM*(gi2+1)],
                    in_=cx.dram["L1_attn_rff_w"][idx, gi2, r, :]))
        for fi in range(6):
            lgw.inc(g.dma_start(
                out=pjw[:, fi, :],
                in_=cx.dram["L1_attn_proj_w"][idx, 128*fi:128*(fi+1), :]))
        ldr.inc(q.dma_start(out=brow[0:1, 0:C],
                            in_=cx.dram["L1_attn_conv_b"][idx:idx+1, :]))
        ldr.inc(q.dma_start(
            out=brow[0:1, C:C+3*GM],
            in_=cx.dram["L1_attn_rff_b"][idx, :, :].rearrange(
                "g m -> (g m)").unsqueeze(0)))
        ldr.inc(q.dma_start(out=brow[0:1, C+3*GM:C+3*GM+C],
                            in_=cx.dram["L1_attn_proj_b"][idx:idx+1, :]))
        t_.wait(ldr)
        t_.wait(lgw)
        emit_pp_from_row(cx, brow[:, 0:C], C, cbpp, pnr, c_pp)
        emit_pp_from_row(cx, brow[:, C:C+3*GM], 3 * GM, rbpp, pnr, c_pp)
        v.wait(c_pp)
        cV.inc(v.tensor_copy(pbrow_bf[:], brow[:, C+3*GM:C+3*GM+C]))
        v.wait(cV)
        cV.inc(v.tensor_scalar(rb2pp[:, :], rbpp[:, :], float(1.0 / (2 * np.pi)),
                               0.25, op0=ALU.mult, op1=ALU.add))
        g.wait(c_pp)
        g.wait(lgw)
        cG.inc(g.tensor_scalar(pjw.ap().rearrange("p a b -> p (a b)"),
                               pjw.ap().rearrange("p a b -> p (a b)"),
                               float(np.sqrt(2.0 / GM)), None, op0=ALU.mult))
        t_.wait(cV)
        t_.wait(cG)
        s.wait(cV)

        SP = 512
        for sp_i in range(T // SP):
            buf = sp_i % 2
            t0 = SP * sp_i
            g.wait(cT)
            lo = max(t0 - 1, 0)
            hi = min(t0 + SP + 1, T)
            woff = 1 - (t0 - lo)
            for ci in range(NCH):
                ldx[buf].inc(g.dma_start(
                    out=xin[:, buf, ci, woff:woff + (hi - lo)],
                    in_=cx.dram["xn1_cm"][128*ci:128*(ci+1), lo:hi]))
            v.wait(ldx[buf])
            if sp_i == 0:
                cV.inc(v.memset(xin[:, 0, :, 0:1], 0.0))
            if sp_i == T // SP - 1:
                cV.inc(v.memset(xin[:, buf, :, 513:514], 0.0))
            t_.wait_ge(ldx[buf].sem, ldx[buf].n)
            t_.wait(cV)
            t_.wait(cS)
            for ob in range(NCH):
                slot = ob % 2
                grp = ob // 2
                first = True
                for cc2 in range(2):
                    for w in range(3):
                        d = t_.matmul(pcm[slot][:],
                                      ckw[:, cc2 * 3 + w, 128*ob:128*(ob+1)],
                                      xin[:, buf, grp * 2 + cc2, w:w + 512],
                                      start=first, stop=(cc2 == 1 and w == 2))
                        first = False
                cT.inc(d)
                s.wait(cT)
                d = s.activation(hsb[:, ob, :], pcm[slot][:], AF.Identity,
                                 bias=cbpp[:, ob:ob+1])
                cS.inc(d)
                if ob >= 1:
                    t_.wait(cS, cS.n - 1)
            t_.wait(cS)
            for mcol in range(6):
                gi2, mc = mcol // 2, mcol % 2
                sgam = float(np.sqrt(2.0 * GAM[gi2]))
                slot = mcol % 2
                if mcol >= 2:
                    t_.wait(cS, cS.n)
                for ci in range(NCH):
                    d = t_.matmul(pcm[slot][:],
                                  rfw[:, ci, GM*gi2 + 128*mc:GM*gi2 + 128*(mc+1)],
                                  hsb[:, ci, :], start=(ci == 0),
                                  stop=(ci == NCH - 1))
                cT.inc(d)
                v.wait(cT)
                v.wait(cS, cS.n)
                cV.inc(v.tensor_scalar(wk2[:, 0, :], pcm[slot][:],
                                       float(sgam / (2 * np.pi)),
                                       rb2pp[:, mcol:mcol+1],
                                       op0=ALU.mult, op1=ALU.add))
                v.wait(cV)
                cV.inc(v.tensor_scalar(wk2[:, 0, :], wk2[:, 0, :], MAGIC, None,
                                       op0=ALU.add))
                v.wait(cV)
                cV.inc(v.tensor_scalar(wk2[:, 1, :], wk2[:, 0, :], MAGIC, -TPI_HI,
                                       op0=ALU.subtract, op1=ALU.mult))
                cV.inc(v.tensor_scalar(wk2[:, 2, :], wk2[:, 0, :], MAGIC, -TPI_LO,
                                       op0=ALU.subtract, op1=ALU.mult))
                v.wait(cV)
                cV.inc(v.tensor_scalar(wk2[:, 3, :], pcm[slot][:], sgam,
                                       rbpp[:, mcol:mcol+1],
                                       op0=ALU.mult, op1=ALU.add))
                v.wait(cV)
                cV.inc(v.tensor_add(wk2[:, 3, :], wk2[:, 3, :], wk2[:, 1, :]))
                v.wait(cV)
                cV.inc(v.scalar_tensor_tensor(wk2[:, 3, :], wk2[:, 3, :],
                                              float(np.pi / 2), wk2[:, 2, :],
                                              op0=ALU.add, op1=ALU.add))
                v.wait(cV)
                cV.inc(v.tensor_scalar(wk2[:, 3, :], wk2[:, 3, :],
                                       -float(np.pi), float(np.pi),
                                       op0=ALU.max, op1=ALU.min))
                s.wait(cV)
                d = s.activation(zsb[:, mcol, :], wk2[:, 3, :], AF.Sin)
                cS.inc(d)
            for tt in range(4):
                tb2 = tt % 2
                tile = sp_i * 4 + tt
                for oc in range(2):
                    slot = oc
                    t_.wait(cS)
                    t_.wait(cV, cV.n)
                    t_.matmul(ptm[slot][:], cx.sb["ones_row_bf"][0:1, :],
                              pbrow_bf[0:1, 512*oc:512*(oc+1)],
                              start=True, stop=False)
                    for fi in range(6):
                        d = t_.matmul(ptm[slot][:],
                                      zsb[:, fi, 128*tt:128*(tt+1)],
                                      pjw[:, fi, 512*oc:512*(oc+1)],
                                      start=False, stop=(fi == 5))
                    cT.inc(d)
                    v.wait(cT)
                    cV.inc(v.tensor_copy(otm[:, tb2, 512*oc:512*(oc+1)],
                                         ptm[slot][:]))
                if idx == 2:
                    v.wait(cV)
                    if tile >= 2:
                        v.wait(spq)
                    cV.inc(v.tensor_copy(obf[:, tb2, :], otm[:, tb2, :]))
                    q.wait(cV)
                    for h in range(16):
                        spq.inc(q.dma_start(
                            out=cx.dram["v1_tm"][h, 128*tile:128*(tile+1), :],
                            in_=obf[:, tb2, 64*h:64*(h+1)]))
                else:
                    v.wait(cV)
                    if tile >= 2:
                        v.wait(spq)
                    for hg in range(2):
                        d = _emit_rope_apply(
                            cx, v, cV,
                            otm.ap()[:, tb2, 512*hg:512*(hg+1)],
                            obf.ap()[:, tb2, 512*hg:512*(hg+1)],
                            None, None,
                            tabs[0][:, 64*tile:64*(tile+1)],
                            tabs[1][:, 64*tile:64*(tile+1)], ptmp)
                        cV.inc(d)
                    s.wait(cV)
                    cS.inc(s.activation(ptmp[:, 0, :], obf[:, tb2, 0:512],
                                        AF.Square))
                    cS.inc(s.activation(ptmp[:, 1, :], obf[:, tb2, 512:1024],
                                        AF.Square))
                    v.wait(cS)
                    cV.inc(v.reduce_sum(
                        n2t[:, 0:16],
                        ptmp.ap()[:, 0:2, :].rearrange(
                            "p a (h j) -> p (a h) j", h=8, j=64),
                        axis=AX.X, negate=(idx == 0)))
                    v.wait(cV)
                    if idx == 0:
                        cV.inc(v.tensor_scalar(n2bf[:, 0:16], n2t[:, 0:16], 0.5,
                                               None, op0=ALU.mult))
                    else:
                        cV.inc(v.tensor_copy(n2bf[:, 0:16], n2t[:, 0:16]))
                    t_.wait(cV)
                    t_.wait(cS, cS.n)
                    d = t_.transpose(ptpn[0:16, 0:128], n2bf[:, 0:16],
                                     cx.sb["f_idb"][:])
                    cT.inc(d)
                    s.wait(cT)
                    cS.inc(s.activation(n2cm[0:16, tb2, :], ptpn[0:16, 0:128],
                                        AF.Copy))
                    t_.wait(cS, cS.n)
                    for hh in range(16):
                        d = t_.transpose(
                            ptpn[(hh % 2)*64:(hh % 2)*64+64,
                                 128*(hh//2):128*(hh//2)+128],
                            obf[:, tb2, 64*hh:64*(hh+1)], cx.sb["f_idb"][:])
                    cT.inc(d)
                    s.wait(cT)
                    cS.inc(s.activation(ocm[:, tb2, :], ptpn[:, 0:1024], AF.Copy))
                    q.wait(cS)
                    dstq = cx.dram["q1_cm"] if idx == 0 else cx.dram["k1r_cm"]
                    dn2 = cx.dram["qn2_d"] if idx == 0 else cx.dram["kr2_d"]
                    spq.inc(q.dma_start(
                        out=dstq.rearrange("(hp r) t -> r hp t", hp=8, r=128)[
                            :, :, 128*tile:128*(tile+1)],
                        in_=ocm[:, tb2, :].rearrange("p (hp b) -> p hp b",
                                                     hp=8, b=128)))
                    spq.inc(q.dma_start(out=dn2[:, 128*tile:128*(tile+1)],
                                        in_=n2cm[0:16, tb2, :]))
    cx.barrier()


def emit_l1_attn(cx: Ctx):
    t_, v, s, g, q = cx.t, cx.v, cx.s, cx.g, cx.q
    H, D = 16, 64
    cT, cV, cS, cG = cx.fine[20], cx.fine[21], cx.fine[22], cx.fine[23]
    with cx.scope():
        cx.sb["f_idb"] = cx.sbuf("f_idbt", [128, 128], BF16)
        cosT = cx.sbuf("f_cosT", [128, 2048], F32)
        sinT = cx.sbuf("f_sinT", [128, 2048], F32)
        cosI2 = cx.sbuf("f_cosI2", [128, 2048], F32)
        sinI2 = cx.sbuf("f_sinI2", [128, 2048], F32)
        with cx.scope():
            iot = cx.sbuf("f_iot", [128, 128], F32)
            vcol = cx.sbuf("f_vcol", [128, 128], F32)
            hpi = cx.sbuf("f_hpi", [128, 1], F32)
            d = g.iota(iot[:], pattern=[[1, 128]], base=0, channel_multiplier=0,
                       allow_small_or_imprecise_dtypes=True)
            cG.inc(d)
            d = g.iota(vcol[:], pattern=[[0, 128]], base=0, channel_multiplier=1,
                       allow_small_or_imprecise_dtypes=True)
            cG.inc(d)
            v.wait(cG)
            d = v.tensor_tensor(cx.sb["f_idb"][:], iot[:], vcol[:],
                                op=ALU.is_equal)
            cV.inc(d)
            d = v.memset(hpi[:], float(np.pi / 2))
            cV.inc(d)
            t_.wait(cV)
            s.wait(cV)
            g.wait(cV)
            tbsc = {
                "irow": cx.sbuf("f_irow", [1, 384], F32),
                "r32": cx.sbuf("f_r32", [1, 96], F32),
                "wk": cx.sbuf("f_wkt", [128, 4, 512], F32),
                "halfpi": hpi,
            }
            pq0 = cx.psum("f_pq0", [128, 512])
            pq1 = cx.psum("f_pq1", [128, 512])
            _emit_rope_tables(cx, (cosT, sinT, cosI2, sinI2), tbsc, pq0, pq1,
                              xpos=False)
        cx.barrier()
        _emit_l1_proj(cx, 0, (cosT, sinT))
        _emit_l1_proj(cx, 1, (cosT, sinT))
        _emit_l1_proj(cx, 2, (cosT, sinT))
        with cx.scope():
            lamt = cx.sbuf("f_lamt", [64, 64], BF16)
            lamn = cx.sbuf("f_lamn", [64, 1], F32)
            lsq = cx.sbuf("f_lsq", [64, 64], F32)
            kin = cx.sbuf("f_kin", [64, 2, 16, 512], BF16)
            krows = cx.sbuf("f_krows", [1, 2, 16, 512], BF16)
            knyo = cx.sbuf("f_knyo", [64, 2, 16, 512], BF16)
            ksq = cx.sbuf("f_ksq", [64, 512], F32)
            k2o = cx.sbuf("f_k2o", [1, 2, 16, 512], F32)
            pny = [cx.psum(f"f_pny{i}", [128, 512]) for i in range(2)]
            pn1 = cx.psum("f_pn1", [1, 512])
            ldk = [cx.dmac[11], cx.dmac[12]]
            lgk = [cx.dmac[15], cx.dmac[16]]
            spk = cx.dmac[13]
            ldr = cx.dmac[9]
            lgw2 = cx.dmac[14]
            ldr.inc(q.dma_start(out=lsq[:, :],
                                in_=cx.dram["L1_attn_landmarks"][:, :]))
            s.wait(ldr)
            cS.inc(s.activation(lsq[:, :], lsq[:, :], AF.Square,
                                accum_out=lamn[:, 0:1]))
            s.wait(cS)
            cS.inc(s.activation(lamn[:, 0:1], lamn[:, 0:1], AF.Identity,
                                scale=float(-1.0 / 64.0)))
            lgw2.inc(g.dma_start(out=lamt[:, :], in_=cx.dram["L1_attn_lamT"][:, :]))
            g.wait(lgw2)
            cG.inc(g.tensor_scalar(lamt[:, :], lamt[:, :], -2.0, None,
                                   op0=ALU.mult))
            t_.wait(cG)
            t_.wait(cS)
            for sp_i in range(T // 512):
                buf = sp_i % 2
                t0 = 512 * sp_i
                q.wait(cS)
                for h in range(H):
                    ldk[buf].inc(q.dma_start(
                        out=kin[0:64, buf, h, :],
                        in_=cx.dram["k1r_cm"][64*h:64*(h+1), t0:t0+512]))
                g.wait(cT)
                for h in range(H):
                    lgk[buf].inc(g.dma_start(
                        out=krows[0:1, buf, h, :],
                        in_=cx.dram["kr2_d"][h:h+1, t0:t0+512]))
                t_.wait_ge(ldk[buf].sem, ldk[buf].n)
                t_.wait_ge(lgk[buf].sem, lgk[buf].n)
                if sp_i >= 2:
                    s.wait(spk)
                    v.wait(spk)
                for h in range(H):
                    slot = h % 2
                    if h >= 2 or sp_i >= 1:
                        t_.wait(cS, cS.n)
                    t_.matmul(pny[slot][0:64, :], lamt[:, :],
                              kin[0:64, buf, h, :], start=True, stop=False)
                    d = t_.matmul(pny[slot][0:64, :],
                                  cx.sb["ones_row_bf"][0:1, 0:64],
                                  krows[0:1, buf, h, :], start=False, stop=True)
                    cT.inc(d)
                    s.wait(cT)
                    d = s.activation(knyo[0:64, buf, h, :],
                                     pny[slot][0:64, :], AF.Exp,
                                     scale=float(-1.0 / 64.0), bias=lamn[:, 0:1])
                    cS.inc(d)
                for h in range(H):
                    s.wait(cS)
                    if h >= 1:
                        s.wait(cT, cT.n)
                    cS.inc(s.activation(ksq[0:64, :], knyo[0:64, buf, h, :],
                                        AF.Square))
                    t_.wait(cS)
                    if h >= 1 or sp_i >= 1:
                        t_.wait(cV, cV.n)
                    d = t_.matmul(pn1[0:1, :], cx.sb["ones_col"][0:64, 0:1],
                                  ksq[0:64, :], start=True, stop=True)
                    cT.inc(d)
                    v.wait(cT)
                    cV.inc(v.tensor_scalar(k2o[0:1, buf, h, :], pn1[0:1, :],
                                           -0.5, None, op0=ALU.mult))
                q.wait(cS)
                q.wait(cV)
                for h in range(H):
                    spk.inc(q.dma_start(
                        out=cx.dram["kny_cm"][64*h:64*(h+1), t0:t0+512],
                        in_=knyo[0:64, buf, h, :]))
                spk.inc(q.dma_start(
                    out=cx.dram["kn2_d"][:, t0:t0+512].unsqueeze(0),
                    in_=k2o[:, buf, :, :]))
        cx.barrier()
        with cx.scope():
            omg = cx.sbuf("f_omg", [65, 256], BF16)
            mln16 = cx.sbuf("f_mln16", [128, 1], F32)
            ktil = cx.sbuf("f_ktil", [65, 2, T], BF16)
            qtil = cx.sbuf("f_qtil", [65, 2, T], BF16)
            vhd = cx.sbuf("f_vhd", [128, 2, 32, D], BF16)
            kp_tm = cx.sbuf("f_kptm", [128, 32, 256], BF16)
            qp_cm = cx.sbuf("f_qpcm", [128, 2, T], BF16)
            ksrow = cx.sbuf("f_ksrow", [1, 256], F32)
            ksumc = cx.sbuf("f_ksumc", [128, 2], BF16)
            kvsb = cx.sbuf("f_kvsb", [128, 2, D], BF16)
            dreq = cx.sbuf("f_dreq", [1, T], F32)
            rinvc = cx.sbuf("f_rinvc", [128, 32], F32)
            numt = cx.sbuf("f_numt", [128, 2, D], BF16)
            ncm = cx.sbuf("f_ncm", [64, 2, 128], BF16)
            pk5 = [cx.psum(f"f_pk5{i}", [128, 512]) for i in range(2)]
            pd5 = cx.psum("f_pd5", [128, 512])
            pt5 = cx.psum("f_pt5", [128, 1024], BF16)
            ldh = [cx.dmac[11], cx.dmac[12]]
            lgh = [cx.dmac[15], cx.dmac[16]]
            spn = cx.dmac[13]
            lgo = cx.dmac[14]
            lgo.inc(g.dma_start(out=omg[0:64, :],
                                in_=cx.dram["L1_attn_omega"][:, :]))
            v.wait(lgo)
            cV.inc(v.memset(omg[64:65, :], 1.0))
            cV.inc(v.memset(mln16[:, :], -float(np.log(16.0))))
            t_.wait(cV)
            s.wait(cV)
            for h in range(H):
                hb = h % 2
                t_.wait(cS, cS.n)
                t_.wait(cV, cV.n)
                if h >= 2:
                    q.wait(cT)
                    g.wait(cT)
                ldh[hb].inc(q.dma_start(out=ktil[0:64, hb, :],
                                        in_=cx.dram["kny_cm"][64*h:64*(h+1), :]))
                ldh[hb].inc(q.dma_start(out=qtil[0:64, hb, :],
                                        in_=cx.dram["q1_cm"][64*h:64*(h+1), :]))
                ldh[hb].inc(q.dma_start(
                    out=vhd[:, hb, :, :],
                    in_=cx.dram["v1_tm"][h, :, :].rearrange("(a p) d -> p a d",
                                                            p=128)))
                lgh[hb].inc(g.dma_start(out=ktil[64:65, hb, :],
                                        in_=cx.dram["kn2_d"][h:h+1, :]))
                lgh[hb].inc(g.dma_start(out=qtil[64:65, hb, :],
                                        in_=cx.dram["qn2_d"][h:h+1, :]))
                t_.wait_ge(ldh[hb].sem, ldh[hb].n)
                t_.wait_ge(lgh[hb].sem, lgh[hb].n)
                for tc in range(32):
                    slot = tc % 2
                    if tc >= 2:
                        t_.wait(cS, cS.n)
                    d = t_.matmul(pk5[slot][:, 0:256],
                                  ktil[:, hb, 128*tc:128*(tc+1)],
                                  omg[:, :], start=True, stop=True)
                    cT.inc(d)
                    s.wait(cT)
                    cS.inc(s.activation(kp_tm[:, tc, :], pk5[slot][:, 0:256],
                                        AF.Exp, bias=mln16[:, 0:1]))
                t_.wait(cS)
                for tc in range(32):
                    d = t_.matmul(pd5[0:1, 0:256], cx.sb["ones_col_bf"][:, 0:1],
                                  kp_tm[:, tc, :], start=(tc == 0),
                                  stop=(tc == 31))
                cT.inc(d)
                v.wait(cT)
                cV.inc(v.tensor_copy(ksrow[0:1, :], pd5[0:1, 0:256]))
                t_.wait(cV)
                for mc in range(2):
                    d = t_.matmul(pd5[0:128, 300+mc:301+mc],
                                  ksrow[0:1, 128*mc:128*(mc+1)],
                                  cx.sb["ones_row"][0:1, 0:1],
                                  start=True, stop=True)
                cT.inc(d)
                v.wait(cT)
                cV.inc(v.tensor_copy(ksumc[:, 0:2], pd5[0:128, 300:302]))
                for mc in range(2):
                    for tc in range(32):
                        d = t_.matmul(pk5[mc][:, 0:D],
                                      kp_tm[:, tc, 128*mc:128*(mc+1)],
                                      vhd[:, hb, tc, :], start=(tc == 0),
                                      stop=(tc == 31))
                    cT.inc(d)
                    v.wait(cT)
                    cV.inc(v.tensor_copy(kvsb[:, mc, :], pk5[mc][:, 0:D]))
                t_.wait(cV)
                for mc in range(2):
                    for tc2 in range(8):
                        slot = tc2 % 2
                        t_.wait(cS, cS.n)
                        d = t_.matmul(pk5[slot][:],
                                      omg[:, 128*mc:128*(mc+1)],
                                      qtil[:, hb, 512*tc2:512*(tc2+1)],
                                      start=True, stop=True)
                        cT.inc(d)
                        s.wait(cT)
                        cS.inc(s.activation(qp_cm[:, mc, 512*tc2:512*(tc2+1)],
                                            pk5[slot][:], AF.Exp,
                                            bias=mln16[:, 0:1]))
                t_.wait(cS)
                for tc2 in range(8):
                    if tc2 >= 1:
                        t_.wait(cV, cV.n)
                    for mc in range(2):
                        d = t_.matmul(pd5[0:1, 0:512], ksumc[:, mc:mc+1],
                                      qp_cm[:, mc, 512*tc2:512*(tc2+1)],
                                      start=(mc == 0), stop=(mc == 1))
                    cT.inc(d)
                    v.wait(cT)
                    cV.inc(v.tensor_copy(dreq[0:1, 512*tc2:512*(tc2+1)],
                                         pd5[0:1, 0:512]))
                v.wait(cV)
                cV.inc(v.tensor_scalar(dreq[0:1, :], dreq[0:1, :], 1e-6, None,
                                       op0=ALU.add))
                v.wait(cV)
                cV.inc(v.reciprocal(dreq[0:1, :], dreq[0:1, :]))
                t_.wait(cV)
                for tc in range(32):
                    d = t_.matmul(pd5[0:128, tc:tc+1],
                                  dreq[0:1, 128*tc:128*(tc+1)],
                                  cx.sb["ones_row"][0:1, 0:1],
                                  start=True, stop=True)
                cT.inc(d)
                v.wait(cT)
                cV.inc(v.tensor_copy(rinvc[:, 0:32], pd5[0:128, 0:32]))
                t_.wait(cV)
                t_.wait(cS, cS.n)
                for tc in range(32):
                    tpar = tc % 2
                    if tc >= 2:
                        t_.wait(cV, cV.n)
                    for mc in range(2):
                        d = t_.matmul(pk5[tpar][:, 0:D],
                                      qp_cm[:, mc, 128*tc:128*(tc+1)],
                                      kvsb[:, mc, :], start=(mc == 0),
                                      stop=(mc == 1))
                    cT.inc(d)
                    v.wait(cT)
                    cV.inc(v.tensor_scalar(numt[:, tpar, :], pk5[tpar][:, 0:D],
                                           rinvc[:, tc:tc+1], None, op0=ALU.mult))
                    t_.wait(cV)
                    if tc >= 2 or h >= 1:
                        t_.wait(cS, cS.n)
                    d = t_.transpose(pt5[0:D, 128*tpar:128*(tpar+1)],
                                     numt[:, tpar, :], cx.sb["f_idb"][:])
                    cT.inc(d)
                    s.wait(cT)
                    if tc >= 2 or h >= 1:
                        s.wait(spn)
                    cS.inc(s.activation(ncm[0:D, tpar, :],
                                        pt5[0:D, 128*tpar:128*(tpar+1)], AF.Copy))
                    q.wait(cS)
                    if tc >= 2 or h >= 1:
                        q.wait(spn)
                    spn.inc(q.dma_start(
                        out=cx.dram["num_cm"][64*h:64*(h+1), 128*tc:128*(tc+1)],
                        in_=ncm[0:D, tpar, :]))
        cx.barrier()
        with cx.scope():
            wps = cx.sbuf("f_wow", [128, NCH, C], BF16)
            bprow = cx.sbuf("f_bprow", [1, C], F32)
            bppp = cx.sbuf("f_bppp", [128, NCH], F32)
            nin = cx.sbuf("f_nin", [128, 2, NCH, 512], BF16)
            asb = cx.sbuf("f_asb", [128, 2, 512], F32)
            pcol6 = cx.psum("f_pcol6", [128, 8])
            pa6 = [cx.psum(f"f_pa6{i}", [128, 512]) for i in range(2)]
            c_pp = cx.fine[5]
            lgw6 = cx.dmac[14]
            ldr6 = cx.dmac[9]
            ldn = [cx.dmac[11], cx.dmac[12]]
            spp = [cx.dmac[13], cx.dmac[8]]
            for ci in range(NCH):
                lgw6.inc(g.dma_start(
                    out=wps[:, ci, :],
                    in_=cx.dram["L1_attn_out_w"][128*ci:128*(ci+1), :]))
            ldr6.inc(q.dma_start(out=bprow[0:1, :],
                                 in_=cx.dram["L1_attn_out_b"][0:1, :]))
            t_.wait(ldr6)
            t_.wait(lgw6)
            emit_pp_from_row(cx, bprow[:], C, bppp, pcol6, c_pp)
            s.wait(c_pp)
            for it in range(NT):
                t0 = 512 * it
                buf = it % 2
                q.wait(cT)
                for ci in range(NCH):
                    ldn[buf].inc(q.dma_start(
                        out=nin[:, buf, ci, :],
                        in_=cx.dram["num_cm"][128*ci:128*(ci+1), t0:t0+512]))
                t_.wait_ge(ldn[buf].sem, ldn[buf].n)
                for ob in range(NCH):
                    slot = ob % 2
                    if it > 0 or ob >= 2:
                        t_.wait(cS, cS.n)
                    for ci in range(NCH):
                        d = t_.matmul(pa6[slot][:], wps[:, ci, 128*ob:128*(ob+1)],
                                      nin[:, buf, ci, :], start=(ci == 0),
                                      stop=(ci == NCH - 1))
                    cT.inc(d)
                    s.wait(cT)
                    if it > 0 or ob >= 2:
                        s.wait(spp[slot])
                    d = s.activation(asb[:, slot, :], pa6[slot][:], AF.Identity,
                                     bias=bppp[:, ob:ob+1])
                    cS.inc(d)
                    q.wait(cS)
                    spp[slot].inc(q.dma_start(
                        out=cx.dram["P_cm"][128*ob:128*(ob+1), t0:t0+512],
                        in_=asb[:, slot, :]))
    cx.barrier()


_DRAM_TABLE = [
    ("x_cm", [C, T], "I"),
    ("L0_attn_Wq", [C, C], "I"), ("L0_attn_bq", [1, C], "I"),
    ("L0_attn_Wk", [C, C], "I"), ("L0_attn_bk", [1, C], "I"),
    ("L0_attn_Wv", [C, C], "I"), ("L0_attn_bv", [1, C], "I"),
    ("L0_attn_Wp", [C, C], "I"), ("L0_attn_bp", [1, C], "I"),
    ("L0_ffn_Wqkv", [6, C, 3072], "I"),
    ("L0_ffn_q_scale", [6, 64], "I"), ("L0_ffn_k_scale", [6, 64], "I"),
    ("L0_ffn_Wo", [6, C, C], "I"),
    ("L0_ffn_ln_g", [6, C], "I"), ("L0_ffn_ln_b", [6, C], "I"),
    ("L0_ffn_W1", [6, C, 5460], "I"), ("L0_ffn_b1", [6, 5460], "I"),
    ("L0_ffn_W2", [6, 2730, C], "I"), ("L0_ffn_b2", [6, C], "I"),
    ("L1_attn_conv_k", [3, 3, 256, C], "I"), ("L1_attn_conv_b", [3, C], "I"),
    ("L1_attn_rff_w", [3, 3, C, 256], "I"), ("L1_attn_rff_b", [3, 3, 256], "I"),
    ("L1_attn_proj_w", [3, 768, C], "I"), ("L1_attn_proj_b", [3, C], "I"),
    ("L1_attn_omega", [64, 256], "I"), ("L1_attn_landmarks", [64, 64], "I"),
    ("L1_attn_lamT", [64, 64], "I"),
    ("L1_attn_out_w", [C, C], "I"), ("L1_attn_out_b", [1, C], "I"),
    ("L1_ffn_W_rT", [C, C], "I"), ("L1_ffn_W_kT", [C, C], "I"),
    ("L1_ffn_W_vT", [C, C], "I"),
    ("L1_ffn_mu_r", [1, C], "I"), ("L1_ffn_mu_k", [1, C], "I"),
    ("xn_cm", [C, T], "S"), ("A_cm", [C, T], "S"), ("z_cm", [C, T], "S"),
    ("x1_cm", [C, T], "S"), ("xn1_cm", [C, T], "S"), ("P_cm", [C, T], "S"),
    ("q0_cm", [C, T], "Sb"), ("k0_cm", [C, T], "Sb"), ("v0_tm", [16, T, 64], "Sb"),
    ("q1_cm", [C, T], "Sb"), ("v1_tm", [16, T, 64], "Sb"),
    ("k1r_cm", [C, T], "Sb"), ("kny_cm", [C, T], "Sb"),
    ("qn2_d", [16, T], "S"), ("kr2_d", [16, T], "S"),
    ("kn2_d", [16, T], "S"),
    ("num_cm", [C, T], "Sb"),
    ("out_cm", [C, T], "O"),
]
_DBG_TABLE = [("ycm_dbg", [NCH, 128, T], "Sb"),
              ("zlay_dbg", [6, C, T], "S")]

PHASES_ALL = ("A", "B", "C", "D", "E", "F", "G")


def build_program(phases=PHASES_ALL, debug_inputs=(), debug_outputs=()):
    nc = bass.Bass("TRN2", target_bir_lowering=False)
    es = ExitStack()
    cx = Ctx(nc, es)
    table = list(_DRAM_TABLE) + [x for x in _DBG_TABLE if x[0] in debug_outputs]
    for name, shape, kind in table:
        dt = BF16 if kind == "Sb" else F32
        k = {"I": "ExternalInput", "S": "Internal", "Sb": "Internal",
             "O": "ExternalOutput"}[kind]
        if name in debug_inputs:
            k = "ExternalInput"
        if name in debug_outputs:
            k = "ExternalOutput"
        cx.dram[name] = nc.dram_tensor(name, shape, dt, kind=k).ap()

    phase_consts(cx)
    if "A" in phases:
        with cx.scope():
            sc = alloc_ln_scratch(cx)
            emit_ln_cm(cx, "x_cm", "xn_cm", sc)
        cx.barrier()
    if "B" in phases:
        emit_l0_attn(cx)
    if "C" in phases:
        emit_l0_ffn(cx)
    if "D" in phases:
        emit_add3(cx, "x_cm", "A_cm", "z_cm", "x1_cm")
    if "E" in phases:
        with cx.scope():
            sc = alloc_ln_scratch(cx)
            emit_ln_cm(cx, "x1_cm", "xn1_cm", sc)
        cx.barrier()
    if "F" in phases:
        emit_l1_attn(cx)
    if "G" in phases:
        emit_l1ffn_combine(cx)

    for c in cx.all_dma_ctrs():
        if c.n:
            cx.q.wait(c)
    cx.g.wait(cx.lg)

    block = es.enter_context(nc.Block())
    block.tensor(_replay(cx.streams["tensor"]))
    block.vector(_replay(cx.streams["vector"]))
    block.scalar(_replay(cx.streams["scalar"]))
    block.gpsimd(_replay(cx.streams["gpsimd"]))
    block.sync(_replay(cx.streams["sync"]))
    es.close()
    return nc


def make_in_maps(inputs, extra_per_core=None):
    x = np.asarray(inputs["L0_attn_x"], dtype=np.float32)
    shared = {}
    for name, shape, kind in _DRAM_TABLE:
        if kind != "I" or name in ("x_cm", "L1_attn_lamT"):
            continue
        if name in ("L1_ffn_W_rT", "L1_ffn_W_kT", "L1_ffn_W_vT"):
            base = {"L1_ffn_W_rT": "L1_ffn_W_r", "L1_ffn_W_kT": "L1_ffn_W_k",
                    "L1_ffn_W_vT": "L1_ffn_W_v"}[name]
            shared[name] = np.ascontiguousarray(
                np.asarray(inputs[base], np.float32).T)
        else:
            arr = np.asarray(inputs[name], np.float32)
            shared[name] = np.ascontiguousarray(arr.reshape(shape))
    shared["L1_attn_lamT"] = np.ascontiguousarray(
        np.asarray(inputs["L1_attn_landmarks"], np.float32).T)
    in_maps = []
    for b in range(8):
        m = dict(shared)
        m["x_cm"] = np.ascontiguousarray(x[b].T)
        if extra_per_core:
            for k, v in extra_per_core.items():
                m[k] = v[b]
        in_maps.append(m)
    return in_maps


def kernel(**inputs):
    in_maps = make_in_maps(inputs)
    nc = build_program()
    res = run_bass_kernel_spmd(nc, in_maps, core_ids=list(range(8)))
    out = np.stack([np.asarray(res.results[b]["out_cm"], np.float32).T
                    for b in range(8)])
    return np.ascontiguousarray(out.astype(np.float32))
